# version 02
# baseline (speedup 1.0000x reference)
import numpy as np
import ml_dtypes
import concourse.bass as bass
import concourse.mybir as mybir
from concourse.bass_utils import run_bass_kernel_spmd

F32 = mybir.dt.float32
BF16 = mybir.dt.bfloat16
AF = mybir.ActivationFunctionType
ALU = mybir.AluOpType
AX = mybir.AxisListType

T = 2048
D = 1024
KC = D // 128
NT = T // 512
RMAGIC = 12582912.0
KONST_VALUES = (1e-5, 1e-6, 1e-10, 1.0, 0.5)



_DVE_TENSOR_OPS = (
    "tensor_", "scalar_tensor_tensor", "reduce_", "reciprocal", "memset",
    "memzero", "select", "iota", "pool", "bn_")
_ACT_TENSOR_OPS = ("activation", "square", "sqrt", "mul", "copy", "add",
                   "sign", "memzero")


class _EngW:

    def __init__(self, ck, name, eng):
        object.__setattr__(self, "_ck", ck)
        object.__setattr__(self, "_name", name)
        object.__setattr__(self, "_eng", eng)

    def __getattr__(self, attr):
        fn = getattr(self._eng, attr)
        guard = ((self._name in ("dve", "pool")
                  and any(attr.startswith(p) for p in _DVE_TENSOR_OPS))
                 or (self._name == "act"
                     and any(attr.startswith(p) for p in _ACT_TENSOR_OPS)))

        def wrap(*a, **k):
            inst = fn(*a, **k)
            if inst is not None and hasattr(inst, "then_inc"):
                if guard and a and hasattr(a[0], "free_size"):
                    try:
                        small = a[0].free_size() <= 16
                    except Exception:
                        small = False
                    if small:
                        inst = self._eng.drain()
                self._ck._last[self._name] = inst
                self._ck._last_tok[self._name] = None
            return inst

        return wrap


class Ck:

    DMAQ = {"qs": "sp", "qa": "act", "qp": "pool"}

    def __init__(self, nc):
        self.nc = nc
        self.raw = dict(pe=nc.tensor, dve=nc.vector, act=nc.scalar,
                        pool=nc.gpsimd, sp=nc.sync)
        self.pe = _EngW(self, "pe", nc.tensor)
        self.dve = _EngW(self, "dve", nc.vector)
        self.act = _EngW(self, "act", nc.scalar)
        self.pool = _EngW(self, "pool", nc.gpsimd)
        self.sp = _EngW(self, "sp", nc.sync)
        self.sem, self.cnt, self._last, self._last_tok = {}, {}, {}, {}
        self.waited = {}
        for name in ("pe", "dve", "act", "pool", "sp", "qs", "qa", "qp"):
            self.sem[name] = nc.alloc_semaphore(name=f"ck_{name}")
            self.cnt[name] = 0
            self._last[name] = None
            self._last_tok[name] = None

    def tok(self, eng):
        if self._last[eng] is None:
            return (eng, self.cnt[eng])
        if self._last_tok[eng] is None:
            self.cnt[eng] += 1
            self._last[eng].then_inc(self.sem[eng], 1)
            self._last_tok[eng] = (eng, self.cnt[eng])
        return self._last_tok[eng]

    def wait(self, eng, tok):
        if tok is None:
            return
        src, val = tok
        if src == eng or val <= 0:
            return
        key = (eng, src)
        if self.waited.get(key, 0) >= val:
            return
        self.raw[eng].wait_ge(self.sem[src], val)
        self.waited[key] = val

    def waits(self, eng, toks):
        for t in toks:
            self.wait(eng, t)

    def dma(self, q, out, in_, after=(), **kw):
        iss = self.DMAQ[q]
        for t in after:
            self.wait(iss, t)
        inst = self.raw[iss].dma_start(out=out, in_=in_, **kw)
        self.cnt[q] += 16
        inst.then_inc(self.sem[q], 16)
        return (q, self.cnt[q])

    def vreduce(self, out, in_, axis, op, **kw):
        self.raw["dve"].tensor_reduce(out, in_, axis, op, **kw)
        inst = self.raw["dve"].drain()
        self._last["dve"] = inst
        self._last_tok["dve"] = None
        return inst

    def recip(self, out, in_):
        self.raw["dve"].drain()
        self.raw["dve"].reciprocal(out, in_)
        inst = self.raw["dve"].drain()
        self._last["dve"] = inst
        self._last_tok["dve"] = None
        return inst

    def qtok(self, q):
        return (q, self.cnt[q])

    def join(self):
        engines = ("pe", "dve", "act", "pool", "sp")
        toks = [self.tok(e) for e in engines]
        toks += [self.qtok("qs"), self.qtok("qa"), self.qtok("qp")]
        for e in engines:
            for t in toks:
                self.wait(e, t)

    def finish(self, out_toks):
        self.join()
        for t in out_toks:
            self.wait("sp", t)
            self.wait("pool", t)
        for name in self.sem:
            self.raw["pool"].sem_clear(self.sem[name])




class WStream:

    def __init__(self, ck, name, ap, K, N, nb=128, depth=2):
        nc = ck.nc
        self.ck, self.ap, self.K, self.N, self.nb = ck, ap, K, N, nb
        assert K % 128 == 0 and N % nb == 0
        self.kc = K // 128
        self.nblk = N // nb
        self.depth = min(depth, self.nblk)
        self._ctx, self._bufs = [], []
        for i in range(self.depth):
            g = nc.sbuf_tensor(f"{name}_b{i}", [128, self.kc, nb], BF16)
            self._bufs.append(g.__enter__())
            self._ctx.append(g)
        self._load_tok = {}
        self._free_tok = [None] * self.depth
        self._epoch = 0

    def close(self):
        for g in reversed(self._ctx):
            g.__exit__(None, None, None)

    def reset(self):
        self._epoch += 1

    def _issue(self, j):
        if j >= self.nblk or (self._epoch, j) in self._load_tok:
            return
        b = j % self.depth
        src = self.ap[:, j * self.nb:(j + 1) * self.nb].rearrange(
            "(kc kp) n -> kp kc n", kp=128)
        after = [self._free_tok[b]] if self._free_tok[b] else []
        self._load_tok[(self._epoch, j)] = self.ck.dma(
            "qp", self._bufs[b][:, :, :], src, after=after)

    def start(self):
        for j in range(self.depth):
            self._issue(j)

    def get(self, j, eng="pe"):
        self._issue(j)
        self.ck.wait(eng, self._load_tok[(self._epoch, j)])
        return self._bufs[j % self.depth]

    def done(self, j, eng="pe"):
        self._free_tok[j % self.depth] = self.ck.tok(eng)
        self._issue(j + self.depth)




def bcast_rows(ck, scr_rows, row_aps, out_tiles, width=T):
    toks = []
    ups = []
    for i, r in enumerate(row_aps):
        ups.append(ck.dma("qs", scr_rows[i:i + 1, 0:width], r))
    for i, o in enumerate(out_tiles):
        src = scr_rows[i:i + 1, 0:width].partition_broadcast(128)
        toks.append(ck.dma("qs", o.unsqueeze(1), src, after=ups))
    return toks


class LNParams:

    def __init__(self, nc, ck, bufs):
        self.ck = ck
        self._ctx = []

        def sb(name, shape, dt=F32):
            g = nc.sbuf_tensor(name, shape, dt)
            self._ctx.append(g)
            return g.__enter__()

        self.bufs = bufs
        self.ones_col = bufs.ones_bf
        self.sq = bufs.scb1
        self.b1 = bufs.sc1
        self.b2 = bufs.sc2
        self.tmp = bufs.sc3
        self.ps_s = bufs.ps_row[0:1, 0:512]
        self.ps_q = bufs.ps_row[0:1, 512:1024]
        self.init_tok = bufs.ones_tok


def emit_ln(ck, ln, src_bf, outs, eps, gamma_col=None, beta_col=None,
            extra_bf=None, n_chunks=KC, src_tok=None):
    inv_n = 1.0 / (128 * n_chunks)
    last_read = None
    for tc in range(NT):
        sl = slice(tc * 512, (tc + 1) * 512)
        for c in range(n_chunks):
            ck.wait("act", ln.init_tok)
            if src_tok is not None:
                ck.wait("act", src_tok)
                ck.wait("pe", src_tok)
            tprev = ck.tok("pe") if c > 0 else last_read
            if tprev is not None:
                ck.wait("act", tprev)
            ck.act.activation(ln.sq[:, 0:512], src_bf[c][:, sl], AF.Square)
            tq = ck.tok("act")
            ck.wait("pe", ln.init_tok)
            if last_read is not None:
                ck.wait("pe", last_read)
            ck.pe.matmul(ln.ps_s, ln.ones_col[:, :], src_bf[c][:, sl],
                         start=(c == 0), stop=(c == n_chunks - 1))
            ck.wait("pe", tq)
            ck.pe.matmul(ln.ps_q, ln.ones_col[:, :], ln.sq[:, 0:512],
                         start=(c == 0), stop=(c == n_chunks - 1))
        tp = ck.tok("pe")
        ck.wait("dve", tp)
        rA, rB = ck.nc, None
        bufs = ln.bufs
        ck.dve.tensor_scalar_mul(bufs.rowA[0:1, sl], ln.ps_s, inv_n)
        ck.dve.tensor_scalar_mul(bufs.rowB[0:1, sl], ln.ps_q, inv_n)
        last_read = ck.tok("dve")
        ck.wait("act", last_read)
        ck.act.activation(bufs.sc3[0:1, sl], bufs.rowA[0:1, sl], AF.Square)
        ta = ck.tok("act")
        ck.wait("dve", ta)
        ck.dve.tensor_sub(bufs.rowB[0:1, sl], bufs.rowB[0:1, sl],
                          bufs.sc3[0:1, sl])
        td = ck.tok("dve")
        ck.wait("act", td)
        ck.act.activation(bufs.rowB[0:1, sl], bufs.rowB[0:1, sl], AF.Ln,
                          bias=float(eps))
        ck.act.activation(bufs.rowB[0:1, sl], bufs.rowB[0:1, sl], AF.Exp,
                          scale=-0.5)
        ta = ck.tok("act")
        ck.wait("dve", ta)
        ck.dve.tensor_mul(bufs.rowA[0:1, sl], bufs.rowA[0:1, sl],
                          bufs.rowB[0:1, sl])
        ck.dve.tensor_scalar_mul(bufs.rowA[0:1, sl], bufs.rowA[0:1, sl],
                                 -1.0)
    trows = ck.tok("dve")
    ck.wait("sp", trows)
    bufs = ln.bufs
    bts = bcast_rows(ck, ck.scr_rows, [bufs.rowB[0:1, :], bufs.rowA[0:1, :]],
                     [ln.b1[:, :], ln.b2[:, :]])
    out_toks = []
    for c in range(n_chunks):
        ck.waits("dve", bts)
        if gamma_col is None:
            ck.dve.tensor_mul(ln.tmp[:, :], src_bf[c][:, :], ln.b1[:, :])
            ck.dve.tensor_add(outs[c], ln.tmp[:, :], ln.b2[:, :])
        else:
            ck.dve.tensor_mul(ln.tmp[:, :], src_bf[c][:, :], ln.b1[:, :])
            ck.dve.tensor_add(ln.tmp[:, :], ln.tmp[:, :], ln.b2[:, :])
            ck.dve.tensor_scalar(outs[c], ln.tmp[:, :], gamma_col[c],
                                 beta_col[c], ALU.mult, ALU.add)
        if extra_bf is not None:
            ck.dve.tensor_copy(extra_bf[c], outs[c])
    return ck.tok("dve")




class Bufs:

    def __init__(self, nc, ck):
        self.nc, self.ck = nc, ck
        self._ctx = []

        def sb(name, shape, dt=F32):
            g = nc.sbuf_tensor(name, shape, dt)
            self._ctx.append(g)
            return g.__enter__()

        def psum(name, shape, dt=F32):
            g = nc.psum_tensor(name, shape, dt)
            self._ctx.append(g)
            return g.__enter__()

        self.X = sb("Xres", [128, KC, T])
        self.xn = sb("xn_bf", [128, KC, T], BF16)
        self.sc1 = sb("sc1", [128, T])
        self.sc2 = sb("sc2", [128, T])
        self.sc3 = sb("sc3", [128, T])
        self.scb1 = sb("scb1", [128, T], BF16)
        self.scb2 = sb("scb2", [128, T], BF16)
        self.colA = sb("colA", [128, 64])
        self.colB = sb("colB", [128, 64])
        self.rowA = self.sc1
        self.rowB = self.sc2
        self.id_bf = sb("id_bf", [128, 128], BF16)
        self.id_f32 = sb("id_f32", [128, 128])
        self.ps_big = psum("ps_big", [128, 2048])
        self.ps_mid = psum("ps_mid", [128, 512])
        self.ps_row = psum("ps_row", [1, 1024])
        self.ps_trb = psum("ps_trb", [128, 1024], BF16)
        self.colA2 = sb("colA2", [16, 128])
        self.ones_bf = sb("ones_bf", [128, 1], BF16)
        self.ones_f32 = sb("ones_f32", [128, 1])
        ck.pool.memset(self.ones_bf[:, :], 1.0)
        ck.pool.memset(self.ones_f32[:, :], 1.0)
        self.konst = sb("konst", [128, 8])
        for i, v in enumerate(KONST_VALUES):
            ck.pool.memset(self.konst[:, i:i + 1], float(v))
            nc.const_aps.register(F32, float(v), self.konst[:, i:i + 1])
        self.ones_tok = ck.tok("pool")
        self.id_tok = None
        self.trb_free = None
        self.mid_free = None
        self.big_free = None
        self.wo_free = None
        self.we_free = [None, None]



def col_loader(ck, nc, name, dram_ap, ncols):
    g = nc.sbuf_tensor(name, [128, ncols], F32)
    h = g.__enter__()
    tok = ck.dma("qs", h[:, :], dram_ap)
    return h, tok, g


class ScrStream:

    def __init__(self, ck, name, scr_ap, kc, N, nb=128, depth=2):
        nc = ck.nc
        self.ck, self.scr, self.kc, self.N, self.nb = ck, scr_ap, kc, N, nb
        self.nblk = N // nb
        self.depth = min(depth, self.nblk)
        self._ctx, self._bufs = [], []
        for i in range(self.depth):
            g = nc.sbuf_tensor(f"{name}_b{i}", [128, kc, nb], BF16)
            self._bufs.append(g.__enter__())
            self._ctx.append(g)
        self._load_tok = {}
        self._free_tok = [None] * self.depth
        self._epoch = 0

    def close(self):
        for g in reversed(self._ctx):
            g.__exit__(None, None, None)

    def reset(self):
        self._epoch += 1

    def _key(self, j):
        return (self._epoch, j)

    def _issue(self, j, ready_tok=None):
        if j >= self.nblk or self._key(j) in self._load_tok:
            return
        b = j % self.depth
        src = self.scr[:, :, j * self.nb:(j + 1) * self.nb]
        after = [t for t in (self._free_tok[b], ready_tok) if t]
        self._load_tok[self._key(j)] = self.ck.dma(
            "qs", self._bufs[b][:, :, :], src, after=after)

    def start(self, ready_tok=None):
        for j in range(self.depth):
            self._issue(j, ready_tok)

    def get(self, j, eng="pe"):
        self._issue(j)
        self.ck.wait(eng, self._load_tok[self._key(j)])
        return self._bufs[j % self.depth]

    def done(self, j, eng="pe"):
        self._free_tok[j % self.depth] = self.ck.tok(eng)
        self._issue(j + self.depth)


def quantize_bitlinear_weight(ck, bufs, w_ap, K, N, scr_ap, name):
    nc = ck.nc
    kc = K // 128
    nb = 2048 // kc
    nblk = N // nb
    ctx, fbufs, qbufs = [], [], []
    for i in range(2):
        g = nc.sbuf_tensor(f"{name}_f{i}", [128, kc, nb], F32)
        fbufs.append(g.__enter__())
        ctx.append(g)
        g = nc.sbuf_tensor(f"{name}_q{i}", [128, kc, nb], BF16)
        qbufs.append(g.__enter__())
        ctx.append(g)
    load_tok = [None] * nblk
    free_tok = [None, None]

    def issue(j, extra=()):
        if j >= nblk or load_tok[j] is not None:
            return
        b = j % 2
        src = w_ap[:, j * nb:(j + 1) * nb].rearrange(
            "(kc kp) n -> kp kc n", kp=128)
        aft = [t for t in (free_tok[b], *extra) if t]
        load_tok[j] = ck.dma("qs", fbufs[b][:, :, :], src, after=aft)

    tpe0 = ck.tok("pe")
    td0 = ck.tok("dve")
    ck.wait("pool", bufs.ones_tok)
    ck.wait("pool", tpe0)
    ck.wait("pool", td0)
    ck.wait("act", td0)
    ck.pool.memset(bufs.colA[:, 0:1], 0.0)
    tz = ck.tok("pool")
    issue(0)
    issue(1)
    for j in range(nblk):
        b = j % 2
        ck.wait("act", load_tok[j])
        ck.act.activation(fbufs[b][:, :, :], fbufs[b][:, :, :], AF.Abs,
                          accum_out=bufs.colA[:, 1:2])
        ta = ck.tok("act")
        ck.waits("dve", [ta, tz])
        ck.dve.tensor_add(bufs.colA[:, 0:1], bufs.colA[:, 0:1],
                          bufs.colA[:, 1:2])
        free_tok[b] = ck.tok("dve")
        load_tok[j] = None
        issue(j + 2)
    td = ck.tok("dve")
    ck.wait("pe", td)
    ck.pe.matmul(bufs.ps_row[0:1, 0:1], bufs.ones_f32[:, :],
                 bufs.colA[:, 0:1], start=True, stop=True)
    tp = ck.tok("pe")
    ck.wait("dve", tp)
    ck.wait("dve", ck.qtok("qs"))
    ck.dve.tensor_scalar(bufs.rowA[0:1, 0:1], bufs.ps_row[0:1, 0:1],
                         1.0 / (K * N), 1e-5, ALU.mult, ALU.max)
    ck.recip(bufs.rowA[0:1, 1:2], bufs.rowA[0:1, 0:1])
    trow = ck.tok("dve")
    t_up = ck.dma("qs", ck.scr_rows[0:1, 0:2], bufs.rowA[0:1, 0:2],
                  after=[trow])
    t_mn = ck.dma("qs", bufs.colA[:, 4:5].unsqueeze(1),
                  ck.scr_rows[0:1, 0:1].partition_broadcast(128),
                  after=[t_up])
    t_s = ck.dma("qs", bufs.colA[:, 5:6].unsqueeze(1),
                 ck.scr_rows[0:1, 1:2].partition_broadcast(128),
                 after=[t_up])
    out_toks = []
    for j in range(nblk):
        b = j % 2
        issue(j)
        ck.wait("dve", load_tok[j])
        ck.waits("dve", [t_mn, t_s])
        f = fbufs[b][:, :, :]
        q = qbufs[b][:, :, :]
        if j >= 2:
            ck.wait("dve", out_toks[j - 2])
        ck.dve.tensor_scalar(f, f, bufs.colA[:, 5:6], RMAGIC,
                             ALU.mult, ALU.add)
        ck.dve.tensor_scalar(f, f, RMAGIC, -1.0, ALU.subtract, ALU.max)
        ck.dve.tensor_scalar(q, f, 1.0, bufs.colA[:, 4:5],
                             ALU.min, ALU.mult)
        tq = ck.tok("dve")
        out_toks.append(ck.dma("qs", scr_ap[:, :, j * nb:(j + 1) * nb], q,
                               after=[tq]))
        free_tok[b] = out_toks[-1]
        load_tok[j] = None
        issue(j + 1, extra=(tq,))
    ck.waits("sp", out_toks)
    for g in reversed(ctx):
        g.__exit__(None, None, None)
    return out_toks





def act_quant(ck, bufs, src, out_q, s_b, sinv_b, src_tok=None):
    nch = len(src)
    w = src[0].shape[1]
    ntt = w // 128
    mx = bufs.colB
    for tt in range(ntt):
        tsl = slice(tt * 128, (tt + 1) * 128)
        for c in range(nch):
            ck.wait("pe", bufs.id_tok)
            if src_tok is not None:
                ck.wait("pe", src_tok)
            if bufs.trb_free is not None:
                ck.wait("pe", bufs.trb_free)
            pst = bufs.ps_trb[:, (c % 4) * 128:(c % 4) * 128 + 128]
            ck.pe.transpose(pst, src[c][:, tsl], bufs.id_bf[:, :])
            tp = ck.tok("pe")
            ck.wait("dve", tp)
            sc = 24 + (c % 2)
            ck.vreduce(mx[:, sc:sc + 1], pst, AX.X, ALU.max,
                       apply_absolute_value=True)
            if c == 0:
                ck.dve.tensor_copy(mx[:, tt:tt + 1], mx[:, sc:sc + 1])
            else:
                ck.dve.tensor_max(mx[:, tt:tt + 1], mx[:, tt:tt + 1],
                                  mx[:, sc:sc + 1])
            bufs.trb_free = ck.tok("dve")
    ck.dve.tensor_scalar_max(mx[:, 0:ntt], mx[:, 0:ntt], 1e-5)
    ck.dve.tensor_scalar_mul(mx[:, 8 + ntt:8 + 2 * ntt], mx[:, 0:ntt],
                             1.0 / 127.0)
    ck.recip(mx[:, 8:8 + ntt], mx[:, 0:ntt])
    ck.dve.tensor_scalar_mul(mx[:, 8:8 + ntt], mx[:, 8:8 + ntt], 127.0)
    tcols = ck.tok("dve")
    ck.waits("pe", [tcols, bufs.id_tok, bufs.mid_free])
    ck.pe.transpose(bufs.ps_mid[0:2 * ntt, 0:128], mx[:, 8:8 + 2 * ntt],
                    bufs.id_f32[:, :])
    tp = ck.tok("pe")
    ck.wait("dve", tp)
    ck.dve.tensor_copy(bufs.colA2[0:2 * ntt, :],
                       bufs.ps_mid[0:2 * ntt, 0:128])
    trows = ck.tok("dve")
    bufs.mid_free = trows
    if getattr(bufs, "dbg_aq", None) is not None:
        d1, d2 = bufs.dbg_aq
        ck.dma("qs", d1, bufs.colB[:, 0:32], after=[trows])
        ck.dma("qs", d2, bufs.colA2[:, :], after=[trows])
        bufs.dbg_aq = None
    dns = []
    for r in range(ntt):
        u1 = ck.dma("qs", ck.scr_rows[4 + r:5 + r, 0:128],
                    bufs.colA2[r:r + 1, :], after=[trows])
        u2 = ck.dma("qs", ck.scr_rows[12 + r:13 + r, 0:128],
                    bufs.colA2[ntt + r:ntt + r + 1, :], after=[trows])
        dns.append(ck.dma("qs", s_b[:, r * 128:(r + 1) * 128].unsqueeze(1),
                          ck.scr_rows[4 + r:5 + r, 0:128]
                          .partition_broadcast(128), after=[u1]))
        dns.append(ck.dma("qs", sinv_b[:, r * 128:(r + 1) * 128].unsqueeze(1),
                          ck.scr_rows[12 + r:13 + r, 0:128]
                          .partition_broadcast(128), after=[u2]))
    for c in range(nch):
        ck.waits("dve", dns)
        ck.dve.tensor_mul(bufs.sc3[:, 0:w], src[c], s_b[:, 0:w])
        ck.dve.tensor_scalar(bufs.sc3[:, 0:w], bufs.sc3[:, 0:w], RMAGIC,
                             RMAGIC, ALU.add, ALU.subtract)
        ck.dve.tensor_mul(out_q[c], bufs.sc3[:, 0:w], sinv_b[:, 0:w])
    return ck.tok("dve")




def emit_l1(ck, bufs, ln, W, cols, phases="ABC"):
    nc = ck.nc
    scr_g, scr_u, scr_d = W["scr_g"], W["scr_u"], W["scr_d"]
    if "A" in phases:
        quantize_bitlinear_weight(ck, bufs, W["wg"], D, 4096, scr_g, "l1wg")
        quantize_bitlinear_weight(ck, bufs, W["wu"], D, 4096, scr_u, "l1wu")
        quantize_bitlinear_weight(ck, bufs, W["wd"], 4096, D, scr_d, "l1wd")

    import os as _os
    _dbgA = _os.environ.get("KDBG_L1A")
    if _dbgA:
        bufs.dbg_aq = (
            nc.dram_tensor("d_colB", [128, 32], F32,
                           kind="ExternalOutput").ap(),
            nc.dram_tensor("d_colA2", [16, 128], F32,
                           kind="ExternalOutput").ap())
        d_xq = nc.dram_tensor("d_xq", [128, KC, 512], BF16,
                              kind="ExternalOutput").ap()
        d_gu = nc.dram_tensor("d_gu", [128, 32, 512], BF16,
                              kind="ExternalOutput").ap()
        d_sb = nc.dram_tensor("d_sb", [128, 512], F32,
                              kind="ExternalOutput").ap()
        d_sib = nc.dram_tensor("d_sib", [128, 512], F32,
                               kind="ExternalOutput").ap()
    if "A" in phases:
        with (
          nc.sbuf_tensor("l1_xq", [128, KC, 512], BF16) as xq,
          nc.sbuf_tensor("l1_gu", [128, 32, 512], BF16) as gu,
          nc.sbuf_tensor("l1_sb", [128, 512], F32) as s_b,
          nc.sbuf_tensor("l1_sib", [128, 512], F32) as sinv_b,
      ):
          sg = ScrStream(ck, "l1sg", scr_g, KC, 4096, nb=128)
          su = ScrStream(ck, "l1su", scr_u, KC, 4096, nb=128)
          sd = ScrStream(ck, "l1sd", scr_d, 32, D, nb=128)
          for tc in range(NT):
              tsl = slice(tc * 512, (tc + 1) * 512)
              tq = act_quant(ck, bufs, [bufs.xn[:, c, tsl] for c in range(KC)],
                             [xq[:, c, :] for c in range(KC)], s_b[:, :],
                             sinv_b[:, :])
              sg.reset(); su.reset(); sd.reset()
              sg.start(); su.start()
              gu_toks = {}
              for n in range(32):
                  gt = sg.get(n)
                  ut = su.get(n)
                  bk_g = bufs.ps_big[:, (n % 2) * 512:(n % 2) * 512 + 512]
                  bk_u = bufs.ps_big[:, 1024 + (n % 2) * 512:1536 + (n % 2) * 512]
                  if n >= 2:
                      ck.wait("pe", gu_toks[n - 2])
                  ck.wait("pe", tq)
                  for kc in range(KC):
                      ck.pe.matmul(bk_g, gt[:, kc, :], xq[:, kc, :],
                                   start=(kc == 0), stop=(kc == KC - 1))
                  sg.done(n)
                  for kc in range(KC):
                      ck.pe.matmul(bk_u, ut[:, kc, :], xq[:, kc, :],
                                   start=(kc == 0), stop=(kc == KC - 1))
                  su.done(n)
                  tp = ck.tok("pe")
                  ck.wait("act", tp)
                  ck.act.activation(bufs.scb2[:, 0:512], bk_g, AF.Silu)
                  ta = ck.tok("act")
                  ck.waits("dve", [tp, ta])
                  ck.dve.tensor_mul(gu[:, n, :], bufs.scb2[:, 0:512], bk_u)
                  gu_toks[n] = ck.tok("dve")
              if _dbgA and tc == 0:
                  ck.waits("sp", [tq, gu_toks[31]])
                  ck.dma("qs", d_xq, xq[:, :, :])
                  ck.dma("qs", d_gu, gu[:, :, :])
                  ck.dma("qs", d_sb, s_b[:, :])
                  ck.dma("qs", d_sib, sinv_b[:, :])
                  ck.join()
              tq2 = act_quant(ck, bufs, [gu[:, n, :] for n in range(32)],
                              [gu[:, n, :] for n in range(32)], s_b[:, :],
                              sinv_b[:, :], src_tok=gu_toks[31])
              xtoks = {}
              for oc in range(KC):
                  dt_ = sd.get(oc)
                  bk = bufs.ps_big[:, (oc % 4) * 512:(oc % 4) * 512 + 512]
                  if oc >= 4:
                      ck.wait("pe", xtoks[oc - 4])
                  ck.wait("pe", tq2)
                  for kc in range(32):
                      ck.pe.matmul(bk, dt_[:, kc, :], gu[:, kc, :],
                                   start=(kc == 0), stop=(kc == 31))
                  sd.done(oc)
                  tp = ck.tok("pe")
                  ck.wait("dve", tp)
                  ck.dve.tensor_add(bufs.X[:, oc, tsl], bufs.X[:, oc, tsl], bk)
                  xtoks[oc] = ck.tok("dve")
              ck.wait("pe", xtoks[KC - 1])
          sd.close(); su.close(); sg.close()

    if "B" in phases:
      with (
          nc.sbuf_tensor("l1_x2", [128, KC, T], BF16) as x2,
          nc.sbuf_tensor("l1_wh00", [128, KC, 128], BF16) as wh00,
          nc.sbuf_tensor("l1_wh01", [128, KC, 128], BF16) as wh01,
          nc.sbuf_tensor("l1_wh02", [128, KC, 128], BF16) as wh02,
          nc.sbuf_tensor("l1_wh10", [128, KC, 128], BF16) as wh10,
          nc.sbuf_tensor("l1_wh11", [128, KC, 128], BF16) as wh11,
          nc.sbuf_tensor("l1_wh12", [128, KC, 128], BF16) as wh12,
          nc.sbuf_tensor("l1_kh", [128, 16, 128], BF16) as kh,
          nc.sbuf_tensor("l1_vh", [128, 16, 128], BF16) as vh,
          nc.sbuf_tensor("l1_Qh", [128, T], BF16) as Qh,
          nc.sbuf_tensor("l1_kvh", [128, 128], BF16) as kvh,
          nc.sbuf_tensor("l1_ksc", [128, 1], BF16) as ksc,
      ):
          whb = [[wh00, wh01, wh02], [wh10, wh11, wh12]]
          wh_load = {}
          wh_free = [None, None]

          def load_head(h):
              if h >= KC or h in wh_load:
                  return
              b = whb[h % 2]
              after = [wh_free[h % 2]] if wh_free[h % 2] else []
              toks = []
              for i, wn in enumerate(("wq", "wk", "wv")):
                  hs = slice(h * 128, (h + 1) * 128)
                  srcap = W[wn][:, hs].rearrange("(kc kp) n -> kp kc n", kp=128)
                  toks.append(ck.dma("qp", b[i][:, :, :], srcap, after=after))
              wh_load[h] = toks

          ck.join()
          load_head(0)
          load_head(1)
          tln = emit_ln(ck, ln, [bufs.xn[:, c, :] for c in range(KC)],
                        [x2[:, c, :] for c in range(KC)], 1e-5,
                        gamma_col=[cols["l1_g1"][:, c:c + 1] for c in range(KC)],
                        beta_col=[cols["l1_be1"][:, c:c + 1] for c in range(KC)])

          def elu1_evac(ps, out_bf, w):
              ck.dve.tensor_scalar_min(bufs.scb2[:, 0:w], ps, 0.0)
              td = ck.tok("dve")
              ck.wait("act", td)
              ck.act.activation(bufs.scb1[:, 0:w], bufs.scb2[:, 0:w], AF.Exp)
              ta = ck.tok("act")
              ck.wait("dve", ta)
              ck.dve.scalar_tensor_tensor(out_bf, ps, 0.0, bufs.scb1[:, 0:w],
                                          ALU.max, ALU.add)
              return ck.tok("dve")

          _dbgH = _os.environ.get("KDBG_L1H")
          if _dbgH:
              d_kv = nc.dram_tensor("d_kv", [2, 128, 128], BF16,
                                    kind="ExternalOutput").ap()
              d_ks = nc.dram_tensor("d_ks", [2, 128, 1], BF16,
                                    kind="ExternalOutput").ap()
              d_q = nc.dram_tensor("d_q", [2, 128, T], BF16,
                                   kind="ExternalOutput").ap()
              d_z = nc.dram_tensor("d_z", [2, 1, T], F32,
                                   kind="ExternalOutput").ap()
              d_kh = nc.dram_tensor("d_kh", [2, 128, 16, 128], BF16,
                                    kind="ExternalOutput").ap()
              d_kps = nc.dram_tensor("d_kps", [128, 128], F32,
                                     kind="ExternalOutput").ap()
              d_whk = nc.dram_tensor("d_whk", [128, KC, 128], BF16,
                                     kind="ExternalOutput").ap()
              d_x2 = nc.dram_tensor("d_x2", [128, KC, T], BF16,
                                    kind="ExternalOutput").ap()
          prev_zb = []
          for h in range(KC):
              wh = whb[h % 2]
              ck.waits("pe", wh_load[h])
              ck.wait("pe", tln)
              ck.wait("pe", bufs.mid_free)
              for tt in range(16):
                  ttsl = slice(tt * 128, (tt + 1) * 128)
                  bk = bufs.ps_mid
                  for kc in range(KC):
                      ck.pe.matmul(bk[:, 0:128], x2[:, kc, ttsl],
                                   wh[1][:, kc, :],
                                   start=(kc == 0), stop=(kc == KC - 1))
                  for kc in range(KC):
                      ck.pe.matmul(bk[:, 128:256], x2[:, kc, ttsl],
                                   wh[2][:, kc, :],
                                   start=(kc == 0), stop=(kc == KC - 1))
                  tp = ck.tok("pe")
                  ck.wait("dve", tp)
                  if _dbgH and h == 0 and tt == 0:
                      ck.dve.tensor_copy(bufs.sc2[:, 0:128], bk[:, 0:128])
                      _td = ck.tok("dve")
                      _t6 = ck.dma("qs", d_kps, bufs.sc2[:, 0:128], after=[_td])
                      _t7 = ck.dma("qs", d_x2, x2[:, :, :], after=[_td])
                      _t8 = ck.dma("qs", d_whk, wh[1][:, :, :], after=[_td])
                      ck.waits("dve", [_t6, _t7, _t8])
                  t1 = elu1_evac(bk[:, 0:128], kh[:, tt, :], 128)
                  ck.wait("act", tp)
                  ck.act.activation(vh[:, tt, :], bk[:, 128:256], AF.Copy,
                                    scale=1.0 / T)
                  tv = ck.tok("act")
                  ck.waits("pe", [t1, tv])
              ck.wait("pe", bufs.mid_free)
              for tt in range(16):
                  ck.pe.matmul(bufs.ps_mid[:, 0:128], kh[:, tt, :],
                               vh[:, tt, :], start=(tt == 0), stop=(tt == 15))
              ck.wait("pe", bufs.ones_tok)
              for tt in range(16):
                  ck.pe.matmul(bufs.ps_mid[:, 256:257], kh[:, tt, :],
                               bufs.ones_bf[:, :], start=(tt == 0),
                               stop=(tt == 15))
              tp = ck.tok("pe")
              ck.wait("dve", tp)
              ck.dve.tensor_copy(kvh[:, :], bufs.ps_mid[:, 0:128])
              ck.dve.tensor_copy(ksc[:, :], bufs.ps_mid[:, 256:257])
              tks = ck.tok("dve")
              bufs.mid_free = tks
              if _dbgH and h < 2:
                  _t1 = ck.dma("qs", d_kv[h], kvh[:, :], after=[tks])
                  _t2 = ck.dma("qs", d_ks[h], ksc[:, :], after=[tks])
                  _t3 = ck.dma("qs", d_kh[h], kh[:, :, :], after=[tks])
                  ck.waits("pe", [_t1, _t2, _t3])
                  ck.waits("dve", [_t1, _t2, _t3])
              for tc2 in range(NT):
                  tsl = slice(tc2 * 512, (tc2 + 1) * 512)
                  bk = bufs.ps_big[:, (tc2 % 2) * 512:(tc2 % 2) * 512 + 512]
                  for kc in range(KC):
                      ck.pe.matmul(bk, wh[0][:, kc, :], x2[:, kc, tsl],
                                   start=(kc == 0), stop=(kc == KC - 1))
                  tp = ck.tok("pe")
                  ck.wait("dve", tp)
                  tqe = elu1_evac(bk, Qh[:, tsl], 512)
                  ck.wait("pe", tqe)
              wh_free[h % 2] = ck.tok("pe")
              load_head(h + 1)
              ck.waits("dve", prev_zb)
              for tc2 in range(NT):
                  tsl = slice(tc2 * 512, (tc2 + 1) * 512)
                  ck.wait("pe", tks)
                  ck.pe.matmul(bufs.ps_row[0:1, 512:1024], ksc[:, :],
                               Qh[:, tsl], start=True, stop=True)
                  tp = ck.tok("pe")
                  ck.wait("dve", tp)
                  ck.dve.tensor_scalar_add(bufs.rowB[0:1, tsl],
                                           bufs.ps_row[0:1, 512:1024], 1e-6)
                  ck.wait("pe", ck.tok("dve"))
              ck.recip(bufs.rowB[0:1, :], bufs.rowB[0:1, :])
              ck.dve.tensor_scalar_mul(bufs.rowB[0:1, :], bufs.rowB[0:1, :],
                                       float(T))
              tz = ck.tok("dve")
              ck.wait("sp", tz)
              if _dbgH and h < 2:
                  _t4 = ck.dma("qs", d_q[h], Qh[:, :], after=[tz])
                  _t5 = ck.dma("qs", d_z[h], bufs.rowB[0:1, :], after=[tz])
                  ck.waits("pe", [_t4, _t5])
                  ck.waits("dve", [_t4, _t5])
              zb = bcast_rows(ck, ck.scr_rows, [bufs.rowB[0:1, :]],
                              [bufs.sc3[:, :]])
              prev_zb = zb
              for tc2 in range(NT):
                  tsl = slice(tc2 * 512, (tc2 + 1) * 512)
                  bk = bufs.ps_big[:, 1024 + (tc2 % 2) * 512:1536 + (tc2 % 2) * 512]
                  ck.pe.matmul(bk, kvh[:, :], Qh[:, tsl], start=True, stop=True)
                  tp = ck.tok("pe")
                  ck.waits("dve", zb + [tp])
                  ck.dve.tensor_mul(bufs.sc2[:, 0:512], bk, bufs.sc3[:, tsl])
                  ck.dve.tensor_add(bufs.xn[:, h, tsl], bufs.sc2[:, 0:512],
                                    bufs.xn[:, h, tsl])
                  ck.wait("pe", ck.tok("dve"))


    if _os.environ.get("KDBG_L1B"):
        d_xl1 = nc.dram_tensor("d_xl1", [128, KC, T], BF16,
                               kind="ExternalOutput").ap()
        ck.join()
        ck.dma("qs", d_xl1, bufs.xn[:, :, :])
        ck.join()

    if "C" in phases:
      TC3 = 256
      with (
          nc.sbuf_tensor("l1_x3", [128, KC, T], BF16) as x3,
          nc.sbuf_tensor("l1_h", [128, 32, TC3], BF16) as hbuf,
      ):
          ck.join()
          tln3 = emit_ln(ck, ln, [bufs.xn[:, c, :] for c in range(KC)],
                         [x3[:, c, :] for c in range(KC)], 1e-5,
                         gamma_col=[cols["l1_g2"][:, c:c + 1] for c in range(KC)],
                         beta_col=[cols["l1_be2"][:, c:c + 1] for c in range(KC)])
          ck.wait("pe", tln3)
          w1s = WStream(ck, "l1w1", W["w1"], D, 4096, nb=128)
          w2s = WStream(ck, "l1w2", W["w2"], 4096, D, nb=128)
          for tc in range(T // TC3):
              tsl = slice(tc * TC3, (tc + 1) * TC3)
              w1s.reset(); w2s.reset()
              w1s.start()
              htoks = {}
              for n in range(32):
                  wt = w1s.get(n)
                  bk = bufs.ps_big[:, (n % 2) * 512:(n % 2) * 512 + TC3]
                  if n >= 2:
                      ck.wait("pe", htoks[n - 2])
                  for kc in range(KC):
                      ck.pe.matmul(bk, wt[:, kc, :], x3[:, kc, tsl],
                                   start=(kc == 0), stop=(kc == KC - 1))
                  w1s.done(n)
                  tp = ck.tok("pe")
                  ck.wait("act", tp)
                  ck.act.activation(hbuf[:, n, :], bk, AF.Gelu,
                                    bias=cols["l1_b1"][:, n:n + 1])
                  htoks[n] = ck.tok("act")
              xtk = {}
              for oc in range(KC):
                  wt = w2s.get(oc)
                  bk = bufs.ps_big[:, 1024 + (oc % 2) * 512:1024 + (oc % 2) * 512 + TC3]
                  ck.wait("pe", htoks[31])
                  if oc >= 2:
                      ck.wait("pe", xtk[oc - 2])
                  for kc in range(32):
                      ck.pe.matmul(bk, wt[:, kc, :], hbuf[:, kc, :],
                                   start=(kc == 0), stop=(kc == 31))
                  w2s.done(oc)
                  tp = ck.tok("pe")
                  ck.wait("dve", tp)
                  ck.dve.scalar_tensor_tensor(
                      bufs.sc3[:, 0:TC3], bk, cols["l1_b2"][:, oc:oc + 1],
                      bufs.xn[:, oc, tsl], ALU.add, ALU.add)
                  ck.dve.tensor_add(bufs.X[:, oc, tsl], bufs.X[:, oc, tsl],
                                    bufs.sc3[:, 0:TC3])
                  xtk[oc] = ck.tok("dve")
              ck.wait("pe", xtk[KC - 1])
          w2s.close()
          w1s.close()




def emit_outer_ln(ck, bufs, ln, x_tok=None):
    for c in range(KC):
        if x_tok is not None:
            ck.wait("dve", x_tok)
        ck.dve.tensor_copy(bufs.xn[:, c, :], bufs.X[:, c, :])
    tcast = ck.tok("dve")
    chunks = [bufs.xn[:, c, :] for c in range(KC)]
    return emit_ln(ck, ln, chunks, chunks, 1e-5, src_tok=tcast)



LAYER_INPUTS = {
    0: dict(win=[D, 4096], wx=[2048, 33], wout0=[2048, D], wbase=[D, D],
            wd1=[8, D, 256], wd2=[8, 256, D], wadown=[D, 512],
            waup=[512, D], baup=[8, D], ascan=[128, 256]),
    1: dict(wq=[D, D], wk=[D, D], wv=[D, D], w1=[D, 4096], w2=[4096, D],
            wg=[D, 4096], wu=[D, 4096], wd=[4096, D]),
    2: dict(wq2=[16, D, 64], wk2=[16, D, 64], wv2=[16, D, 64],
            wo2=[D, D], rw=[D, 8], rb=[8], we2=[8, D, D]),
    3: dict(wqkv=[D, 3 * D], wout=[D, D], wfc=[4096, D], wproj=[D, 4096],
            bv=[1, D]),
}
LAYER_COLS = {
    0: dict(l0_nw=8, l0_cw=64, l0_cb=16, l0_wdt=16, l0_bdt=16, l0_D=16,
            l0_bb=8, l0_bd1=16, l0_bd2=64, l0_bad=4),
    1: dict(l1_g1=8, l1_be1=8, l1_g2=8, l1_be2=8, l1_b1=32, l1_b2=8),
    2: dict(),
    3: dict(l3_bq=8, l3_bk=8, l3_bo=8, l3_bfc=32, l3_bpj=8),
}
LAYER_SCR = {
    0: dict(),
    1: dict(scr_g=[128, KC, 4096], scr_u=[128, KC, 4096],
            scr_d=[128, 32, D]),
    2: dict(),
    3: dict(scr_fc=[128, KC, 4096], scr_pj=[128, 32, D]),
}


def build_program(layers, l1_phases="ABC"):
    nc = bass.Bass("TRN2", target_bir_lowering=False)
    ck = Ck(nc)
    x_in = nc.dram_tensor("x_in", [D, T], F32, kind="ExternalInput").ap()
    y_out = nc.dram_tensor("y_out", [D, T], F32, kind="ExternalOutput").ap()
    id_bf_d = nc.dram_tensor("c_id_bf", [128, 128], BF16,
                             kind="ExternalInput").ap()
    id_f_d = nc.dram_tensor("c_id_f32", [128, 128], F32,
                            kind="ExternalInput").ap()
    ck.scr_rows = nc.dram_tensor("scr_rows", [32, T], F32,
                                 kind="Internal").ap()
    ck.scr_grows = nc.dram_tensor("scr_grows", [8, T], F32,
                                  kind="Internal").ap()
    Ws, scrs = {}, {}
    for L in layers:
        for name, shape in LAYER_INPUTS.get(L, {}).items():
            Ws[name] = nc.dram_tensor(f"L{L}_{name}", shape, F32,
                                      kind="ExternalInput").ap()
        if L == 3:
            Ws["em"] = nc.dram_tensor("L3_em", [128, 2176], BF16,
                                      kind="ExternalInput").ap()
        if L == 0:
            Ws["sred"] = nc.dram_tensor("L0_sred", [128, 16, 128], BF16,
                                        kind="ExternalInput").ap()
            ck.scr_dt = nc.dram_tensor("scr_dt", [2048, T], BF16,
                                       kind="Internal").ap()
            ck.scr_u = nc.dram_tensor("scr_u", [2048, T], BF16,
                                      kind="Internal").ap()
            ck.scr_z = nc.dram_tensor("scr_z", [2048, T], BF16,
                                      kind="Internal").ap()
            ck.scr_bc = nc.dram_tensor("scr_bc", [32, T], F32,
                                       kind="Internal").ap()
            ck.scr_x = nc.dram_tensor("scr_x", [128, KC, T], F32,
                                      kind="Internal").ap()
        import os as _os
        _k = "ExternalOutput" if _os.environ.get("KDBG_SCR") else "Internal"
        for name, shape in LAYER_SCR.get(L, {}).items():
            Ws[name] = nc.dram_tensor(f"L{L}_{name}", shape, BF16,
                                      kind=_k).ap()
    col_handles = {}
    col_ctx = []
    bufs_ctx = None
    with_stack = []

    bufs = Bufs(nc, ck)
    ln = LNParams(nc, ck, bufs)
    t1 = ck.dma("qs", bufs.id_bf[:, :], id_bf_d)
    t2 = ck.dma("qs", bufs.id_f32[:, :], id_f_d)
    bufs.id_tok = t2
    ck.wait("pe", t1)
    for L in layers:
        for name, ncols in LAYER_COLS.get(L, {}).items():
            d = nc.dram_tensor(f"c_{name}", [128, ncols], F32,
                               kind="ExternalInput").ap()
            g = nc.sbuf_tensor(f"s_{name}", [128, ncols], F32)
            h = g.__enter__()
            col_ctx.append(g)
            tk = ck.dma("qs", h[:, :], d)
            ck.wait("dve", tk)
            ck.wait("act", tk)
            col_handles[name] = h
    tx = ck.dma("qs", bufs.X[:, :, :],
                x_in.rearrange("(kc kp) t -> kp kc t", kp=128))
    ck.wait("dve", tx)
    for L in layers:
        emit_outer_ln(ck, bufs, ln)
        ck.join()
        if L == 0:
            emit_l0(ck, bufs, ln, Ws, col_handles,
                    phases=l1_phases if l1_phases != "ABC" else "FM")
        elif L == 1:
            emit_l1(ck, bufs, ln, Ws, col_handles, phases=l1_phases)
        elif L == 2:
            emit_l2(ck, bufs, ln, Ws, col_handles, phases=l1_phases
                    if l1_phases != "ABC" else "RAM")
        elif L == 3:
            emit_l3(ck, bufs, ln, Ws, col_handles,
                    phases=l1_phases if l1_phases != "ABC" else "AF")
        else:
            raise NotImplementedError(L)
        ck.join()
    tX = ck.tok("dve")
    tout = ck.dma("qs", y_out.rearrange("(kc kp) t -> kp kc t", kp=128),
                  bufs.X[:, :, :], after=[tX])
    ck.finish([tout])
    for g in reversed(col_ctx):
        g.__exit__(None, None, None)
    return nc




def _pt(a):
    return np.ascontiguousarray(np.asarray(a).T)


def _cols(a, n):
    return np.ascontiguousarray(np.asarray(a).reshape(n, 128).T)


def prep_shared(inputs, layers):
    sh = {}
    sh["c_id_bf"] = np.eye(128, dtype=ml_dtypes.bfloat16)
    sh["c_id_f32"] = np.eye(128, dtype=np.float32)
    if 0 in layers:
        sh["L0_win"] = np.ascontiguousarray(inputs["L0_attn_W_in"])
        sh["L0_wx"] = np.ascontiguousarray(inputs["L0_attn_W_x"])
        sh["L0_wout0"] = np.ascontiguousarray(inputs["L0_attn_W_out"])
        sh["L0_wbase"] = np.ascontiguousarray(inputs["L0_ffn_W_base"])
        sh["L0_wd1"] = np.ascontiguousarray(inputs["L0_ffn_Wd1"])
        sh["L0_wd2"] = np.ascontiguousarray(inputs["L0_ffn_Wd2"])
        sh["L0_wadown"] = np.concatenate(
            [inputs["L0_ffn_Wa_down"][e] for e in range(8)], axis=1)
        sh["L0_waup"] = np.concatenate(
            [inputs["L0_ffn_Wa_up"][e] for e in range(8)], axis=0)
        sh["L0_baup"] = np.ascontiguousarray(inputs["L0_ffn_ba_up"])
        al = np.asarray(inputs["L0_attn_A_log"])
        asc = np.zeros((256, 128), np.float32)
        for kc in range(16):
            for j in range(16):
                for s in range(16):
                    for cl in range(8):
                        asc[kc * 16 + j, s * 8 + cl] =                             al[kc * 128 + 8 * j + cl, s]
        sh["L0_ascan"] = np.ascontiguousarray(asc.T)
        sred = np.zeros((16, 128, 128), np.float32)
        for j in range(16):
            for p in range(128):
                sred[j, p, 8 * j + (p % 8)] = 1.0
        sh["L0_sred"] = sred.transpose(1, 0, 2).astype(
            ml_dtypes.bfloat16).copy()
        sh["c_l0_nw"] = _cols(inputs["L0_attn_norm_w"], 8)
        cw = np.asarray(inputs["L0_attn_conv_w"]).reshape(2048, 4)
        sh["c_l0_cw"] = np.ascontiguousarray(
            cw.reshape(16, 128, 4).transpose(1, 0, 2).reshape(128, 64))
        sh["c_l0_cb"] = _cols(inputs["L0_attn_conv_b"], 16)
        sh["c_l0_wdt"] = _cols(inputs["L0_attn_w_dt"], 16)
        sh["c_l0_bdt"] = _cols(inputs["L0_attn_b_dt"], 16)
        sh["c_l0_D"] = _cols(inputs["L0_attn_D_param"], 16)
        sh["c_l0_bb"] = _cols(inputs["L0_ffn_b_base"], 8)
        bd1 = np.asarray(inputs["L0_ffn_bd1"])
        sh["c_l0_bd1"] = np.ascontiguousarray(
            bd1.reshape(8, 2, 128).transpose(2, 0, 1).reshape(128, 16))
        bd2 = np.asarray(inputs["L0_ffn_bd2"])
        sh["c_l0_bd2"] = np.ascontiguousarray(
            bd2.reshape(8, 8, 128).transpose(2, 0, 1).reshape(128, 64))
        sh["c_l0_bad"] = _cols(np.asarray(inputs["L0_ffn_ba_down"])
                               .reshape(512), 4)
    if 1 in layers:
        sh["L1_wq"] = _pt(inputs["L1_attn_wq"])
        sh["L1_wk"] = _pt(inputs["L1_attn_wk"])
        sh["L1_wv"] = _pt(inputs["L1_attn_wv"])
        sh["L1_w1"] = _pt(inputs["L1_attn_w1"])
        sh["L1_w2"] = _pt(inputs["L1_attn_w2"])
        sh["L1_wg"] = _pt(inputs["L1_ffn_w_gate"])
        sh["L1_wu"] = _pt(inputs["L1_ffn_w_up"])
        sh["L1_wd"] = _pt(inputs["L1_ffn_w_down"])
        sh["c_l1_g1"] = _cols(inputs["L1_attn_g1"], 8)
        sh["c_l1_be1"] = _cols(inputs["L1_attn_be1"], 8)
        sh["c_l1_g2"] = _cols(inputs["L1_attn_g2"], 8)
        sh["c_l1_be2"] = _cols(inputs["L1_attn_be2"], 8)
        sh["c_l1_b1"] = _cols(inputs["L1_attn_b1"], 32)
        sh["c_l1_b2"] = _cols(inputs["L1_attn_b2"], 8)
    if 2 in layers:
        sh["L2_wq2"] = np.ascontiguousarray(inputs["L2_attn_W_q"])
        sh["L2_wk2"] = np.ascontiguousarray(inputs["L2_attn_W_k"])
        sh["L2_wv2"] = np.ascontiguousarray(inputs["L2_attn_W_v"])
        sh["L2_wo2"] = np.ascontiguousarray(inputs["L2_attn_W_o"])
        sh["L2_rw"] = _pt(inputs["L2_ffn_router_w"])
        sh["L2_rb"] = np.ascontiguousarray(inputs["L2_ffn_router_b"])
        sh["L2_we2"] = np.ascontiguousarray(inputs["L2_ffn_expert_weights"])
    if 3 in layers:
        sh["L3_wqkv"] = _pt(inputs["L3_attn_w_qkv"])
        sh["L3_wout"] = _pt(inputs["L3_attn_w_out"])
        sh["L3_wfc"] = np.ascontiguousarray(inputs["L3_ffn_w_fc"])
        sh["L3_wproj"] = np.ascontiguousarray(inputs["L3_ffn_w_proj"])
        b = np.asarray(inputs["L3_attn_b_qkv"])
        sh["c_l3_bq"] = _cols(b[0:D], 8)
        sh["c_l3_bk"] = _cols(b[D:2 * D], 8)
        sh["L3_bv"] = np.ascontiguousarray(b[2 * D:3 * D]).reshape(1, D)
        sh["c_l3_bo"] = _cols(inputs["L3_attn_b_out"], 8)
        sh["c_l3_bfc"] = _cols(inputs["L3_ffn_b_fc"], 32)
        sh["c_l3_bpj"] = _cols(inputs["L3_ffn_b_proj"], 8)
        p = np.arange(128)[:, None]
        f = np.arange(2176)[None, :]
        ramp = (f - 2048) - p
        em = np.where(ramp <= 0, np.exp(np.minimum(ramp, 0)), 0.0)
        sh["L3_em"] = em.astype(ml_dtypes.bfloat16)
    return sh


def run_layers(inputs, x_stack, layers):
    sh = prep_shared(inputs, layers)
    nc = build_program(layers)
    in_maps = []
    for c in range(8):
        m = dict(sh)
        m["x_in"] = x_stack[c]
        in_maps.append(m)
    res = run_bass_kernel_spmd(nc, in_maps, core_ids=list(range(8)))
    return [np.asarray(res.results[c]["y_out"]) for c in range(8)]


def kernel(**inputs):
    x = np.asarray(inputs["L0_attn_x"])
    xs = [np.ascontiguousarray(x[c].T) for c in range(8)]
    for L in (0, 1, 2, 3):
        xs = run_layers(inputs, xs, [L])
    return np.stack([o.T for o in xs]).astype(np.float32)




def emit_l2(ck, bufs, ln, W, cols, phases="RAM"):
    import os as _os
    nc = ck.nc
    NH, HD = 16, 64

    with (
        nc.sbuf_tensor("l2_rw", [128, KC, 8], BF16) as rw,
        nc.sbuf_tensor("l2_rbt", [128, 8], F32) as rbt,
        nc.sbuf_tensor("l2_lg", [128, 8], F32) as lg,
        nc.sbuf_tensor("l2_gcols", [128, 8], F32) as gcols,
        nc.sbuf_tensor("l2_grows", [8, T], F32) as grows,
        nc.sbuf_tensor("l2_c", [128, 16], F32) as c,
    ):
        ck.join()
        t_rw = ck.dma("qp", rw[:, :, :],
                      W["rw"].rearrange("(kc kp) n -> kp kc n", kp=128))
        t_rb = ck.dma("qs", rbt[:, :].unsqueeze(1),
                      W["rb"].unsqueeze(0).partition_broadcast(128))
        for tt in range(16):
            ttsl = slice(tt * 128, (tt + 1) * 128)
            ck.waits("pe", [t_rw])
            ck.wait("pe", bufs.mid_free)
            for kc in range(KC):
                ck.pe.matmul(bufs.ps_mid[:, 0:8], bufs.xn[:, kc, ttsl],
                             rw[:, kc, :], start=(kc == 0),
                             stop=(kc == KC - 1))
            tp = ck.tok("pe")
            ck.waits("dve", [tp, t_rb])
            ck.dve.tensor_add(lg[:, :], bufs.ps_mid[:, 0:8], rbt[:, :])
            bufs.mid_free = ck.tok("dve")
            ck.vreduce(c[:, 0:1], lg[:, :], AX.X, ALU.max)
            ck.dve.tensor_scalar_mul(c[:, 1:2], c[:, 0:1], -1.0)
            td = ck.tok("dve")
            ck.wait("act", td)
            ck.act.activation(lg[:, :], lg[:, :], AF.Exp, bias=c[:, 1:2],
                              accum_out=c[:, 2:3])
            ta = ck.tok("act")
            ck.wait("dve", ta)
            ck.vreduce(c[:, 3:4], lg[:, :], AX.X, ALU.max)
            ck.dve.tensor_scalar(c[:, 4:5], c[:, 3:4], -1.0, None, ALU.mult)
            ck.dve.scalar_tensor_tensor(gcols[:, 0:8], lg[:, :], c[:, 3:4],
                                        lg[:, :], ALU.is_equal, ALU.bypass)
            ck.dve.scalar_tensor_tensor(lg[:, :], gcols[:, 0:8], c[:, 4:5],
                                        lg[:, :], ALU.mult, ALU.add)
            ck.vreduce(c[:, 5:6], lg[:, :], AX.X, ALU.max)
            ck.dve.tensor_add(c[:, 6:7], c[:, 3:4], c[:, 5:6])
            ck.dve.tensor_scalar(c[:, 7:8], c[:, 2:3], 1e-10, None, ALU.mult)
            ck.dve.tensor_add(c[:, 6:7], c[:, 6:7], c[:, 7:8])
            ck.recip(c[:, 8:9], c[:, 6:7])
            ck.dve.tensor_mul(c[:, 9:10], c[:, 3:4], c[:, 8:9])
            ck.dve.tensor_mul(c[:, 10:11], c[:, 5:6], c[:, 8:9])
            ck.dve.tensor_add(c[:, 11:12], c[:, 9:10], c[:, 10:11])
            ck.dve.tensor_scalar_max(c[:, 11:12], c[:, 11:12], 1e-10)
            ck.recip(c[:, 12:13], c[:, 11:12])
            ck.dve.tensor_scalar_add(c[:, 12:13], c[:, 12:13], 1e-10)
            ck.dve.tensor_mul(c[:, 13:14], c[:, 9:10], c[:, 12:13])
            ck.dve.tensor_mul(c[:, 14:15], c[:, 10:11], c[:, 12:13])
            ck.dve.scalar_tensor_tensor(lg[:, :], lg[:, :], c[:, 5:6],
                                        lg[:, :], ALU.is_equal, ALU.bypass)
            ck.dve.scalar_tensor_tensor(gcols[:, 0:8], gcols[:, 0:8],
                                        c[:, 13:14], gcols[:, 0:8],
                                        ALU.mult, ALU.bypass)
            ck.dve.scalar_tensor_tensor(gcols[:, 0:8], lg[:, :], c[:, 14:15],
                                        gcols[:, 0:8], ALU.mult, ALU.add)
            tg = ck.tok("dve")
            ck.waits("pe", [tg, bufs.id_tok])
            ck.pe.transpose(bufs.ps_mid[0:8, 256:384], gcols[:, 0:8],
                            bufs.id_f32[:, :])
            tp = ck.tok("pe")
            ck.wait("dve", tp)
            ck.dve.tensor_copy(grows[0:8, ttsl], bufs.ps_mid[0:8, 256:384])
            bufs.mid_free = ck.tok("dve")
        tgr = ck.tok("dve")
        if _os.environ.get("KDBG_L2R"):
            d_gr = nc.dram_tensor("d_gr", [8, T], F32,
                                  kind="ExternalOutput").ap()
            d_c = nc.dram_tensor("d_c", [128, 16], F32,
                                 kind="ExternalOutput").ap()
            ck.dma("qs", d_gr, grows[:, :], after=[tgr])
            ck.dma("qs", d_c, c[:, :], after=[tgr])
            ck.join()
        ck.wait("sp", tgr)
        t_up = ck.dma("qs", ck.scr_grows[:, :], grows[:, :])

    avg = nc.sbuf_tensor("l2_av", [128, KC, T], BF16)
    av_all = avg.__enter__()
    if "A" not in phases:
        ck.join()
        ck.pool.memset(av_all[:, :, :], 0.0)
        ck.join()
    if "A" in phases:
     with (
         nc.sbuf_tensor("l2_wh0", [128, KC, 3 * HD], BF16) as wqkv0,
         nc.sbuf_tensor("l2_wh1", [128, KC, 3 * HD], BF16) as wqkv1,
         nc.sbuf_tensor("l2_qh", [64, T], BF16) as qh,
         nc.sbuf_tensor("l2_kh", [64, T], BF16) as kh,
         nc.sbuf_tensor("l2_vh", [128, 16, HD], BF16) as vh,
         nc.sbuf_tensor("l2_P", [128, T], BF16) as Pb,
         nc.sbuf_tensor("l2_PT", [128, 16, 128], BF16) as PT,
         nc.sbuf_tensor("l2_sq", [64, T], BF16) as sqb,
         nc.sbuf_tensor("l2_nb", [128, 16], F32) as nbc,
         nc.sbuf_tensor("l2_rs", [128, 16], F32) as rsc,
         nc.sbuf_tensor("l2_r16", [16, 128], F32) as r16,
     ):
         ck.join()
         whb = [wqkv0, wqkv1]
         wh_load = {}
         wh_free = [None, None]

         def load_head(h):
             if h >= NH or h in wh_load:
                 return
             b = whb[h % 2]
             after = [wh_free[h % 2]] if wh_free[h % 2] else []
             toks = [ck.dma("qp", b[:, :, 0:HD],
                            W["wq2"][h].rearrange("(kc kp) n -> kp kc n",
                                                  kp=128), after=after)]
             toks.append(ck.dma("qp", b[:, :, HD:2 * HD],
                                W["wk2"][h].rearrange("(kc kp) n -> kp kc n",
                                                      kp=128), after=after))
             toks.append(ck.dma("qp", b[:, :, 2 * HD:3 * HD],
                                W["wv2"][h].rearrange("(kc kp) n -> kp kc n",
                                                      kp=128), after=after))
             wh_load[h] = toks

         load_head(0)
         load_head(1)
         for h in range(NH):
             wh = whb[h % 2]
             ck.waits("pe", wh_load[h])
             for tc in range(NT):
                 tsl = slice(tc * 512, (tc + 1) * 512)
                 bq = bufs.ps_mid[0:64, 0:512]
                 for kc in range(KC):
                     ck.pe.matmul(bq, wh[:, kc, 0:HD], bufs.xn[:, kc, tsl],
                                  start=(kc == 0), stop=(kc == KC - 1))
                 tp = ck.tok("pe")
                 ck.wait("dve", tp)
                 ck.dve.tensor_copy(qh[:, tsl], bq)
                 tq = ck.tok("dve")
                 ck.wait("pe", tq)
                 for kc in range(KC):
                     ck.pe.matmul(bq, wh[:, kc, HD:2 * HD],
                                  bufs.xn[:, kc, tsl],
                                  start=(kc == 0), stop=(kc == KC - 1))
                 tp = ck.tok("pe")
                 ck.wait("dve", tp)
                 ck.dve.tensor_copy(kh[:, tsl], bq)
                 ck.wait("pe", ck.tok("dve"))
             for tt in range(16):
                 ttsl = slice(tt * 128, (tt + 1) * 128)
                 bv = bufs.ps_mid[:, 0:64]
                 for kc in range(KC):
                     ck.pe.matmul(bv, bufs.xn[:, kc, ttsl], wh[:, kc,
                                  2 * HD:3 * HD],
                                  start=(kc == 0), stop=(kc == KC - 1))
                 tp = ck.tok("pe")
                 ck.wait("dve", tp)
                 ck.dve.tensor_copy(vh[:, tt, :], bv)
                 ck.wait("pe", ck.tok("dve"))
             wh_free[h % 2] = ck.tok("pe")
             load_head(h + 1)
             tnb = None
             for qt in range(16):
                 qsl = slice(qt * 128, (qt + 1) * 128)
                 ck.wait("pe", bufs.big_free)
                 for kb in range(4):
                     ck.pe.matmul(bufs.ps_big[:, kb * 512:(kb + 1) * 512],
                                  qh[:, qsl], kh[:, kb * 512:(kb + 1) * 512],
                                  start=True, stop=True)
                 tp = ck.tok("pe")
                 ck.wait("dve", tp)
                 ck.vreduce(nbc[:, qt:qt + 1], bufs.ps_big[:, :], AX.X,
                            ALU.max)
                 ck.dve.tensor_scalar_mul(nbc[:, qt:qt + 1],
                                          nbc[:, qt:qt + 1], -0.125)
                 tnb = ck.tok("dve")
                 ck.waits("act", [tp, tnb])
                 ck.act.activation(Pb[:, :], bufs.ps_big[:, :], AF.Exp,
                                   bias=nbc[:, qt:qt + 1], scale=0.125,
                                   accum_out=rsc[:, qt:qt + 1])
                 ta = ck.tok("act")
                 bufs.big_free = ta
                 for half in range(2):
                     ck.wait("pe", ta)
                     ck.wait("pe", bufs.trb_free)
                     for b8 in range(8):
                         kt = half * 8 + b8
                         ck.pe.transpose(
                             bufs.ps_trb[:, b8 * 128:(b8 + 1) * 128],
                             Pb[:, kt * 128:(kt + 1) * 128], bufs.id_bf[:, :])
                     tp = ck.tok("pe")
                     ck.wait("dve", tp)
                     ck.dve.tensor_copy(
                         PT[:, half * 8:(half + 1) * 8, :],
                         bufs.ps_trb[:, :].rearrange("p (b n) -> p b n", n=128))
                     bufs.trb_free = ck.tok("dve")
                 tpt = ck.tok("dve")
                 ck.wait("pe", tpt)
                 for kt in range(16):
                     ck.pe.matmul(bufs.ps_mid[0:64, 0:128], vh[:, kt, :],
                                  PT[:, kt, :], start=(kt == 0),
                                  stop=(kt == 15))
                 tp = ck.tok("pe")
                 ck.wait("dve", ta)
                 ck.recip(rsc[:, qt:qt + 1], rsc[:, qt:qt + 1])
                 trr = ck.tok("dve")
                 ck.waits("pe", [trr, bufs.id_tok])
                 ck.pe.transpose(bufs.ps_mid[0:1, 384:512],
                                 rsc[:, qt:qt + 1], bufs.id_f32[:, :])
                 tp2 = ck.tok("pe")
                 ck.wait("dve", tp2)
                 ck.dve.tensor_copy(bufs.rowB[0:1, qsl],
                                    bufs.ps_mid[0:1, 384:512])
                 trow = ck.tok("dve")
                 ck.wait("sp", trow)
                 tu3 = ck.dma("qs", ck.scr_rows[18:19, qsl],
                              bufs.rowB[0:1, qsl])
                 tb3 = ck.dma("qs",
                              bufs.sc3[:, qsl].unsqueeze(1),
                              ck.scr_rows[18:19, qsl]
                              .partition_broadcast(128), after=[tu3])
                 ck.waits("dve", [tb3, ck.tok("pe")])
                 hc, hp = h // 2, (h % 2) * 64
                 ck.dve.tensor_mul(av_all[hp:hp + 64, hc, qsl],
                                   bufs.ps_mid[0:64, 0:128],
                                   bufs.sc3[hp:hp + 64, qsl])
                 ck.wait("pe", ck.tok("dve"))

    TCM = 256
    if "M" in phases:
     with (
         nc.sbuf_tensor("l2_xe", [128, 8, KC, TCM], BF16) as xe,
         nc.sbuf_tensor("l2_wo", [128, KC, 128], BF16) as wob,
         nc.sbuf_tensor("l2_we0", [128, KC, 128], BF16) as we0,
         nc.sbuf_tensor("l2_we1", [128, KC, 128], BF16) as we1,
         nc.sbuf_tensor("l2_gb", [128, TCM], F32) as gb,
     ):
         ck.join()
         for tc in range(T // TCM):
             tsl = slice(tc * TCM, (tc + 1) * TCM)
             tprev = ck.tok("dve")
             for e in range(8):
                 tg = ck.dma("qs", gb[:, :].unsqueeze(1),
                             ck.scr_grows[e:e + 1, tsl]
                             .partition_broadcast(128), after=[tprev])
                 ck.wait("dve", tg)
                 for kc in range(KC):
                     ck.dve.tensor_mul(xe[:, e, kc, :], bufs.xn[:, kc, tsl],
                                       gb[:, :])
                 tprev = ck.tok("dve")
             txe = ck.tok("dve")
             web = [we0, we1]
             for oc in range(KC):
                 two = ck.dma("qp", wob[:, :, :],
                              W["wo2"][:, oc * 128:(oc + 1) * 128]
                              .rearrange("(kc kp) n -> kp kc n", kp=128),
                              after=[bufs.wo_free] if bufs.wo_free else [])
                 bk = bufs.ps_big[:, (oc % 4) * 512:(oc % 4) * 512 + TCM]
                 ck.wait("pe", bufs.big_free)
                 ck.waits("pe", [two, txe])
                 for kc in range(KC):
                     ck.pe.matmul(bk, wob[:, kc, :], av_all[:, kc, tsl],
                                  start=(kc == 0), stop=False)
                 bufs.wo_free = ck.tok("pe")
                 for e in range(8):
                     tw = ck.dma("qp", web[e % 2][:, :, :],
                                 W["we2"][e][:, oc * 128:(oc + 1) * 128]
                                 .rearrange("(kc kp) n -> kp kc n", kp=128),
                                 after=[bufs.we_free[e % 2]]
                                 if bufs.we_free[e % 2] else [])
                     ck.wait("pe", tw)
                     for kc in range(KC):
                         ck.pe.matmul(bk, web[e % 2][:, kc, :],
                                      xe[:, e, kc, :], start=False,
                                      stop=(e == 7 and kc == KC - 1))
                     bufs.we_free[e % 2] = ck.tok("pe")
                 tp = ck.tok("pe")
                 ck.wait("dve", tp)
                 ck.dve.tensor_add(bufs.X[:, oc, tsl], bufs.X[:, oc, tsl], bk)
                 bufs.big_free = ck.tok("dve")
    avg.__exit__(None, None, None)




def quantize_bfp_weight(ck, bufs, w_ap, N, Kd, scr_ap, name):
    nc = ck.nc
    kc = Kd // 128
    nchunk = N // 128
    nblk = Kd // 32
    with (
        nc.sbuf_tensor(f"{name}_f", [128, Kd], F32) as fb,
        nc.sbuf_tensor(f"{name}_q", [128, Kd], BF16) as qb,
        nc.sbuf_tensor(f"{name}_am", [128, nblk], F32) as am,
        nc.sbuf_tensor(f"{name}_rc", [128, nblk], F32) as rc,
        nc.sbuf_tensor(f"{name}_st", [128, 1024], BF16) as stg,
    ):
        ck.join()
        prev = None
        prev_store = [None]
        for r in range(nchunk):
            tl = ck.dma("qs", fb[:, :], w_ap[r * 128:(r + 1) * 128, :],
                        after=[prev] if prev else [])
            ck.wait("dve", tl)
            v3 = fb[:, :].rearrange("p (b j) -> p b j", j=32)
            ck.vreduce(am[:, :], v3, AX.X, ALU.max,
                       apply_absolute_value=True)
            ck.dve.tensor_scalar_max(am[:, :], am[:, :], 1.17549435e-38)
            ck.dve.tensor_scalar(am[:, :].bitcast(mybir.dt.int32),
                                 am[:, :].bitcast(mybir.dt.int32),
                                 0x7F800000, None, ALU.bitwise_and)
            ck.dve.drain()
            ck.recip(rc[:, :], am[:, :])
            ck.dve.tensor_scalar_mul(rc[:, :], rc[:, :], 8.0)
            ck.dve.tensor_scalar_mul(am[:, :], am[:, :], 0.125)
            ck.dve.drain()
            rcb = rc[:, :].unsqueeze(2).broadcast_to([128, nblk, 32])
            amb = am[:, :].unsqueeze(2).broadcast_to([128, nblk, 32])
            ck.dve.tensor_tensor(v3, v3, rcb, ALU.mult)
            ck.dve.tensor_scalar(fb[:, :], fb[:, :], RMAGIC, RMAGIC,
                                 ALU.add, ALU.subtract)
            ck.dve.tensor_tensor(qb[:, :].rearrange("p (b j) -> p b j", j=32),
                                 fb[:, :].rearrange("p (b j) -> p b j", j=32),
                                 amb, ALU.mult)
            tq = ck.tok("dve")
            prev = tq
            for g in range(kc // 8):
                ck.waits("pe", [tq, bufs.id_tok, bufs.trb_free])
                for b8 in range(8):
                    kb = g * 8 + b8
                    ck.pe.transpose(bufs.ps_trb[:, b8 * 128:(b8 + 1) * 128],
                                    qb[:, kb * 128:(kb + 1) * 128],
                                    bufs.id_bf[:, :])
                tp = ck.tok("pe")
                ck.wait("dve", tp)
                ck.wait("dve", prev_store[0] if prev_store else None)
                ck.dve.tensor_copy(stg[:, :], bufs.ps_trb[:, :])
                bufs.trb_free = ck.tok("dve")
                ts_ = ck.dma("qs",
                             scr_ap[:, g * 8:(g + 1) * 8,
                                    r * 128:(r + 1) * 128],
                             stg[:, :].rearrange("p (b n) -> p b n", n=128),
                             after=[bufs.trb_free])
                prev_store[0] = ts_
            ck.wait("dve", ck.tok("pe"))
        ck.waits("sp", [prev_store[0]])


def emit_l3(ck, bufs, ln, W, cols, phases="AF"):
    import os as _os
    nc = ck.nc
    NH, HD = 16, 64

    avg = nc.sbuf_tensor("l3_av", [128, KC, T], BF16)
    av_all = avg.__enter__()
    if "A" not in phases:
        ck.join()
        ck.pool.memset(av_all[:, :, :], 0.0)
        ck.join()
    else:
     with (
        nc.sbuf_tensor("l3_em", [128, 2176], BF16) as em,
        nc.sbuf_tensor("l3_wq0", [128, KC, HD], BF16) as wq0,
        nc.sbuf_tensor("l3_wk0", [128, KC, HD], BF16) as wk0,
        nc.sbuf_tensor("l3_wv0", [128, KC, HD], BF16) as wv0,
        nc.sbuf_tensor("l3_wq1", [128, KC, HD], BF16) as wq1,
        nc.sbuf_tensor("l3_wk1", [128, KC, HD], BF16) as wk1,
        nc.sbuf_tensor("l3_wv1", [128, KC, HD], BF16) as wv1,
        nc.sbuf_tensor("l3_qh", [64, T], BF16) as qh,
        nc.sbuf_tensor("l3_kh", [64, T], BF16) as kh,
        nc.sbuf_tensor("l3_vh", [128, 16, HD], BF16) as vh,
        nc.sbuf_tensor("l3_P", [128, T], BF16) as Pb,
        nc.sbuf_tensor("l3_PT", [128, 16, 128], BF16) as PT,
        nc.sbuf_tensor("l3_sq", [64, T], BF16) as sqb,
        nc.sbuf_tensor("l3_nb", [128, 16], F32) as nbc,
        nc.sbuf_tensor("l3_rs", [128, 16], F32) as rsc,
     ):
        ck.join()
        t_em = ck.dma("qs", em[:, :], W["em"])
        bufs.vbrow_tok = ck.dma("qs", bufs.sc1[:, 0:1024].unsqueeze(1),
                                W["bv"].partition_broadcast(128))
        whb = [[wq0, wk0, wv0], [wq1, wk1, wv1]]
        wh_load = {}
        wh_free = [None, None]

        def load_head(h):
            if h >= NH or h in wh_load:
                return
            b = whb[h % 2]
            after = [wh_free[h % 2]] if wh_free[h % 2] else []
            toks = []
            for i in range(3):
                colsl = slice(i * D + h * HD, i * D + (h + 1) * HD)
                toks.append(ck.dma(
                    "qp", b[i][:, :, :],
                    W["wqkv"][:, colsl].rearrange("(kc kp) n -> kp kc n",
                                                  kp=128), after=after))
            wh_load[h] = toks

        load_head(0)
        load_head(1)
        for h in range(NH):
            wh = whb[h % 2]
            ck.waits("pe", wh_load[h])
            for tc in range(NT):
                tsl = slice(tc * 512, (tc + 1) * 512)
                bq = bufs.ps_mid[0:64, 0:512]
                for kc in range(KC):
                    ck.pe.matmul(bq, wh[0][:, kc, :], bufs.xn[:, kc, tsl],
                                 start=(kc == 0), stop=(kc == KC - 1))
                tp = ck.tok("pe")
                ck.wait("dve", tp)
                ck.dve.tensor_scalar(qh[:, tsl], bq,
                                     cols["l3_bq"]
                                     [(h % 2) * 64:(h % 2) * 64 + 64,
                                      h // 2:h // 2 + 1], None, ALU.add)
                ta = ck.tok("dve")
                ck.wait("pe", ta)
                for kc in range(KC):
                    ck.pe.matmul(bq, wh[1][:, kc, :], bufs.xn[:, kc, tsl],
                                 start=(kc == 0), stop=(kc == KC - 1))
                tp = ck.tok("pe")
                ck.wait("dve", tp)
                ck.dve.tensor_scalar(kh[:, tsl], bq,
                                     cols["l3_bk"]
                                     [(h % 2) * 64:(h % 2) * 64 + 64,
                                      h // 2:h // 2 + 1], None, ALU.add)
                ta = ck.tok("dve")
                ck.wait("pe", ta)
            for tt in range(16):
                ttsl = slice(tt * 128, (tt + 1) * 128)
                bv = bufs.ps_mid[:, 0:64]
                for kc in range(KC):
                    ck.pe.matmul(bv, bufs.xn[:, kc, ttsl], wh[2][:, kc, :],
                                 start=(kc == 0), stop=(kc == KC - 1))
                tp = ck.tok("pe")
                ck.waits("dve", [tp, bufs.vbrow_tok])
                ck.dve.tensor_add(vh[:, tt, :], bv,
                                  bufs.sc1[:, h * 64:(h + 1) * 64])
                ck.wait("pe", ck.tok("dve"))
            wh_free[h % 2] = ck.tok("pe")
            load_head(h + 1)
            tnb = None
            for qt in range(16):
                qsl = slice(qt * 128, (qt + 1) * 128)
                Wk = (qt + 1) * 128
                nkb = (Wk + 511) // 512
                ck.wait("pe", bufs.big_free)
                for kb in range(nkb):
                    wkb = min(512, Wk - kb * 512)
                    ck.pe.matmul(bufs.ps_big[:, kb * 512:kb * 512 + wkb],
                                 qh[:, qsl], kh[:, kb * 512:kb * 512 + wkb],
                                 start=True, stop=True)
                tp = ck.tok("pe")
                ck.wait("dve", tp)
                ck.vreduce(nbc[:, qt:qt + 1], bufs.ps_big[:, 0:Wk], AX.X,
                           ALU.max)
                ck.dve.tensor_scalar_mul(nbc[:, qt:qt + 1],
                                         nbc[:, qt:qt + 1], -0.125)
                tnb = ck.tok("dve")
                ck.waits("act", [tp, t_em, tnb])
                ck.act.activation(Pb[:, 0:Wk], bufs.ps_big[:, 0:Wk], AF.Exp,
                                  bias=nbc[:, qt:qt + 1], scale=0.125)
                ta = ck.tok("act")
                ck.wait("dve", ta)
                ck.dve.scalar_tensor_tensor(
                    Pb[:, 0:Wk], Pb[:, 0:Wk], 0.0,
                    em[:, 2048 - qt * 128:2048 - qt * 128 + Wk],
                    ALU.add, ALU.mult, accum_out=rsc[:, qt:qt + 1])
                tmsk = ck.tok("dve")
                bufs.big_free = tmsk
                nkt = qt + 1
                for half in range((nkt + 7) // 8):
                    n8 = min(8, nkt - half * 8)
                    ck.wait("pe", tmsk)
                    ck.wait("pe", bufs.trb_free)
                    for b8 in range(n8):
                        kt = half * 8 + b8
                        ck.pe.transpose(
                            bufs.ps_trb[:, b8 * 128:(b8 + 1) * 128],
                            Pb[:, kt * 128:(kt + 1) * 128], bufs.id_bf[:, :])
                    tp = ck.tok("pe")
                    ck.wait("dve", tp)
                    ck.dve.tensor_copy(
                        PT[:, half * 8:half * 8 + n8, :],
                        bufs.ps_trb[:, 0:n8 * 128]
                        .rearrange("p (b n) -> p b n", n=128))
                    bufs.trb_free = ck.tok("dve")
                tpt = ck.tok("dve")
                ck.wait("pe", tpt)
                for kt in range(nkt):
                    ck.pe.matmul(bufs.ps_mid[0:64, 0:128], vh[:, kt, :],
                                 PT[:, kt, :], start=(kt == 0),
                                 stop=(kt == nkt - 1))
                tp = ck.tok("pe")
                ck.wait("dve", ta)
                ck.recip(rsc[:, qt:qt + 1], rsc[:, qt:qt + 1])
                trr = ck.tok("dve")
                ck.waits("pe", [trr, bufs.id_tok])
                ck.pe.transpose(bufs.ps_mid[0:1, 384:512],
                                rsc[:, qt:qt + 1], bufs.id_f32[:, :])
                tp2 = ck.tok("pe")
                ck.wait("dve", tp2)
                ck.dve.tensor_copy(bufs.rowB[0:1, qsl],
                                   bufs.ps_mid[0:1, 384:512])
                trow = ck.tok("dve")
                ck.wait("sp", trow)
                tu3 = ck.dma("qs", ck.scr_rows[18:19, qsl],
                             bufs.rowB[0:1, qsl])
                tb3 = ck.dma("qs", bufs.sc3[:, qsl].unsqueeze(1),
                             ck.scr_rows[18:19, qsl]
                             .partition_broadcast(128), after=[tu3])
                ck.waits("dve", [tb3, ck.tok("pe")])
                hc, hp = h // 2, (h % 2) * 64
                ck.dve.tensor_mul(av_all[hp:hp + 64, hc, qsl],
                                  bufs.ps_mid[0:64, 0:128],
                                  bufs.sc3[hp:hp + 64, qsl])
                ck.wait("pe", ck.tok("dve"))
            if _os.environ.get("KDBG_L3A") and h == 0:
                d_rs = nc.dram_tensor("d_rs", [128, 16], F32,
                                      kind="ExternalOutput").ap()
                d_nb = nc.dram_tensor("d_nb", [128, 16], F32,
                                      kind="ExternalOutput").ap()
                d_q3 = nc.dram_tensor("d_q3", [64, T], BF16,
                                      kind="ExternalOutput").ap()
                d_k3 = nc.dram_tensor("d_k3", [64, T], BF16,
                                      kind="ExternalOutput").ap()
                d_P3 = nc.dram_tensor("d_P3", [128, T], BF16,
                                      kind="ExternalOutput").ap()
                ck.join()
                ck.dma("qs", d_rs, rsc[:, :])
                ck.dma("qs", d_nb, nbc[:, :])
                ck.dma("qs", d_q3, qh[:, :])
                ck.dma("qs", d_k3, kh[:, :])
                ck.dma("qs", d_P3, Pb[:, :])
                ck.join()

    quantize_bfp_weight(ck, bufs, W["wfc"], 4096, D, W["scr_fc"], "l3fc")
    quantize_bfp_weight(ck, bufs, W["wproj"], D, 4096, W["scr_pj"], "l3pj")
    TC3 = 256
    with (
        nc.sbuf_tensor("l3_tm", [128, 2048], BF16) as tmb,
        nc.sbuf_tensor("l3_am", [128, 64], F32) as am3,
        nc.sbuf_tensor("l3_rc", [128, 64], F32) as rc3,
        nc.sbuf_tensor("l3_h", [128, 32, TC3], BF16) as hbuf,
        nc.sbuf_tensor("l3_wo", [128, KC, 128], BF16) as wo3,
    ):
        ck.join()
        xq = bufs.xn

        def quant_tm(width):
            nb = width // 32
            v3 = tmb[:, 0:width].rearrange("p (b j) -> p b j", j=32)
            ck.vreduce(am3[:, 0:nb], v3, AX.X, ALU.max,
                       apply_absolute_value=True)
            ck.dve.tensor_scalar_max(am3[:, 0:nb], am3[:, 0:nb],
                                     1.17549435e-38)
            ck.dve.tensor_scalar(am3[:, 0:nb].bitcast(mybir.dt.int32),
                                 am3[:, 0:nb].bitcast(mybir.dt.int32),
                                 0x7F800000, None, ALU.bitwise_and)
            ck.dve.drain()
            ck.recip(rc3[:, 0:nb], am3[:, 0:nb])
            ck.dve.tensor_scalar_mul(rc3[:, 0:nb], rc3[:, 0:nb], 8.0)
            ck.dve.tensor_scalar_mul(am3[:, 0:nb], am3[:, 0:nb], 0.125)
            ck.dve.drain()
            rcb = rc3[:, 0:nb].unsqueeze(2).broadcast_to([128, nb, 32])
            amb = am3[:, 0:nb].unsqueeze(2).broadcast_to([128, nb, 32])
            ck.dve.tensor_tensor(bufs.sc2[:, 0:width]
                                 .rearrange("p (b j) -> p b j", j=32),
                                 v3, rcb, ALU.mult)
            ck.dve.tensor_scalar(bufs.sc2[:, 0:width], bufs.sc2[:, 0:width],
                                 RMAGIC, RMAGIC, ALU.add, ALU.subtract)
            ck.dve.tensor_tensor(v3, bufs.sc2[:, 0:width]
                                 .rearrange("p (b j) -> p b j", j=32),
                                 amb, ALU.mult)
            return ck.tok("dve")

        for tt in range(16):
            ttsl = slice(tt * 128, (tt + 1) * 128)
            ck.wait("pe", bufs.trb_free)
            ck.wait("pe", bufs.id_tok)
            for c in range(KC):
                ck.pe.transpose(bufs.ps_trb[:, c * 128:(c + 1) * 128],
                                bufs.xn[:, c, ttsl], bufs.id_bf[:, :])
            tp = ck.tok("pe")
            ck.wait("dve", tp)
            ck.dve.tensor_copy(tmb[:, 0:1024], bufs.ps_trb[:, :])
            bufs.trb_free = ck.tok("dve")
            tq = quant_tm(1024)
            ck.waits("pe", [tq, bufs.trb_free])
            for c in range(KC):
                ck.pe.transpose(bufs.ps_trb[:, c * 128:(c + 1) * 128],
                                tmb[:, c * 128:(c + 1) * 128],
                                bufs.id_bf[:, :])
            tp = ck.tok("pe")
            ck.wait("dve", tp)
            for c in range(KC):
                ck.dve.tensor_copy(xq[:, c, ttsl],
                                   bufs.ps_trb[:, c * 128:(c + 1) * 128])
            bufs.trb_free = ck.tok("dve")

        sfc = ScrStream(ck, "l3sfc", W["scr_fc"], KC, 4096, nb=128)
        spj = ScrStream(ck, "l3spj", W["scr_pj"], 32, D, nb=128)
        for tc in range(T // TC3):
            tsl = slice(tc * TC3, (tc + 1) * TC3)
            sfc.reset(); spj.reset()
            sfc.start()
            htoks = {}
            for n in range(32):
                wt = sfc.get(n)
                bk = bufs.ps_big[:, (n % 2) * 512:(n % 2) * 512 + TC3]
                if n >= 2:
                    ck.wait("pe", htoks[n - 2])
                for kc in range(KC):
                    ck.pe.matmul(bk, wt[:, kc, :], xq[:, kc, tsl],
                                 start=(kc == 0), stop=(kc == KC - 1))
                sfc.done(n)
                tp = ck.tok("pe")
                ck.wait("act", tp)
                ck.act.activation(hbuf[:, n, :], bk, AF.Gelu,
                                  bias=cols["l3_bfc"][:, n:n + 1])
                htoks[n] = ck.tok("act")
            for tt in range(TC3 // 128):
                t0 = tt * 128
                for half in range(2):
                    c0 = half * 16
                    for g in range(2):
                        ck.waits("pe", [htoks[31], bufs.trb_free,
                                        bufs.id_tok])
                        for b8 in range(8):
                            c = c0 + g * 8 + b8
                            ck.pe.transpose(
                                bufs.ps_trb[:, b8 * 128:(b8 + 1) * 128],
                                hbuf[:, c, t0:t0 + 128], bufs.id_bf[:, :])
                        tp = ck.tok("pe")
                        ck.wait("dve", tp)
                        ck.dve.tensor_copy(tmb[:, g * 1024:(g + 1) * 1024],
                                           bufs.ps_trb[:, :])
                        bufs.trb_free = ck.tok("dve")
                    tq = quant_tm(2048)
                    ck.waits("pe", [tq, bufs.trb_free])
                    for g in range(2):
                        ck.wait("pe", bufs.trb_free)
                        for b8 in range(8):
                            cc = g * 8 + b8
                            ck.pe.transpose(
                                bufs.ps_trb[:, b8 * 128:(b8 + 1) * 128],
                                tmb[:, cc * 128:(cc + 1) * 128],
                                bufs.id_bf[:, :])
                        tp = ck.tok("pe")
                        ck.wait("dve", tp)
                        for b8 in range(8):
                            cc = g * 8 + b8
                            ck.dve.tensor_copy(
                                hbuf[:, c0 + cc, t0:t0 + 128],
                                bufs.ps_trb[:, b8 * 128:(b8 + 1) * 128])
                        bufs.trb_free = ck.tok("dve")
            thq = ck.tok("dve")
            xtk = {}
            for oc in range(KC):
                two = ck.dma(
                    "qp", wo3[:, :, :],
                    W["wout"][:, oc * 128:(oc + 1) * 128]
                    .rearrange("(kc kp) n -> kp kc n", kp=128),
                    after=[bufs.wo_free] if bufs.wo_free else [])
                wt = spj.get(oc)
                bk = bufs.ps_big[:, 1024 + (oc % 2) * 512:
                                 1024 + (oc % 2) * 512 + TC3]
                if oc >= 2:
                    ck.wait("pe", xtk[oc - 2])
                ck.waits("pe", [two, thq])
                for kc in range(KC):
                    ck.pe.matmul(bk, wo3[:, kc, :], av_all[:, kc, tsl],
                                 start=(kc == 0), stop=False)
                bufs.wo_free = ck.tok("pe")
                for kc in range(32):
                    ck.pe.matmul(bk, wt[:, kc, :], hbuf[:, kc, :],
                                 start=False, stop=(kc == 31))
                spj.done(oc)
                tp = ck.tok("pe")
                ck.wait("dve", tp)
                ck.dve.scalar_tensor_tensor(
                    bufs.sc3[:, 0:TC3], bk, cols["l3_bo"][:, oc:oc + 1],
                    bufs.X[:, oc, tsl], ALU.add, ALU.add)
                ck.dve.tensor_scalar(bufs.X[:, oc, tsl], bufs.sc3[:, 0:TC3],
                                     cols["l3_bpj"][:, oc:oc + 1], None,
                                     ALU.add)
                xtk[oc] = ck.tok("dve")
            ck.wait("pe", xtk[KC - 1])
        spj.close()
        sfc.close()
    avg.__exit__(None, None, None)




def emit_l0(ck, bufs, ln, W, cols, phases="FM"):
    import os as _os
    nc = ck.nc
    DI = 2048
    CI = DI // 128

    if "F" in phases:
     with (
        nc.sbuf_tensor("l0_wd1", [128, KC, 256], BF16) as wd1a,
        nc.sbuf_tensor("l0_wd1b", [128, KC, 256], BF16) as wd1b,
        nc.sbuf_tensor("l0_wd2", [128, 2, 1024], BF16) as wd2a,
        nc.sbuf_tensor("l0_wd2b", [128, 2, 1024], BF16) as wd2b,
        nc.sbuf_tensor("l0_he", [128, 2, T], BF16) as he,
        nc.sbuf_tensor("l0_lacc", [128, 12], F32) as lacc,
        nc.sbuf_tensor("l0_a", [128, 4, T], BF16) as aab,
        nc.sbuf_tensor("l0_wad", [128, KC, 128], BF16) as wad,
        nc.sbuf_tensor("l0_wau", [128, 4, 128], BF16) as wau,
        nc.sbuf_tensor("l0_wbb", [128, KC, 128], BF16) as wbb,
        nc.sbuf_tensor("l0_ohc", [128, 4], F32) as ohc,
        nc.sbuf_tensor("l0_oh8", [8, 1], BF16) as oh8,
        nc.sbuf_tensor("l0_bup", [128, 8], F32) as bup,
        nc.sbuf_tensor("l0_baupw", [8, 1024], BF16) as baupw,
     ):
        ck.join()
        wd1s = [wd1a, wd1b]
        wd2s = [wd2a, wd2b]
        ld1 = {}
        ld2 = {}
        fr1 = [None, None]
        fr2 = [None, None]

        def load_e(e):
            if e >= 8 or e in ld1:
                return
            b = e % 2
            ld1[e] = ck.dma("qp", wd1s[b][:, :, :],
                            W["wd1"][e].rearrange("(kc kp) n -> kp kc n",
                                                  kp=128),
                            after=[fr1[b]] if fr1[b] else [])
            ld2[e] = ck.dma("qp", wd2s[b][:, :, :],
                            W["wd2"][e].rearrange("(kc kp) n -> kp kc n",
                                                  kp=128),
                            after=[fr2[b]] if fr2[b] else [])

        load_e(0)
        load_e(1)
        t_bw = ck.dma("qp", baupw[:, :], W["baup"])
        ck.wait("pool", bufs.ones_tok)
        ck.pool.memset(lacc[:, 0:12], 0.0)
        tz = ck.tok("pool")
        for e in range(8):
            b = e % 2
            ck.waits("pe", [ld1[e]])
            for n2 in range(2):
                for tc in range(NT):
                    tsl = slice(tc * 512, (tc + 1) * 512)
                    bk = bufs.ps_big[:, ((n2 * NT + tc) % 2) * 512:
                                     ((n2 * NT + tc) % 2) * 512 + 512]
                    ck.wait("pe", bufs.big_free)
                    for kc in range(KC):
                        ck.pe.matmul(bk, wd1s[b][:, kc,
                                     n2 * 128:(n2 + 1) * 128],
                                     bufs.xn[:, kc, tsl],
                                     start=(kc == 0), stop=(kc == KC - 1))
                    tp = ck.tok("pe")
                    ck.wait("act", tp)
                    ck.act.activation(he[:, n2, tsl], bk, AF.Relu,
                                      bias=cols["l0_bd1"][:, e * 2 + n2:
                                                          e * 2 + n2 + 1])
                    bufs.big_free = ck.tok("act")
            fr1[b] = ck.tok("pe")
            the = ck.tok("act")
            ck.waits("pe", [ld2[e], the])
            for oc in range(KC):
                for tc in range(NT):
                    tsl = slice(tc * 512, (tc + 1) * 512)
                    bk = bufs.ps_big[:, ((oc * NT + tc) % 2) * 512:
                                     ((oc * NT + tc) % 2) * 512 + 512]
                    ck.wait("pe", bufs.big_free)
                    for kc in range(2):
                        ck.pe.matmul(bk, wd2s[b][:, kc,
                                     oc * 128:(oc + 1) * 128],
                                     he[:, kc, tsl],
                                     start=(kc == 0), stop=(kc == 1))
                    tp = ck.tok("pe")
                    ck.waits("dve", [tp, tz])
                    ck.dve.scalar_tensor_tensor(
                        bufs.scb1[:, 0:512], bk,
                        cols["l0_bd2"][:, e * 8 + oc:e * 8 + oc + 1],
                        bufs.xn[:, oc, tsl], ALU.add, ALU.subtract)
                    td = ck.tok("dve")
                    bufs.big_free = td
                    ck.wait("act", td)
                    ck.act.activation(bufs.scb2[:, 0:512],
                                      bufs.scb1[:, 0:512], AF.Square,
                                      accum_out=lacc[:, 9:10])
                    ta = ck.tok("act")
                    ck.wait("dve", ta)
                    ck.dve.tensor_add(lacc[:, e:e + 1], lacc[:, e:e + 1],
                                      lacc[:, 9:10])
            fr2[b] = ck.tok("pe")
            load_e(e + 1)
        td = ck.tok("dve")
        ck.waits("pe", [td, bufs.ones_tok])
        ck.pe.matmul(bufs.ps_row[0:1, 0:8], bufs.ones_f32[:, :],
                     lacc[:, 0:8], start=True, stop=True)
        tp = ck.tok("pe")
        ck.wait("dve", tp)
        ck.dve.tensor_copy(bufs.rowA[0:1, 0:8], bufs.ps_row[0:1, 0:8])
        ck.vreduce(bufs.rowA[0:1, 8:9], bufs.rowA[0:1, 0:8], AX.X, ALU.min)
        ck.dve.tensor_scalar(bufs.rowA[0:1, 0:8], bufs.rowA[0:1, 0:8],
                             bufs.rowA[0:1, 8:9], None, ALU.is_le)
        toh = ck.tok("dve")
        ck.wait("sp", toh)
        tup = ck.dma("qs", ck.scr_rows[20:21, 0:8], bufs.rowA[0:1, 0:8])
        ohds = []
        ohv = ck.scr_rows[20:21, 0:8].rearrange("one e -> (one e)") \
            .unsqueeze(1).broadcast_to([8, 64])
        for r in range(4):
            ohds.append(ck.dma("qs", ohc[:, r:r + 1],
                               ohv[2 * r:2 * r + 2, :], after=[tup]))
        t8 = ck.dma("qp", oh8[:, :],
                    ck.scr_rows[20:21, 0:8]
                    .rearrange("one e -> (one e)").unsqueeze(1),
                    after=[tup])
        ck.waits("pe", [t8, t_bw])
        for oc in range(KC):
            ck.pe.matmul(bufs.ps_mid[0:128, 256 + 2 * oc:257 + 2 * oc],
                         baupw[:, oc * 128:(oc + 1) * 128], oh8[:, :],
                         start=True, stop=True)
        tp = ck.tok("pe")
        ck.wait("dve", tp)
        ck.dve.tensor_copy(bup[:, 0:8],
                           bufs.ps_mid[:, 256:272]
                           .rearrange("p (n two) -> p n two", two=2)
                           [:, :, 0])
        ck.dve.tensor_scalar(bup[:, 0:8], bup[:, 0:8], None, None,
                             ALU.bypass) if False else None
        ck.dve.tensor_tensor(bup[:, 0:8], bup[:, 0:8],
                             cols["l0_bb"][:, 0:8], ALU.add)
        tbup = ck.tok("dve")
        tld = ck.dma("qp", wad[:, :, :], W["wadown"][:, 0:128]
                     .rearrange("(kc kp) n -> kp kc n", kp=128))
        for r in range(4):
            tnext = None
            if r < 3:
                pass
            ck.waits("pe", [tld])
            for tc in range(NT):
                tsl = slice(tc * 512, (tc + 1) * 512)
                bk = bufs.ps_big[:, (tc % 2) * 512:(tc % 2) * 512 + 512]
                ck.wait("pe", bufs.big_free)
                for kc in range(KC):
                    ck.pe.matmul(bk, wad[:, kc, :], bufs.xn[:, kc, tsl],
                                 start=(kc == 0), stop=(kc == KC - 1))
                tp = ck.tok("pe")
                ck.wait("act", tp)
                ck.act.activation(bufs.scb1[:, 0:512], bk, AF.Relu,
                                  bias=cols["l0_bad"][:, r:r + 1])
                ta = ck.tok("act")
                bufs.big_free = ta
                ck.waits("dve", [ta] + ohds)
                ck.dve.tensor_scalar(aab[:, r, tsl], bufs.scb1[:, 0:512],
                                     ohc[:, r:r + 1], None, ALU.mult)
            tprev = ck.tok("pe")
            if r < 3:
                tld = ck.dma("qp", wad[:, :, :],
                             W["wadown"][:, (r + 1) * 128:(r + 2) * 128]
                             .rearrange("(kc kp) n -> kp kc n", kp=128),
                             after=[tprev])
        ta_done = ck.tok("dve")
        tlu = ck.dma("qp", wau[:, :, :], W["waup"][:, 0:128]
                     .rearrange("(kc kp) n -> kp kc n", kp=128))
        tlb = ck.dma("qp", wbb[:, :, :], W["wbase"][:, 0:128]
                     .rearrange("(kc kp) n -> kp kc n", kp=128))
        for oc in range(KC):
            ck.waits("pe", [tlu, tlb, ta_done])
            for tc in range(NT):
                tsl = slice(tc * 512, (tc + 1) * 512)
                bk = bufs.ps_big[:, (tc % 2) * 512:(tc % 2) * 512 + 512]
                ck.wait("pe", bufs.big_free)
                for kc in range(KC):
                    ck.pe.matmul(bk, wbb[:, kc, :], bufs.xn[:, kc, tsl],
                                 start=(kc == 0), stop=False)
                for kc in range(4):
                    ck.pe.matmul(bk, wau[:, kc, :], aab[:, kc, tsl],
                                 start=False, stop=(kc == 3))
                tp = ck.tok("pe")
                ck.waits("dve", [tp, tbup])
                ck.dve.scalar_tensor_tensor(
                    bufs.X[:, oc, tsl], bk, bup[:, oc:oc + 1],
                    bufs.X[:, oc, tsl], ALU.add, ALU.add)
                bufs.big_free = ck.tok("dve")
            tprev = ck.tok("pe")
            if oc < KC - 1:
                tlu = ck.dma("qp", wau[:, :, :],
                             W["waup"][:, (oc + 1) * 128:(oc + 2) * 128]
                             .rearrange("(kc kp) n -> kp kc n", kp=128),
                             after=[tprev])
                tlb = ck.dma("qp", wbb[:, :, :],
                             W["wbase"][:, (oc + 1) * 128:(oc + 2) * 128]
                             .rearrange("(kc kp) n -> kp kc n", kp=128),
                             after=[tprev])

    if "M" not in phases:
        return
    with (
        nc.sbuf_tensor("l0_win", [128, CI, 128], BF16) as winb,
        nc.sbuf_tensor("l0_win2", [128, CI, 128], BF16) as winb2,
        nc.sbuf_tensor("l0_xp", [128, CI, 2052], BF16) as xp,
    ):
        ck.join()
        inv_n = 1.0 / D
        for tc in range(NT):
            sl = slice(tc * 512, (tc + 1) * 512)
            for c in range(KC):
                tprev = ck.tok("pe") if c > 0 else None
                if tprev:
                    ck.wait("act", tprev)
                ck.act.activation(ln.sq[:, 0:512], bufs.xn[:, c, sl],
                                  AF.Square)
                tq = ck.tok("act")
                ck.waits("pe", [tq, bufs.ones_tok])
                ck.pe.matmul(ln.ps_q, bufs.ones_bf[:, :], ln.sq[:, 0:512],
                             start=(c == 0), stop=(c == KC - 1))
            tp = ck.tok("pe")
            ck.wait("dve", tp)
            ck.dve.tensor_scalar_mul(bufs.rowB[0:1, sl], ln.ps_q, inv_n)
            trow = ck.tok("dve")
            ck.wait("act", trow)
            ck.act.activation(bufs.rowB[0:1, sl], bufs.rowB[0:1, sl], AF.Ln,
                              bias=1e-6)
            ck.act.activation(bufs.rowB[0:1, sl], bufs.rowB[0:1, sl], AF.Exp,
                              scale=-0.5)
            ck.wait("pe", ck.tok("act"))
        ta = ck.tok("act")
        for c in range(KC):
            ck.dve.tensor_add(bufs.X[:, c, :], bufs.X[:, c, :],
                              bufs.xn[:, c, :])
        t_xadd = ck.tok("dve")
        t_xpage = ck.dma("qs", ck.scr_x, bufs.X[:, :, :], after=[t_xadd])
        ck.wait("sp", ta)
        rb = bcast_rows(ck, ck.scr_rows, [bufs.rowB[0:1, :]],
                        [bufs.sc1[:, :]])
        for c in range(KC):
            ck.waits("dve", rb)
            ck.dve.scalar_tensor_tensor(bufs.xn[:, c, :], bufs.xn[:, c, :],
                                        cols["l0_nw"][:, c:c + 1],
                                        bufs.sc1[:, :], ALU.mult, ALU.mult)
        txr = ck.tok("dve")
        xr = bufs.xn

        ck.wait("pool", txr)
        for c in range(CI):
            ck.pool.memset(xp[:, c, 0:3], 0.0)
        tpad = ck.tok("pool")
        wins = [winb, winb2]
        tw = {}
        cons = {}

        def load_blk(n):
            if n >= 32 or n in tw:
                return
            after = [cons[n - 2]] if (n - 2) in cons else []
            tw[n] = ck.dma("qp", wins[n % 2][:, 0:KC, :],
                           W["win"][:, n * 128:(n + 1) * 128]
                           .rearrange("(kc kp) n -> kp kc n", kp=128),
                           after=after)

        load_blk(0)
        load_blk(1)
        for n in range(32):
            b = n % 2
            ck.waits("pe", [tw[n], txr])
            for tc in range(NT):
                tsl = slice(tc * 512, (tc + 1) * 512)
                bk = bufs.ps_big[:, (tc % 2) * 512:(tc % 2) * 512 + 512]
                ck.wait("pe", bufs.big_free)
                for kc in range(KC):
                    ck.pe.matmul(bk, wins[b][:, kc, :], xr[:, kc, tsl],
                                 start=(kc == 0), stop=(kc == KC - 1))
                tp = ck.tok("pe")
                ck.wait("dve", tp)
                if n < CI:
                    ck.wait("dve", tpad)
                    ck.dve.tensor_copy(
                        xp[:, n, 3 + tc * 512:3 + (tc + 1) * 512], bk)
                    bufs.big_free = ck.tok("dve")
                else:
                    ck.dve.tensor_copy(bufs.scb1[:, 0:512], bk)
                    td = ck.tok("dve")
                    bufs.big_free = td
                    ck.dma("qs", ck.scr_z[(n - CI) * 128:(n - CI + 1) * 128,
                                          tsl],
                           bufs.scb1[:, 0:512], after=[td])
            cons[n] = ck.tok("pe")
            load_blk(n + 2)
        tz_done = ck.qtok("qs")

        for c in range(CI):
            for tc in range(NT - 1, -1, -1):
                t0 = tc * 512
                ck.dve.tensor_scalar(
                    bufs.sc2[:, 0:512], xp[:, c, t0:t0 + 512],
                    cols["l0_cw"][:, c * 4:c * 4 + 1], None, ALU.mult)
                for j in (1, 2, 3):
                    ck.dve.scalar_tensor_tensor(
                        bufs.sc2[:, 0:512], xp[:, c, t0 + j:t0 + j + 512],
                        cols["l0_cw"][:, c * 4 + j:c * 4 + j + 1],
                        bufs.sc2[:, 0:512], ALU.mult, ALU.add)
                td = ck.tok("dve")
                ck.wait("act", td)
                ck.act.activation(xp[:, c, 3 + t0:3 + t0 + 512],
                                  bufs.sc2[:, 0:512], AF.Silu,
                                  bias=cols["l0_cb"][:, c:c + 1])
                ck.wait("dve", ck.tok("act"))
        txc = ck.tok("act")
        xc = xp[:, :, 3:2051]

        if True:
            ck.wait("sp", t_xpage)
            ck.waits("pool", [t_xpage])
            wx = bufs.X[:, 6, 0:528].bitcast(BF16).rearrange(
                "p (a b) -> p a b", b=33)[:, 0:CI, :]
            ssm = bufs.X[0:33, 0, :]
            dtc = bufs.X[:, 1, :].bitcast(BF16)[:, 0:T]
            uc = bufs.X[:, 5, :].bitcast(BF16)[:, 0:T]
            negA = bufs.X[:, 2, 0:256]
            sredt = bufs.X[:, 3, :].bitcast(BF16)[:, 0:2048].rearrange(
                "p (a b) -> p a b", b=128)
            zt = bufs.X[:, 4, :].bitcast(BF16)[:, 0:T]
            twx = ck.dma("qp", wx[:, :, :],
                         W["wx"].rearrange("(kc kp) n -> kp kc n", kp=128))
            ck.waits("pe", [twx, txc])
            for tc in range(NT):
                tsl = slice(tc * 512, (tc + 1) * 512)
                ck.wait("pe", bufs.mid_free)
                for kc in range(CI):
                    ck.pe.matmul(bufs.ps_mid[0:33, 0:512], wx[:, kc, :],
                                 xc[:, kc, tsl],
                                 start=(kc == 0), stop=(kc == CI - 1))
                tp = ck.tok("pe")
                ck.wait("dve", tp)
                ck.dve.tensor_copy(ssm[:, tsl], bufs.ps_mid[0:33, 0:512])
                bufs.mid_free = ck.tok("dve")
            tssm = ck.tok("dve")
            ck.wait("sp", tssm)
            bdr = bcast_rows(ck, ck.scr_rows, [ssm[0:1, :]],
                             [bufs.sc1[:, :]])
            tupB = ck.dma("qs", ck.scr_bc[0:16, :], ssm[1:17, :],
                          after=[tssm])
            tupC = ck.dma("qs", ck.scr_bc[16:32, :], ssm[17:33, :],
                          after=[tssm])
            tB = ck.dma("qp", bufs.xn[:, 6, :],
                        ck.scr_bc[0:16, :].unsqueeze(1)
                        .broadcast_to([16, 8, T]), after=[tupB])
            tC = ck.dma("qp", bufs.xn[:, 7, :],
                        ck.scr_bc[16:32, :].unsqueeze(1)
                        .broadcast_to([16, 8, T]), after=[tupC])
            tA = ck.dma("qs", negA[:, :], W["ascan"])
            ck.wait("act", tA)
            ck.act.activation(negA[:, :], negA[:, :], AF.Exp)
            tae = ck.tok("act")
            ck.wait("dve", tae)
            ck.dve.tensor_scalar_mul(negA[:, :], negA[:, :], -1.0)
            tnegA = ck.tok("dve")

            for c in range(CI):
                csl = slice(c * 128, (c + 1) * 128)
                ck.waits("dve", bdr + [tssm])
                ck.dve.tensor_scalar(bufs.sc2[:, :], bufs.sc1[:, :],
                                     cols["l0_wdt"][:, c:c + 1],
                                     cols["l0_bdt"][:, c:c + 1],
                                     ALU.mult, ALU.add)
                td = ck.tok("dve")
                ck.wait("act", td)
                ck.act.activation(bufs.sc3[:, :], bufs.sc2[:, :], AF.Abs)
                ck.act.activation(bufs.sc3[:, :], bufs.sc3[:, :], AF.Exp,
                                  scale=-1.0)
                ck.act.activation(bufs.sc3[:, :], bufs.sc3[:, :], AF.Ln,
                                  bias=1.0)
                ta = ck.tok("act")
                ck.wait("dve", ta)
                ck.wait("dve", getattr(bufs, "l0_du_tok", None))
                ck.dve.scalar_tensor_tensor(dtc[:, :], bufs.sc2[:, :], 0.0,
                                            bufs.sc3[:, :], ALU.max, ALU.add)
                ck.wait("dve", txc)
                ck.dve.tensor_mul(uc[:, :], dtc[:, :], xc[:, c, :])
                tdu = ck.tok("dve")
                ck.dma("qs", ck.scr_dt[csl, :], dtc[:, :], after=[tdu])
                bufs.l0_du_tok = ck.dma("qs", ck.scr_u[csl, :], uc[:, :],
                                        after=[tdu])
            t_sc = ck.qtok("qs")

            ck.join()
            tsred = ck.dma("qs", sredt[:, :, :], W["sred"])
            rep_tok = {}

            def load_tile(kidx):
                if kidx >= 256 or kidx in rep_tok:
                    return
                par = kidx % 2
                ch0 = (kidx // 16) * 128 + (kidx % 16) * 8
                after = [rep_tok[kidx - 2][2]] if (kidx - 2) in rep_tok \
                    else []
                t1 = ck.dma("qs", bufs.xn[:, 0 + par, :],
                            ck.scr_dt[ch0:ch0 + 8, :].unsqueeze(0)
                            .broadcast_to([16, 8, T]), after=after)
                t2 = ck.dma("qs", bufs.xn[:, 2 + par, :],
                            ck.scr_u[ch0:ch0 + 8, :].unsqueeze(0)
                            .broadcast_to([16, 8, T]), after=after)
                rep_tok[kidx] = [t1, t2, None]

            load_tile(0)
            load_tile(1)
            y_evac = None
            for kc16 in range(CI):
                for j in range(16):
                    kidx = kc16 * 16 + j
                    par = kidx % 2
                    t1, t2, _ = rep_tok[kidx]
                    ck.waits("act", [t1, tnegA])
                    ck.act.activation(bufs.xn[:, par, :], bufs.xn[:, par, :],
                                      AF.Exp, scale=negA[:, kidx:kidx + 1])
                    ta = ck.tok("act")
                    ck.waits("dve", [t2, tB])
                    ck.dve.tensor_mul(bufs.xn[:, 2 + par, :],
                                      bufs.xn[:, 2 + par, :],
                                      bufs.xn[:, 6, :])
                    ck.wait("dve", ta)
                    ck.dve.tensor_tensor_scan(
                        bufs.xn[:, 4 + par, :], bufs.xn[:, par, :],
                        bufs.xn[:, 2 + par, :], 0.0, ALU.mult, ALU.add)
                    ck.wait("dve", tC)
                    ck.dve.tensor_mul(bufs.xn[:, 4 + par, :],
                                      bufs.xn[:, 4 + par, :],
                                      bufs.xn[:, 7, :])
                    ttmp = ck.tok("dve")
                    rep_tok[kidx][2] = ttmp
                    load_tile(kidx + 2)
                    ck.waits("pe", [ttmp, tsred])
                    if j == 0:
                        ck.wait("pe", y_evac)
                    for tc in range(NT):
                        ck.pe.matmul(
                            bufs.ps_big[:, tc * 512:(tc + 1) * 512],
                            sredt[:, j, :],
                            bufs.xn[:, 4 + par, tc * 512:(tc + 1) * 512],
                            start=(j == 0), stop=(j == 15))
                tp = ck.tok("pe")
                tzl = ck.dma("qs", zt[:, :],
                             ck.scr_z[kc16 * 128:(kc16 + 1) * 128, :],
                             after=[y_evac] if y_evac else [])
                ck.waits("act", [tzl, tz_done])
                ck.act.activation(bufs.scb1[:, :], zt[:, :], AF.Silu)
                tsz = ck.tok("act")
                ck.waits("dve", [tp, tsz])
                ck.dve.scalar_tensor_tensor(
                    bufs.sc2[:, :], xc[:, kc16, :],
                    cols["l0_D"][:, kc16:kc16 + 1], bufs.ps_big[:, :],
                    ALU.mult, ALU.add)
                ck.dve.tensor_mul(xc[:, kc16, :], bufs.sc2[:, :],
                                  bufs.scb1[:, :])
                y_evac = ck.tok("dve")

            ck.join()
            t_xr = ck.dma("qs", bufs.X[:, :, :], ck.scr_x)
            ck.wait("dve", t_xr)
            ck.wait("pe", t_xr)
            wouts = [winb, winb2]
            tws = {}
            wcons = {}

            def load_wout(ocn):
                if ocn >= KC or ocn in tws:
                    return
                after = [wcons[ocn - 2]] if (ocn - 2) in wcons else \
                    ([cons[31]] if ocn < 2 else [])
                tws[ocn] = ck.dma("qp", wouts[ocn % 2][:, :, :],
                                  W["wout0"][:, ocn * 128:(ocn + 1) * 128]
                                  .rearrange("(kc kp) n -> kp kc n", kp=128),
                                  after=after)

            load_wout(0)
            load_wout(1)
            for oc in range(KC):
                b = oc % 2
                ck.waits("pe", [tws[oc], y_evac])
                for tc in range(NT):
                    tsl = slice(tc * 512, (tc + 1) * 512)
                    bk = bufs.ps_big[:, (tc % 2) * 512:(tc % 2) * 512 + 512]
                    ck.wait("pe", bufs.big_free)
                    for kc in range(CI):
                        ck.pe.matmul(bk, wouts[b][:, kc, :], xc[:, kc, tsl],
                                     start=(kc == 0), stop=(kc == CI - 1))
                    tp = ck.tok("pe")
                    ck.wait("dve", tp)
                    ck.dve.tensor_add(bufs.X[:, oc, tsl],
                                      bufs.X[:, oc, tsl], bk)
                    bufs.big_free = ck.tok("dve")
                wcons[oc] = ck.tok("pe")
                load_wout(oc + 2)


# version 03
# speedup vs baseline: 1.0060x; 1.0060x over previous
import numpy as np
import ml_dtypes
import concourse.bass as bass
import concourse.mybir as mybir
from concourse.bass_utils import run_bass_kernel_spmd

F32 = mybir.dt.float32
BF16 = mybir.dt.bfloat16
AF = mybir.ActivationFunctionType
ALU = mybir.AluOpType
AX = mybir.AxisListType

T = 2048
D = 1024
KC = D // 128
NT = T // 512
RMAGIC = 12582912.0
KONST_VALUES = (1e-5, 1e-6, 1e-10, 1.0, 0.5)



_DVE_TENSOR_OPS = (
    "tensor_", "scalar_tensor_tensor", "reduce_", "reciprocal", "memset",
    "memzero", "select", "iota", "pool", "bn_")
_ACT_TENSOR_OPS = ("activation", "square", "sqrt", "mul", "copy", "add",
                   "sign", "memzero")


class _EngW:

    def __init__(self, ck, name, eng):
        object.__setattr__(self, "_ck", ck)
        object.__setattr__(self, "_name", name)
        object.__setattr__(self, "_eng", eng)

    def __getattr__(self, attr):
        fn = getattr(self._eng, attr)
        guard = ((self._name in ("dve", "pool")
                  and any(attr.startswith(p) for p in _DVE_TENSOR_OPS))
                 or (self._name == "act"
                     and any(attr.startswith(p) for p in _ACT_TENSOR_OPS)))

        def wrap(*a, **k):
            inst = fn(*a, **k)
            if inst is not None and hasattr(inst, "then_inc"):
                if guard and a and hasattr(a[0], "free_size"):
                    try:
                        small = a[0].free_size() <= 16
                    except Exception:
                        small = False
                    if small:
                        inst = self._eng.drain()
                self._ck._last[self._name] = inst
                self._ck._last_tok[self._name] = None
            return inst

        return wrap


class Ck:

    DMAQ = {"qs": "sp", "qa": "act", "qp": "pool"}

    def __init__(self, nc):
        self.nc = nc
        self.raw = dict(pe=nc.tensor, dve=nc.vector, act=nc.scalar,
                        pool=nc.gpsimd, sp=nc.sync)
        self.pe = _EngW(self, "pe", nc.tensor)
        self.dve = _EngW(self, "dve", nc.vector)
        self.act = _EngW(self, "act", nc.scalar)
        self.pool = _EngW(self, "pool", nc.gpsimd)
        self.sp = _EngW(self, "sp", nc.sync)
        self.sem, self.cnt, self._last, self._last_tok = {}, {}, {}, {}
        self.waited = {}
        for name in ("pe", "dve", "act", "pool", "sp", "qs", "qa", "qp"):
            self.sem[name] = nc.alloc_semaphore(name=f"ck_{name}")
            self.cnt[name] = 0
            self._last[name] = None
            self._last_tok[name] = None

    def tok(self, eng):
        if self._last[eng] is None:
            return (eng, self.cnt[eng])
        if self._last_tok[eng] is None:
            self.cnt[eng] += 1
            self._last[eng].then_inc(self.sem[eng], 1)
            self._last_tok[eng] = (eng, self.cnt[eng])
        return self._last_tok[eng]

    def wait(self, eng, tok):
        if tok is None:
            return
        src, val = tok
        if src == eng or val <= 0:
            return
        key = (eng, src)
        if self.waited.get(key, 0) >= val:
            return
        self.raw[eng].wait_ge(self.sem[src], val)
        self.waited[key] = val

    def waits(self, eng, toks):
        for t in toks:
            self.wait(eng, t)

    def dma(self, q, out, in_, after=(), **kw):
        iss = self.DMAQ[q]
        for t in after:
            self.wait(iss, t)
        inst = self.raw[iss].dma_start(out=out, in_=in_, **kw)
        self.cnt[q] += 16
        inst.then_inc(self.sem[q], 16)
        return (q, self.cnt[q])

    def vreduce(self, out, in_, axis, op, **kw):
        self.raw["dve"].tensor_reduce(out, in_, axis, op, **kw)
        inst = self.raw["dve"].drain()
        self._last["dve"] = inst
        self._last_tok["dve"] = None
        return inst

    def recip(self, out, in_):
        self.raw["dve"].drain()
        self.raw["dve"].reciprocal(out, in_)
        inst = self.raw["dve"].drain()
        self._last["dve"] = inst
        self._last_tok["dve"] = None
        return inst

    def qtok(self, q):
        return (q, self.cnt[q])

    def join(self):
        engines = ("pe", "dve", "act", "pool", "sp")
        toks = [self.tok(e) for e in engines]
        toks += [self.qtok("qs"), self.qtok("qa"), self.qtok("qp")]
        for e in engines:
            for t in toks:
                self.wait(e, t)

    def finish(self, out_toks):
        self.join()
        for t in out_toks:
            self.wait("sp", t)
            self.wait("pool", t)
        for name in self.sem:
            self.raw["pool"].sem_clear(self.sem[name])




class WStream:

    def __init__(self, ck, name, ap, K, N, nb=128, depth=2):
        nc = ck.nc
        self.ck, self.ap, self.K, self.N, self.nb = ck, ap, K, N, nb
        assert K % 128 == 0 and N % nb == 0
        self.kc = K // 128
        self.nblk = N // nb
        self.depth = min(depth, self.nblk)
        self._ctx, self._bufs = [], []
        for i in range(self.depth):
            g = nc.sbuf_tensor(f"{name}_b{i}", [128, self.kc, nb], BF16)
            self._bufs.append(g.__enter__())
            self._ctx.append(g)
        self._load_tok = {}
        self._free_tok = [None] * self.depth
        self._epoch = 0

    def close(self):
        for g in reversed(self._ctx):
            g.__exit__(None, None, None)

    def reset(self):
        self._epoch += 1

    def _issue(self, j):
        if j >= self.nblk or (self._epoch, j) in self._load_tok:
            return
        b = j % self.depth
        src = self.ap[:, j * self.nb:(j + 1) * self.nb].rearrange(
            "(kc kp) n -> kp kc n", kp=128)
        after = [self._free_tok[b]] if self._free_tok[b] else []
        self._load_tok[(self._epoch, j)] = self.ck.dma(
            "qp", self._bufs[b][:, :, :], src, after=after)

    def start(self):
        for j in range(self.depth):
            self._issue(j)

    def get(self, j, eng="pe"):
        self._issue(j)
        self.ck.wait(eng, self._load_tok[(self._epoch, j)])
        return self._bufs[j % self.depth]

    def done(self, j, eng="pe"):
        self._free_tok[j % self.depth] = self.ck.tok(eng)
        self._issue(j + self.depth)




def bcast_rows(ck, scr_rows, row_aps, out_tiles, width=T):
    toks = []
    ups = []
    for i, r in enumerate(row_aps):
        ups.append(ck.dma("qs", scr_rows[i:i + 1, 0:width], r))
    for i, o in enumerate(out_tiles):
        src = scr_rows[i:i + 1, 0:width].partition_broadcast(128)
        toks.append(ck.dma("qs", o.unsqueeze(1), src, after=ups))
    return toks


class LNParams:

    def __init__(self, nc, ck, bufs):
        self.ck = ck
        self._ctx = []

        def sb(name, shape, dt=F32):
            g = nc.sbuf_tensor(name, shape, dt)
            self._ctx.append(g)
            return g.__enter__()

        self.bufs = bufs
        self.ones_col = bufs.ones_bf
        self.sq = bufs.scb1
        self.b1 = bufs.sc1
        self.b2 = bufs.sc2
        self.tmp = bufs.sc3
        self.ps_s = bufs.ps_row[0:1, 0:512]
        self.ps_q = bufs.ps_row[0:1, 512:1024]
        self.init_tok = bufs.ones_tok


def emit_ln(ck, ln, src_bf, outs, eps, gamma_col=None, beta_col=None,
            extra_bf=None, n_chunks=KC, src_tok=None):
    inv_n = 1.0 / (128 * n_chunks)
    last_read = None
    for tc in range(NT):
        sl = slice(tc * 512, (tc + 1) * 512)
        for c in range(n_chunks):
            ck.wait("act", ln.init_tok)
            if src_tok is not None:
                ck.wait("act", src_tok)
                ck.wait("pe", src_tok)
            tprev = ck.tok("pe") if c > 0 else last_read
            if tprev is not None:
                ck.wait("act", tprev)
            ck.act.activation(ln.sq[:, 0:512], src_bf[c][:, sl], AF.Square)
            tq = ck.tok("act")
            ck.wait("pe", ln.init_tok)
            if last_read is not None:
                ck.wait("pe", last_read)
            ck.pe.matmul(ln.ps_s, ln.ones_col[:, :], src_bf[c][:, sl],
                         start=(c == 0), stop=(c == n_chunks - 1))
            ck.wait("pe", tq)
            ck.pe.matmul(ln.ps_q, ln.ones_col[:, :], ln.sq[:, 0:512],
                         start=(c == 0), stop=(c == n_chunks - 1))
        tp = ck.tok("pe")
        ck.wait("dve", tp)
        rA, rB = ck.nc, None
        bufs = ln.bufs
        ck.dve.tensor_scalar_mul(bufs.rowA[0:1, sl], ln.ps_s, inv_n)
        ck.dve.tensor_scalar_mul(bufs.rowB[0:1, sl], ln.ps_q, inv_n)
        last_read = ck.tok("dve")
        ck.wait("act", last_read)
        ck.act.activation(bufs.sc3[0:1, sl], bufs.rowA[0:1, sl], AF.Square)
        ta = ck.tok("act")
        ck.wait("dve", ta)
        ck.dve.tensor_sub(bufs.rowB[0:1, sl], bufs.rowB[0:1, sl],
                          bufs.sc3[0:1, sl])
        td = ck.tok("dve")
        ck.wait("act", td)
        ck.act.activation(bufs.rowB[0:1, sl], bufs.rowB[0:1, sl], AF.Ln,
                          bias=float(eps))
        ck.act.activation(bufs.rowB[0:1, sl], bufs.rowB[0:1, sl], AF.Exp,
                          scale=-0.5)
        ta = ck.tok("act")
        ck.wait("dve", ta)
        ck.dve.tensor_mul(bufs.rowA[0:1, sl], bufs.rowA[0:1, sl],
                          bufs.rowB[0:1, sl])
        ck.dve.tensor_scalar_mul(bufs.rowA[0:1, sl], bufs.rowA[0:1, sl],
                                 -1.0)
    trows = ck.tok("dve")
    ck.wait("sp", trows)
    bufs = ln.bufs
    bts = bcast_rows(ck, ck.scr_rows, [bufs.rowB[0:1, :], bufs.rowA[0:1, :]],
                     [ln.b1[:, :], ln.b2[:, :]])
    out_toks = []
    for c in range(n_chunks):
        ck.waits("dve", bts)
        if gamma_col is None:
            ck.dve.tensor_mul(ln.tmp[:, :], src_bf[c][:, :], ln.b1[:, :])
            ck.dve.tensor_add(outs[c], ln.tmp[:, :], ln.b2[:, :])
        else:
            ck.dve.tensor_mul(ln.tmp[:, :], src_bf[c][:, :], ln.b1[:, :])
            ck.dve.tensor_add(ln.tmp[:, :], ln.tmp[:, :], ln.b2[:, :])
            ck.dve.tensor_scalar(outs[c], ln.tmp[:, :], gamma_col[c],
                                 beta_col[c], ALU.mult, ALU.add)
        if extra_bf is not None:
            ck.dve.tensor_copy(extra_bf[c], outs[c])
    return ck.tok("dve")




class Bufs:

    def __init__(self, nc, ck):
        self.nc, self.ck = nc, ck
        self._ctx = []

        def sb(name, shape, dt=F32):
            g = nc.sbuf_tensor(name, shape, dt)
            self._ctx.append(g)
            return g.__enter__()

        def psum(name, shape, dt=F32):
            g = nc.psum_tensor(name, shape, dt)
            self._ctx.append(g)
            return g.__enter__()

        self.X = sb("Xres", [128, KC, T])
        self.xn = sb("xn_bf", [128, KC, T], BF16)
        self.sc1 = sb("sc1", [128, T])
        self.sc2 = sb("sc2", [128, T])
        self.sc3 = sb("sc3", [128, T])
        self.scb1 = sb("scb1", [128, T], BF16)
        self.scb2 = sb("scb2", [128, T], BF16)
        self.colA = sb("colA", [128, 64])
        self.colB = sb("colB", [128, 64])
        self.rowA = self.sc1
        self.rowB = self.sc2
        self.id_bf = sb("id_bf", [128, 128], BF16)
        self.id_f32 = sb("id_f32", [128, 128])
        self.ps_big = psum("ps_big", [128, 2048])
        self.ps_mid = psum("ps_mid", [128, 512])
        self.ps_row = psum("ps_row", [1, 1024])
        self.ps_trb = psum("ps_trb", [128, 1024], BF16)
        self.colA2 = sb("colA2", [16, 128])
        self.ones_bf = sb("ones_bf", [128, 1], BF16)
        self.ones_f32 = sb("ones_f32", [128, 1])
        ck.pool.memset(self.ones_bf[:, :], 1.0)
        ck.pool.memset(self.ones_f32[:, :], 1.0)
        self.konst = sb("konst", [128, 8])
        for i, v in enumerate(KONST_VALUES):
            ck.pool.memset(self.konst[:, i:i + 1], float(v))
            nc.const_aps.register(F32, float(v), self.konst[:, i:i + 1])
        self.ones_tok = ck.tok("pool")
        self.id_tok = None
        self.trb_free = None
        self.mid_free = None
        self.big_free = None
        self.wo_free = None
        self.we_free = [None, None]



def col_loader(ck, nc, name, dram_ap, ncols):
    g = nc.sbuf_tensor(name, [128, ncols], F32)
    h = g.__enter__()
    tok = ck.dma("qs", h[:, :], dram_ap)
    return h, tok, g


class ScrStream:

    def __init__(self, ck, name, scr_ap, kc, N, nb=128, depth=2):
        nc = ck.nc
        self.ck, self.scr, self.kc, self.N, self.nb = ck, scr_ap, kc, N, nb
        self.nblk = N // nb
        self.depth = min(depth, self.nblk)
        self._ctx, self._bufs = [], []
        for i in range(self.depth):
            g = nc.sbuf_tensor(f"{name}_b{i}", [128, kc, nb], BF16)
            self._bufs.append(g.__enter__())
            self._ctx.append(g)
        self._load_tok = {}
        self._free_tok = [None] * self.depth
        self._epoch = 0

    def close(self):
        for g in reversed(self._ctx):
            g.__exit__(None, None, None)

    def reset(self):
        self._epoch += 1

    def _key(self, j):
        return (self._epoch, j)

    def _issue(self, j, ready_tok=None):
        if j >= self.nblk or self._key(j) in self._load_tok:
            return
        b = j % self.depth
        src = self.scr[:, :, j * self.nb:(j + 1) * self.nb]
        after = [t for t in (self._free_tok[b], ready_tok) if t]
        self._load_tok[self._key(j)] = self.ck.dma(
            "qs", self._bufs[b][:, :, :], src, after=after)

    def start(self, ready_tok=None):
        for j in range(self.depth):
            self._issue(j, ready_tok)

    def get(self, j, eng="pe"):
        self._issue(j)
        self.ck.wait(eng, self._load_tok[self._key(j)])
        return self._bufs[j % self.depth]

    def done(self, j, eng="pe"):
        self._free_tok[j % self.depth] = self.ck.tok(eng)
        self._issue(j + self.depth)


def quantize_bitlinear_weight(ck, bufs, w_ap, K, N, scr_ap, name):
    nc = ck.nc
    kc = K // 128
    nb = 2048 // kc
    nblk = N // nb
    ctx, fbufs, qbufs = [], [], []
    for i in range(2):
        g = nc.sbuf_tensor(f"{name}_f{i}", [128, kc, nb], F32)
        fbufs.append(g.__enter__())
        ctx.append(g)
        g = nc.sbuf_tensor(f"{name}_q{i}", [128, kc, nb], BF16)
        qbufs.append(g.__enter__())
        ctx.append(g)
    load_tok = [None] * nblk
    free_tok = [None, None]

    def issue(j, extra=()):
        if j >= nblk or load_tok[j] is not None:
            return
        b = j % 2
        src = w_ap[:, j * nb:(j + 1) * nb].rearrange(
            "(kc kp) n -> kp kc n", kp=128)
        aft = [t for t in (free_tok[b], *extra) if t]
        load_tok[j] = ck.dma("qs", fbufs[b][:, :, :], src, after=aft)

    tpe0 = ck.tok("pe")
    td0 = ck.tok("dve")
    ck.wait("pool", bufs.ones_tok)
    ck.wait("pool", tpe0)
    ck.wait("pool", td0)
    ck.wait("act", td0)
    ck.pool.memset(bufs.colA[:, 0:1], 0.0)
    tz = ck.tok("pool")
    issue(0)
    issue(1)
    for j in range(nblk):
        b = j % 2
        ck.wait("act", load_tok[j])
        ck.act.activation(fbufs[b][:, :, :], fbufs[b][:, :, :], AF.Abs,
                          accum_out=bufs.colA[:, 1:2])
        ta = ck.tok("act")
        ck.waits("dve", [ta, tz])
        ck.dve.tensor_add(bufs.colA[:, 0:1], bufs.colA[:, 0:1],
                          bufs.colA[:, 1:2])
        free_tok[b] = ck.tok("dve")
        load_tok[j] = None
        issue(j + 2)
    td = ck.tok("dve")
    ck.wait("pe", td)
    ck.pe.matmul(bufs.ps_row[0:1, 0:1], bufs.ones_f32[:, :],
                 bufs.colA[:, 0:1], start=True, stop=True)
    tp = ck.tok("pe")
    ck.wait("dve", tp)
    ck.wait("dve", ck.qtok("qs"))
    ck.dve.tensor_scalar(bufs.rowA[0:1, 0:1], bufs.ps_row[0:1, 0:1],
                         1.0 / (K * N), 1e-5, ALU.mult, ALU.max)
    ck.recip(bufs.rowA[0:1, 1:2], bufs.rowA[0:1, 0:1])
    trow = ck.tok("dve")
    t_up = ck.dma("qs", ck.scr_rows[0:1, 0:2], bufs.rowA[0:1, 0:2],
                  after=[trow])
    t_mn = ck.dma("qs", bufs.colA[:, 4:5].unsqueeze(1),
                  ck.scr_rows[0:1, 0:1].partition_broadcast(128),
                  after=[t_up])
    t_s = ck.dma("qs", bufs.colA[:, 5:6].unsqueeze(1),
                 ck.scr_rows[0:1, 1:2].partition_broadcast(128),
                 after=[t_up])
    out_toks = []
    for j in range(nblk):
        b = j % 2
        issue(j)
        ck.wait("dve", load_tok[j])
        ck.waits("dve", [t_mn, t_s])
        f = fbufs[b][:, :, :]
        q = qbufs[b][:, :, :]
        if j >= 2:
            ck.wait("dve", out_toks[j - 2])
        ck.dve.tensor_scalar(f, f, bufs.colA[:, 5:6], RMAGIC,
                             ALU.mult, ALU.add)
        ck.dve.tensor_scalar(f, f, RMAGIC, -1.0, ALU.subtract, ALU.max)
        ck.dve.tensor_scalar(q, f, 1.0, bufs.colA[:, 4:5],
                             ALU.min, ALU.mult)
        tq = ck.tok("dve")
        out_toks.append(ck.dma("qs", scr_ap[:, :, j * nb:(j + 1) * nb], q,
                               after=[tq]))
        free_tok[b] = out_toks[-1]
        load_tok[j] = None
        issue(j + 1, extra=(tq,))
    ck.waits("sp", out_toks)
    for g in reversed(ctx):
        g.__exit__(None, None, None)
    return out_toks





def act_quant(ck, bufs, src, out_q, s_b, sinv_b, src_tok=None):
    nch = len(src)
    w = src[0].shape[1]
    ntt = w // 128
    mx = bufs.colB
    for tt in range(ntt):
        tsl = slice(tt * 128, (tt + 1) * 128)
        for c in range(nch):
            ck.wait("pe", bufs.id_tok)
            if src_tok is not None:
                ck.wait("pe", src_tok)
            if bufs.trb_free is not None:
                ck.wait("pe", bufs.trb_free)
            pst = bufs.ps_trb[:, (c % 4) * 128:(c % 4) * 128 + 128]
            ck.pe.transpose(pst, src[c][:, tsl], bufs.id_bf[:, :])
            tp = ck.tok("pe")
            ck.wait("dve", tp)
            sc = 24 + (c % 2)
            ck.vreduce(mx[:, sc:sc + 1], pst, AX.X, ALU.max,
                       apply_absolute_value=True)
            if c == 0:
                ck.dve.tensor_copy(mx[:, tt:tt + 1], mx[:, sc:sc + 1])
            else:
                ck.dve.tensor_max(mx[:, tt:tt + 1], mx[:, tt:tt + 1],
                                  mx[:, sc:sc + 1])
            bufs.trb_free = ck.tok("dve")
    ck.dve.tensor_scalar_max(mx[:, 0:ntt], mx[:, 0:ntt], 1e-5)
    ck.dve.tensor_scalar_mul(mx[:, 8 + ntt:8 + 2 * ntt], mx[:, 0:ntt],
                             1.0 / 127.0)
    ck.recip(mx[:, 8:8 + ntt], mx[:, 0:ntt])
    ck.dve.tensor_scalar_mul(mx[:, 8:8 + ntt], mx[:, 8:8 + ntt], 127.0)
    tcols = ck.tok("dve")
    ck.waits("pe", [tcols, bufs.id_tok, bufs.mid_free])
    ck.pe.transpose(bufs.ps_mid[0:2 * ntt, 0:128], mx[:, 8:8 + 2 * ntt],
                    bufs.id_f32[:, :])
    tp = ck.tok("pe")
    ck.wait("dve", tp)
    ck.dve.tensor_copy(bufs.colA2[0:2 * ntt, :],
                       bufs.ps_mid[0:2 * ntt, 0:128])
    trows = ck.tok("dve")
    bufs.mid_free = trows
    if getattr(bufs, "dbg_aq", None) is not None:
        d1, d2 = bufs.dbg_aq
        ck.dma("qs", d1, bufs.colB[:, 0:32], after=[trows])
        ck.dma("qs", d2, bufs.colA2[:, :], after=[trows])
        bufs.dbg_aq = None
    dns = []
    for r in range(ntt):
        u1 = ck.dma("qs", ck.scr_rows[4 + r:5 + r, 0:128],
                    bufs.colA2[r:r + 1, :], after=[trows])
        u2 = ck.dma("qs", ck.scr_rows[12 + r:13 + r, 0:128],
                    bufs.colA2[ntt + r:ntt + r + 1, :], after=[trows])
        dns.append(ck.dma("qs", s_b[:, r * 128:(r + 1) * 128].unsqueeze(1),
                          ck.scr_rows[4 + r:5 + r, 0:128]
                          .partition_broadcast(128), after=[u1]))
        dns.append(ck.dma("qs", sinv_b[:, r * 128:(r + 1) * 128].unsqueeze(1),
                          ck.scr_rows[12 + r:13 + r, 0:128]
                          .partition_broadcast(128), after=[u2]))
    for c in range(nch):
        ck.waits("dve", dns)
        ck.dve.tensor_mul(bufs.sc3[:, 0:w], src[c], s_b[:, 0:w])
        ck.dve.tensor_scalar(bufs.sc3[:, 0:w], bufs.sc3[:, 0:w], RMAGIC,
                             RMAGIC, ALU.add, ALU.subtract)
        ck.dve.tensor_mul(out_q[c], bufs.sc3[:, 0:w], sinv_b[:, 0:w])
    return ck.tok("dve")




def emit_l1(ck, bufs, ln, W, cols, phases="ABC"):
    nc = ck.nc
    scr_g, scr_u, scr_d = W["scr_g"], W["scr_u"], W["scr_d"]
    if "A" in phases:
        quantize_bitlinear_weight(ck, bufs, W["wg"], D, 4096, scr_g, "l1wg")
        quantize_bitlinear_weight(ck, bufs, W["wu"], D, 4096, scr_u, "l1wu")
        quantize_bitlinear_weight(ck, bufs, W["wd"], 4096, D, scr_d, "l1wd")

    import os as _os
    _dbgA = _os.environ.get("KDBG_L1A")
    if _dbgA:
        bufs.dbg_aq = (
            nc.dram_tensor("d_colB", [128, 32], F32,
                           kind="ExternalOutput").ap(),
            nc.dram_tensor("d_colA2", [16, 128], F32,
                           kind="ExternalOutput").ap())
        d_xq = nc.dram_tensor("d_xq", [128, KC, 512], BF16,
                              kind="ExternalOutput").ap()
        d_gu = nc.dram_tensor("d_gu", [128, 32, 512], BF16,
                              kind="ExternalOutput").ap()
        d_sb = nc.dram_tensor("d_sb", [128, 512], F32,
                              kind="ExternalOutput").ap()
        d_sib = nc.dram_tensor("d_sib", [128, 512], F32,
                               kind="ExternalOutput").ap()
    if "A" in phases:
        with (
          nc.sbuf_tensor("l1_xq", [128, KC, 512], BF16) as xq,
          nc.sbuf_tensor("l1_gu", [128, 32, 512], BF16) as gu,
          nc.sbuf_tensor("l1_sb", [128, 512], F32) as s_b,
          nc.sbuf_tensor("l1_sib", [128, 512], F32) as sinv_b,
      ):
          sg = ScrStream(ck, "l1sg", scr_g, KC, 4096, nb=128)
          su = ScrStream(ck, "l1su", scr_u, KC, 4096, nb=128)
          sd = ScrStream(ck, "l1sd", scr_d, 32, D, nb=128)
          for tc in range(NT):
              tsl = slice(tc * 512, (tc + 1) * 512)
              tq = act_quant(ck, bufs, [bufs.xn[:, c, tsl] for c in range(KC)],
                             [xq[:, c, :] for c in range(KC)], s_b[:, :],
                             sinv_b[:, :])
              sg.reset(); su.reset(); sd.reset()
              sg.start(); su.start()
              gu_toks = {}
              for n in range(32):
                  gt = sg.get(n)
                  ut = su.get(n)
                  bk_g = bufs.ps_big[:, (n % 2) * 512:(n % 2) * 512 + 512]
                  bk_u = bufs.ps_big[:, 1024 + (n % 2) * 512:1536 + (n % 2) * 512]
                  if n >= 2:
                      ck.wait("pe", gu_toks[n - 2])
                  ck.wait("pe", tq)
                  for kc in range(KC):
                      ck.pe.matmul(bk_g, gt[:, kc, :], xq[:, kc, :],
                                   start=(kc == 0), stop=(kc == KC - 1))
                  sg.done(n)
                  for kc in range(KC):
                      ck.pe.matmul(bk_u, ut[:, kc, :], xq[:, kc, :],
                                   start=(kc == 0), stop=(kc == KC - 1))
                  su.done(n)
                  tp = ck.tok("pe")
                  ck.wait("act", tp)
                  ck.act.activation(bufs.scb2[:, 0:512], bk_g, AF.Silu)
                  ta = ck.tok("act")
                  ck.waits("dve", [tp, ta])
                  ck.dve.tensor_mul(gu[:, n, :], bufs.scb2[:, 0:512], bk_u)
                  gu_toks[n] = ck.tok("dve")
              if _dbgA and tc == 0:
                  ck.waits("sp", [tq, gu_toks[31]])
                  ck.dma("qs", d_xq, xq[:, :, :])
                  ck.dma("qs", d_gu, gu[:, :, :])
                  ck.dma("qs", d_sb, s_b[:, :])
                  ck.dma("qs", d_sib, sinv_b[:, :])
                  ck.join()
              tq2 = act_quant(ck, bufs, [gu[:, n, :] for n in range(32)],
                              [gu[:, n, :] for n in range(32)], s_b[:, :],
                              sinv_b[:, :], src_tok=gu_toks[31])
              xtoks = {}
              for oc in range(KC):
                  dt_ = sd.get(oc)
                  bk = bufs.ps_big[:, (oc % 4) * 512:(oc % 4) * 512 + 512]
                  if oc >= 4:
                      ck.wait("pe", xtoks[oc - 4])
                  ck.wait("pe", tq2)
                  for kc in range(32):
                      ck.pe.matmul(bk, dt_[:, kc, :], gu[:, kc, :],
                                   start=(kc == 0), stop=(kc == 31))
                  sd.done(oc)
                  tp = ck.tok("pe")
                  ck.wait("dve", tp)
                  ck.dve.tensor_add(bufs.X[:, oc, tsl], bufs.X[:, oc, tsl], bk)
                  xtoks[oc] = ck.tok("dve")
              ck.wait("pe", xtoks[KC - 1])
          sd.close(); su.close(); sg.close()

    if "B" in phases:
      with (
          nc.sbuf_tensor("l1_x2", [128, KC, T], BF16) as x2,
          nc.sbuf_tensor("l1_wh00", [128, KC, 128], BF16) as wh00,
          nc.sbuf_tensor("l1_wh01", [128, KC, 128], BF16) as wh01,
          nc.sbuf_tensor("l1_wh02", [128, KC, 128], BF16) as wh02,
          nc.sbuf_tensor("l1_wh10", [128, KC, 128], BF16) as wh10,
          nc.sbuf_tensor("l1_wh11", [128, KC, 128], BF16) as wh11,
          nc.sbuf_tensor("l1_wh12", [128, KC, 128], BF16) as wh12,
          nc.sbuf_tensor("l1_kh", [128, 16, 128], BF16) as kh,
          nc.sbuf_tensor("l1_vh", [128, 16, 128], BF16) as vh,
          nc.sbuf_tensor("l1_Qh", [128, T], BF16) as Qh,
          nc.sbuf_tensor("l1_kvh", [128, 128], BF16) as kvh,
          nc.sbuf_tensor("l1_ksc", [128, 1], BF16) as ksc,
      ):
          whb = [[wh00, wh01, wh02], [wh10, wh11, wh12]]
          wh_load = {}
          wh_free = [None, None]

          def load_head(h):
              if h >= KC or h in wh_load:
                  return
              b = whb[h % 2]
              after = [wh_free[h % 2]] if wh_free[h % 2] else []
              toks = []
              for i, wn in enumerate(("wq", "wk", "wv")):
                  hs = slice(h * 128, (h + 1) * 128)
                  srcap = W[wn][:, hs].rearrange("(kc kp) n -> kp kc n", kp=128)
                  toks.append(ck.dma("qp", b[i][:, :, :], srcap, after=after))
              wh_load[h] = toks

          ck.join()
          load_head(0)
          load_head(1)
          tln = emit_ln(ck, ln, [bufs.xn[:, c, :] for c in range(KC)],
                        [x2[:, c, :] for c in range(KC)], 1e-5,
                        gamma_col=[cols["l1_g1"][:, c:c + 1] for c in range(KC)],
                        beta_col=[cols["l1_be1"][:, c:c + 1] for c in range(KC)])

          def elu1_evac(ps, out_bf, w):
              ck.dve.tensor_scalar_min(bufs.scb2[:, 0:w], ps, 0.0)
              td = ck.tok("dve")
              ck.wait("act", td)
              ck.act.activation(bufs.scb1[:, 0:w], bufs.scb2[:, 0:w], AF.Exp)
              ta = ck.tok("act")
              ck.wait("dve", ta)
              ck.dve.scalar_tensor_tensor(out_bf, ps, 0.0, bufs.scb1[:, 0:w],
                                          ALU.max, ALU.add)
              return ck.tok("dve")

          _dbgH = _os.environ.get("KDBG_L1H")
          if _dbgH:
              d_kv = nc.dram_tensor("d_kv", [2, 128, 128], BF16,
                                    kind="ExternalOutput").ap()
              d_ks = nc.dram_tensor("d_ks", [2, 128, 1], BF16,
                                    kind="ExternalOutput").ap()
              d_q = nc.dram_tensor("d_q", [2, 128, T], BF16,
                                   kind="ExternalOutput").ap()
              d_z = nc.dram_tensor("d_z", [2, 1, T], F32,
                                   kind="ExternalOutput").ap()
              d_kh = nc.dram_tensor("d_kh", [2, 128, 16, 128], BF16,
                                    kind="ExternalOutput").ap()
              d_kps = nc.dram_tensor("d_kps", [128, 128], F32,
                                     kind="ExternalOutput").ap()
              d_whk = nc.dram_tensor("d_whk", [128, KC, 128], BF16,
                                     kind="ExternalOutput").ap()
              d_x2 = nc.dram_tensor("d_x2", [128, KC, T], BF16,
                                    kind="ExternalOutput").ap()
          prev_zb = []
          for h in range(KC):
              wh = whb[h % 2]
              ck.waits("pe", wh_load[h])
              ck.wait("pe", tln)
              ck.wait("pe", bufs.mid_free)
              for tt in range(16):
                  ttsl = slice(tt * 128, (tt + 1) * 128)
                  bk = bufs.ps_mid
                  for kc in range(KC):
                      ck.pe.matmul(bk[:, 0:128], x2[:, kc, ttsl],
                                   wh[1][:, kc, :],
                                   start=(kc == 0), stop=(kc == KC - 1))
                  for kc in range(KC):
                      ck.pe.matmul(bk[:, 128:256], x2[:, kc, ttsl],
                                   wh[2][:, kc, :],
                                   start=(kc == 0), stop=(kc == KC - 1))
                  tp = ck.tok("pe")
                  ck.wait("dve", tp)
                  if _dbgH and h == 0 and tt == 0:
                      ck.dve.tensor_copy(bufs.sc2[:, 0:128], bk[:, 0:128])
                      _td = ck.tok("dve")
                      _t6 = ck.dma("qs", d_kps, bufs.sc2[:, 0:128], after=[_td])
                      _t7 = ck.dma("qs", d_x2, x2[:, :, :], after=[_td])
                      _t8 = ck.dma("qs", d_whk, wh[1][:, :, :], after=[_td])
                      ck.waits("dve", [_t6, _t7, _t8])
                  t1 = elu1_evac(bk[:, 0:128], kh[:, tt, :], 128)
                  ck.wait("act", tp)
                  ck.act.activation(vh[:, tt, :], bk[:, 128:256], AF.Copy,
                                    scale=1.0 / T)
                  tv = ck.tok("act")
                  ck.waits("pe", [t1, tv])
              ck.wait("pe", bufs.mid_free)
              for tt in range(16):
                  ck.pe.matmul(bufs.ps_mid[:, 0:128], kh[:, tt, :],
                               vh[:, tt, :], start=(tt == 0), stop=(tt == 15))
              ck.wait("pe", bufs.ones_tok)
              for tt in range(16):
                  ck.pe.matmul(bufs.ps_mid[:, 256:257], kh[:, tt, :],
                               bufs.ones_bf[:, :], start=(tt == 0),
                               stop=(tt == 15))
              tp = ck.tok("pe")
              ck.wait("dve", tp)
              ck.dve.tensor_copy(kvh[:, :], bufs.ps_mid[:, 0:128])
              ck.dve.tensor_copy(ksc[:, :], bufs.ps_mid[:, 256:257])
              tks = ck.tok("dve")
              bufs.mid_free = tks
              if _dbgH and h < 2:
                  _t1 = ck.dma("qs", d_kv[h], kvh[:, :], after=[tks])
                  _t2 = ck.dma("qs", d_ks[h], ksc[:, :], after=[tks])
                  _t3 = ck.dma("qs", d_kh[h], kh[:, :, :], after=[tks])
                  ck.waits("pe", [_t1, _t2, _t3])
                  ck.waits("dve", [_t1, _t2, _t3])
              for tc2 in range(NT):
                  tsl = slice(tc2 * 512, (tc2 + 1) * 512)
                  bk = bufs.ps_big[:, (tc2 % 2) * 512:(tc2 % 2) * 512 + 512]
                  for kc in range(KC):
                      ck.pe.matmul(bk, wh[0][:, kc, :], x2[:, kc, tsl],
                                   start=(kc == 0), stop=(kc == KC - 1))
                  tp = ck.tok("pe")
                  ck.wait("dve", tp)
                  tqe = elu1_evac(bk, Qh[:, tsl], 512)
                  ck.wait("pe", tqe)
              wh_free[h % 2] = ck.tok("pe")
              load_head(h + 1)
              ck.waits("dve", prev_zb)
              for tc2 in range(NT):
                  tsl = slice(tc2 * 512, (tc2 + 1) * 512)
                  ck.wait("pe", tks)
                  ck.pe.matmul(bufs.ps_row[0:1, 512:1024], ksc[:, :],
                               Qh[:, tsl], start=True, stop=True)
                  tp = ck.tok("pe")
                  ck.wait("dve", tp)
                  ck.dve.tensor_scalar_add(bufs.rowB[0:1, tsl],
                                           bufs.ps_row[0:1, 512:1024], 1e-6)
                  ck.wait("pe", ck.tok("dve"))
              ck.recip(bufs.rowB[0:1, :], bufs.rowB[0:1, :])
              ck.dve.tensor_scalar_mul(bufs.rowB[0:1, :], bufs.rowB[0:1, :],
                                       float(T))
              tz = ck.tok("dve")
              ck.wait("sp", tz)
              if _dbgH and h < 2:
                  _t4 = ck.dma("qs", d_q[h], Qh[:, :], after=[tz])
                  _t5 = ck.dma("qs", d_z[h], bufs.rowB[0:1, :], after=[tz])
                  ck.waits("pe", [_t4, _t5])
                  ck.waits("dve", [_t4, _t5])
              zb = bcast_rows(ck, ck.scr_rows, [bufs.rowB[0:1, :]],
                              [bufs.sc3[:, :]])
              prev_zb = zb
              for tc2 in range(NT):
                  tsl = slice(tc2 * 512, (tc2 + 1) * 512)
                  bk = bufs.ps_big[:, 1024 + (tc2 % 2) * 512:1536 + (tc2 % 2) * 512]
                  ck.pe.matmul(bk, kvh[:, :], Qh[:, tsl], start=True, stop=True)
                  tp = ck.tok("pe")
                  ck.waits("dve", zb + [tp])
                  ck.dve.tensor_mul(bufs.sc2[:, 0:512], bk, bufs.sc3[:, tsl])
                  ck.dve.tensor_add(bufs.xn[:, h, tsl], bufs.sc2[:, 0:512],
                                    bufs.xn[:, h, tsl])
                  ck.wait("pe", ck.tok("dve"))


    if _os.environ.get("KDBG_L1B"):
        d_xl1 = nc.dram_tensor("d_xl1", [128, KC, T], BF16,
                               kind="ExternalOutput").ap()
        ck.join()
        ck.dma("qs", d_xl1, bufs.xn[:, :, :])
        ck.join()

    if "C" in phases:
      TC3 = 256
      with (
          nc.sbuf_tensor("l1_x3", [128, KC, T], BF16) as x3,
          nc.sbuf_tensor("l1_h", [128, 32, TC3], BF16) as hbuf,
      ):
          ck.join()
          tln3 = emit_ln(ck, ln, [bufs.xn[:, c, :] for c in range(KC)],
                         [x3[:, c, :] for c in range(KC)], 1e-5,
                         gamma_col=[cols["l1_g2"][:, c:c + 1] for c in range(KC)],
                         beta_col=[cols["l1_be2"][:, c:c + 1] for c in range(KC)])
          ck.wait("pe", tln3)
          w1s = WStream(ck, "l1w1", W["w1"], D, 4096, nb=128)
          w2s = WStream(ck, "l1w2", W["w2"], 4096, D, nb=128)
          for tc in range(T // TC3):
              tsl = slice(tc * TC3, (tc + 1) * TC3)
              w1s.reset(); w2s.reset()
              w1s.start()
              htoks = {}
              for n in range(32):
                  wt = w1s.get(n)
                  bk = bufs.ps_big[:, (n % 2) * 512:(n % 2) * 512 + TC3]
                  if n >= 2:
                      ck.wait("pe", htoks[n - 2])
                  for kc in range(KC):
                      ck.pe.matmul(bk, wt[:, kc, :], x3[:, kc, tsl],
                                   start=(kc == 0), stop=(kc == KC - 1))
                  w1s.done(n)
                  tp = ck.tok("pe")
                  ck.wait("act", tp)
                  ck.act.activation(hbuf[:, n, :], bk, AF.Gelu,
                                    bias=cols["l1_b1"][:, n:n + 1])
                  htoks[n] = ck.tok("act")
              xtk = {}
              for oc in range(KC):
                  wt = w2s.get(oc)
                  bk = bufs.ps_big[:, 1024 + (oc % 2) * 512:1024 + (oc % 2) * 512 + TC3]
                  ck.wait("pe", htoks[31])
                  if oc >= 2:
                      ck.wait("pe", xtk[oc - 2])
                  for kc in range(32):
                      ck.pe.matmul(bk, wt[:, kc, :], hbuf[:, kc, :],
                                   start=(kc == 0), stop=(kc == 31))
                  w2s.done(oc)
                  tp = ck.tok("pe")
                  ck.wait("dve", tp)
                  ck.dve.scalar_tensor_tensor(
                      bufs.sc3[:, 0:TC3], bk, cols["l1_b2"][:, oc:oc + 1],
                      bufs.xn[:, oc, tsl], ALU.add, ALU.add)
                  ck.dve.tensor_add(bufs.X[:, oc, tsl], bufs.X[:, oc, tsl],
                                    bufs.sc3[:, 0:TC3])
                  xtk[oc] = ck.tok("dve")
              ck.wait("pe", xtk[KC - 1])
          w2s.close()
          w1s.close()




def emit_outer_ln(ck, bufs, ln, x_tok=None):
    for c in range(KC):
        if x_tok is not None:
            ck.wait("dve", x_tok)
        ck.dve.tensor_copy(bufs.xn[:, c, :], bufs.X[:, c, :])
    tcast = ck.tok("dve")
    chunks = [bufs.xn[:, c, :] for c in range(KC)]
    return emit_ln(ck, ln, chunks, chunks, 1e-5, src_tok=tcast)



LAYER_INPUTS = {
    0: dict(win=[D, 4096], wx=[2048, 33], wout0=[2048, D], wbase=[D, D],
            wd1=[8, D, 256], wd2=[8, 256, D], wadown=[D, 512],
            waup=[512, D], baup=[8, D], ascan=[128, 256]),
    1: dict(wq=[D, D], wk=[D, D], wv=[D, D], w1=[D, 4096], w2=[4096, D],
            wg=[D, 4096], wu=[D, 4096], wd=[4096, D]),
    2: dict(wq2=[16, D, 64], wk2=[16, D, 64], wv2=[16, D, 64],
            wo2=[D, D], rw=[D, 8], rb=[8], we2=[8, D, D]),
    3: dict(wqkv=[D, 3 * D], wout=[D, D], wfc=[4096, D], wproj=[D, 4096],
            bv=[1, D]),
}
LAYER_COLS = {
    0: dict(l0_nw=8, l0_cw=64, l0_cb=16, l0_wdt=16, l0_bdt=16, l0_D=16,
            l0_bb=8, l0_bd1=16, l0_bd2=64, l0_bad=4),
    1: dict(l1_g1=8, l1_be1=8, l1_g2=8, l1_be2=8, l1_b1=32, l1_b2=8),
    2: dict(),
    3: dict(l3_bq=8, l3_bk=8, l3_bo=8, l3_bfc=32, l3_bpj=8),
}
LAYER_SCR = {
    0: dict(),
    1: dict(scr_g=[128, KC, 4096], scr_u=[128, KC, 4096],
            scr_d=[128, 32, D]),
    2: dict(),
    3: dict(scr_fc=[128, KC, 4096], scr_pj=[128, 32, D]),
}


def build_program(layers, l1_phases="ABC"):
    nc = bass.Bass("TRN2", target_bir_lowering=False)
    ck = Ck(nc)
    x_in = nc.dram_tensor("x_in", [D, T], F32, kind="ExternalInput").ap()
    y_out = nc.dram_tensor("y_out", [D, T], F32, kind="ExternalOutput").ap()
    id_bf_d = nc.dram_tensor("c_id_bf", [128, 128], BF16,
                             kind="ExternalInput").ap()
    id_f_d = nc.dram_tensor("c_id_f32", [128, 128], F32,
                            kind="ExternalInput").ap()
    ck.scr_rows = nc.dram_tensor("scr_rows", [32, T], F32,
                                 kind="Internal").ap()
    ck.scr_grows = nc.dram_tensor("scr_grows", [8, T], F32,
                                  kind="Internal").ap()
    Ws, scrs = {}, {}
    for L in layers:
        for name, shape in LAYER_INPUTS.get(L, {}).items():
            Ws[name] = nc.dram_tensor(f"L{L}_{name}", shape, F32,
                                      kind="ExternalInput").ap()
        if L == 3:
            Ws["em"] = nc.dram_tensor("L3_em", [128, 2176], BF16,
                                      kind="ExternalInput").ap()
        if L == 0:
            Ws["sred"] = nc.dram_tensor("L0_sred", [128, 16, 128], BF16,
                                        kind="ExternalInput").ap()
            ck.scr_dt = nc.dram_tensor("scr_dt", [2048, T], BF16,
                                       kind="Internal").ap()
            ck.scr_u = nc.dram_tensor("scr_u", [2048, T], BF16,
                                      kind="Internal").ap()
            ck.scr_z = nc.dram_tensor("scr_z", [2048, T], BF16,
                                      kind="Internal").ap()
            ck.scr_bc = nc.dram_tensor("scr_bc", [32, T], F32,
                                       kind="Internal").ap()
            ck.scr_x = nc.dram_tensor("scr_x", [128, KC, T], F32,
                                      kind="Internal").ap()
        import os as _os
        _k = "ExternalOutput" if _os.environ.get("KDBG_SCR") else "Internal"
        for name, shape in LAYER_SCR.get(L, {}).items():
            Ws[name] = nc.dram_tensor(f"L{L}_{name}", shape, BF16,
                                      kind=_k).ap()
    col_handles = {}
    col_ctx = []
    bufs_ctx = None
    with_stack = []

    bufs = Bufs(nc, ck)
    ln = LNParams(nc, ck, bufs)
    t1 = ck.dma("qs", bufs.id_bf[:, :], id_bf_d)
    t2 = ck.dma("qs", bufs.id_f32[:, :], id_f_d)
    bufs.id_tok = t2
    ck.wait("pe", t1)
    for L in layers:
        for name, ncols in LAYER_COLS.get(L, {}).items():
            d = nc.dram_tensor(f"c_{name}", [128, ncols], F32,
                               kind="ExternalInput").ap()
            g = nc.sbuf_tensor(f"s_{name}", [128, ncols], F32)
            h = g.__enter__()
            col_ctx.append(g)
            tk = ck.dma("qs", h[:, :], d)
            ck.wait("dve", tk)
            ck.wait("act", tk)
            col_handles[name] = h
    tx = ck.dma("qs", bufs.X[:, :, :],
                x_in.rearrange("(kc kp) t -> kp kc t", kp=128))
    ck.wait("dve", tx)
    for L in layers:
        emit_outer_ln(ck, bufs, ln)
        ck.join()
        if L == 0:
            emit_l0(ck, bufs, ln, Ws, col_handles,
                    phases=l1_phases if l1_phases != "ABC" else "FM")
        elif L == 1:
            emit_l1(ck, bufs, ln, Ws, col_handles, phases=l1_phases)
        elif L == 2:
            emit_l2(ck, bufs, ln, Ws, col_handles, phases=l1_phases
                    if l1_phases != "ABC" else "RAM")
        elif L == 3:
            emit_l3(ck, bufs, ln, Ws, col_handles,
                    phases=l1_phases if l1_phases != "ABC" else "AF")
        else:
            raise NotImplementedError(L)
        ck.join()
    tX = ck.tok("dve")
    tout = ck.dma("qs", y_out.rearrange("(kc kp) t -> kp kc t", kp=128),
                  bufs.X[:, :, :], after=[tX])
    ck.finish([tout])
    for g in reversed(col_ctx):
        g.__exit__(None, None, None)
    return nc




def _pt(a):
    return np.ascontiguousarray(np.asarray(a).T)


def _cols(a, n):
    return np.ascontiguousarray(np.asarray(a).reshape(n, 128).T)


def prep_shared(inputs, layers):
    sh = {}
    sh["c_id_bf"] = np.eye(128, dtype=ml_dtypes.bfloat16)
    sh["c_id_f32"] = np.eye(128, dtype=np.float32)
    if 0 in layers:
        sh["L0_win"] = np.ascontiguousarray(inputs["L0_attn_W_in"])
        sh["L0_wx"] = np.ascontiguousarray(inputs["L0_attn_W_x"])
        sh["L0_wout0"] = np.ascontiguousarray(inputs["L0_attn_W_out"])
        sh["L0_wbase"] = np.ascontiguousarray(inputs["L0_ffn_W_base"])
        sh["L0_wd1"] = np.ascontiguousarray(inputs["L0_ffn_Wd1"])
        sh["L0_wd2"] = np.ascontiguousarray(inputs["L0_ffn_Wd2"])
        sh["L0_wadown"] = np.concatenate(
            [inputs["L0_ffn_Wa_down"][e] for e in range(8)], axis=1)
        sh["L0_waup"] = np.concatenate(
            [inputs["L0_ffn_Wa_up"][e] for e in range(8)], axis=0)
        sh["L0_baup"] = np.ascontiguousarray(inputs["L0_ffn_ba_up"])
        al = np.asarray(inputs["L0_attn_A_log"])
        asc = np.zeros((256, 128), np.float32)
        for kc in range(16):
            for j in range(16):
                for s in range(16):
                    for cl in range(8):
                        asc[kc * 16 + j, s * 8 + cl] =                             al[kc * 128 + 8 * j + cl, s]
        sh["L0_ascan"] = np.ascontiguousarray(asc.T)
        sred = np.zeros((16, 128, 128), np.float32)
        for j in range(16):
            for p in range(128):
                sred[j, p, 8 * j + (p % 8)] = 1.0
        sh["L0_sred"] = sred.transpose(1, 0, 2).astype(
            ml_dtypes.bfloat16).copy()
        sh["c_l0_nw"] = _cols(inputs["L0_attn_norm_w"], 8)
        cw = np.asarray(inputs["L0_attn_conv_w"]).reshape(2048, 4)
        sh["c_l0_cw"] = np.ascontiguousarray(
            cw.reshape(16, 128, 4).transpose(1, 0, 2).reshape(128, 64))
        sh["c_l0_cb"] = _cols(inputs["L0_attn_conv_b"], 16)
        sh["c_l0_wdt"] = _cols(inputs["L0_attn_w_dt"], 16)
        sh["c_l0_bdt"] = _cols(inputs["L0_attn_b_dt"], 16)
        sh["c_l0_D"] = _cols(inputs["L0_attn_D_param"], 16)
        sh["c_l0_bb"] = _cols(inputs["L0_ffn_b_base"], 8)
        bd1 = np.asarray(inputs["L0_ffn_bd1"])
        sh["c_l0_bd1"] = np.ascontiguousarray(
            bd1.reshape(8, 2, 128).transpose(2, 0, 1).reshape(128, 16))
        bd2 = np.asarray(inputs["L0_ffn_bd2"])
        sh["c_l0_bd2"] = np.ascontiguousarray(
            bd2.reshape(8, 8, 128).transpose(2, 0, 1).reshape(128, 64))
        sh["c_l0_bad"] = _cols(np.asarray(inputs["L0_ffn_ba_down"])
                               .reshape(512), 4)
    if 1 in layers:
        sh["L1_wq"] = _pt(inputs["L1_attn_wq"])
        sh["L1_wk"] = _pt(inputs["L1_attn_wk"])
        sh["L1_wv"] = _pt(inputs["L1_attn_wv"])
        sh["L1_w1"] = _pt(inputs["L1_attn_w1"])
        sh["L1_w2"] = _pt(inputs["L1_attn_w2"])
        sh["L1_wg"] = _pt(inputs["L1_ffn_w_gate"])
        sh["L1_wu"] = _pt(inputs["L1_ffn_w_up"])
        sh["L1_wd"] = _pt(inputs["L1_ffn_w_down"])
        sh["c_l1_g1"] = _cols(inputs["L1_attn_g1"], 8)
        sh["c_l1_be1"] = _cols(inputs["L1_attn_be1"], 8)
        sh["c_l1_g2"] = _cols(inputs["L1_attn_g2"], 8)
        sh["c_l1_be2"] = _cols(inputs["L1_attn_be2"], 8)
        sh["c_l1_b1"] = _cols(inputs["L1_attn_b1"], 32)
        sh["c_l1_b2"] = _cols(inputs["L1_attn_b2"], 8)
    if 2 in layers:
        sh["L2_wq2"] = np.ascontiguousarray(inputs["L2_attn_W_q"])
        sh["L2_wk2"] = np.ascontiguousarray(inputs["L2_attn_W_k"])
        sh["L2_wv2"] = np.ascontiguousarray(inputs["L2_attn_W_v"])
        sh["L2_wo2"] = np.ascontiguousarray(inputs["L2_attn_W_o"])
        sh["L2_rw"] = _pt(inputs["L2_ffn_router_w"])
        sh["L2_rb"] = np.ascontiguousarray(inputs["L2_ffn_router_b"])
        sh["L2_we2"] = np.ascontiguousarray(inputs["L2_ffn_expert_weights"])
    if 3 in layers:
        sh["L3_wqkv"] = _pt(inputs["L3_attn_w_qkv"])
        sh["L3_wout"] = _pt(inputs["L3_attn_w_out"])
        sh["L3_wfc"] = np.ascontiguousarray(inputs["L3_ffn_w_fc"])
        sh["L3_wproj"] = np.ascontiguousarray(inputs["L3_ffn_w_proj"])
        b = np.asarray(inputs["L3_attn_b_qkv"])
        sh["c_l3_bq"] = _cols(b[0:D], 8)
        sh["c_l3_bk"] = _cols(b[D:2 * D], 8)
        sh["L3_bv"] = np.ascontiguousarray(b[2 * D:3 * D]).reshape(1, D)
        sh["c_l3_bo"] = _cols(inputs["L3_attn_b_out"], 8)
        sh["c_l3_bfc"] = _cols(inputs["L3_ffn_b_fc"], 32)
        sh["c_l3_bpj"] = _cols(inputs["L3_ffn_b_proj"], 8)
        p = np.arange(128)[:, None]
        f = np.arange(2176)[None, :]
        ramp = (f - 2048) - p
        em = np.where(ramp <= 0, np.exp(np.minimum(ramp, 0)), 0.0)
        sh["L3_em"] = em.astype(ml_dtypes.bfloat16)
    return sh


def run_layers(inputs, x_stack, layers):
    sh = prep_shared(inputs, layers)
    nc = build_program(layers)
    in_maps = []
    for c in range(8):
        m = dict(sh)
        m["x_in"] = x_stack[c]
        in_maps.append(m)
    res = run_bass_kernel_spmd(nc, in_maps, core_ids=list(range(8)))
    return [np.asarray(res.results[c]["y_out"]) for c in range(8)]


def kernel(**inputs):
    x = np.asarray(inputs["L0_attn_x"])
    xs = [np.ascontiguousarray(x[c].T) for c in range(8)]
    xs = run_layers(inputs, xs, [0, 1, 2, 3])
    return np.stack([o.T for o in xs]).astype(np.float32)




def emit_l2(ck, bufs, ln, W, cols, phases="RAM"):
    import os as _os
    nc = ck.nc
    NH, HD = 16, 64

    with (
        nc.sbuf_tensor("l2_rw", [128, KC, 8], BF16) as rw,
        nc.sbuf_tensor("l2_rbt", [128, 8], F32) as rbt,
        nc.sbuf_tensor("l2_lg", [128, 8], F32) as lg,
        nc.sbuf_tensor("l2_gcols", [128, 8], F32) as gcols,
        nc.sbuf_tensor("l2_grows", [8, T], F32) as grows,
        nc.sbuf_tensor("l2_c", [128, 16], F32) as c,
    ):
        ck.join()
        t_rw = ck.dma("qp", rw[:, :, :],
                      W["rw"].rearrange("(kc kp) n -> kp kc n", kp=128))
        t_rb = ck.dma("qs", rbt[:, :].unsqueeze(1),
                      W["rb"].unsqueeze(0).partition_broadcast(128))
        for tt in range(16):
            ttsl = slice(tt * 128, (tt + 1) * 128)
            ck.waits("pe", [t_rw])
            ck.wait("pe", bufs.mid_free)
            for kc in range(KC):
                ck.pe.matmul(bufs.ps_mid[:, 0:8], bufs.xn[:, kc, ttsl],
                             rw[:, kc, :], start=(kc == 0),
                             stop=(kc == KC - 1))
            tp = ck.tok("pe")
            ck.waits("dve", [tp, t_rb])
            ck.dve.tensor_add(lg[:, :], bufs.ps_mid[:, 0:8], rbt[:, :])
            bufs.mid_free = ck.tok("dve")
            ck.vreduce(c[:, 0:1], lg[:, :], AX.X, ALU.max)
            ck.dve.tensor_scalar_mul(c[:, 1:2], c[:, 0:1], -1.0)
            td = ck.tok("dve")
            ck.wait("act", td)
            ck.act.activation(lg[:, :], lg[:, :], AF.Exp, bias=c[:, 1:2],
                              accum_out=c[:, 2:3])
            ta = ck.tok("act")
            ck.wait("dve", ta)
            ck.vreduce(c[:, 3:4], lg[:, :], AX.X, ALU.max)
            ck.dve.tensor_scalar(c[:, 4:5], c[:, 3:4], -1.0, None, ALU.mult)
            ck.dve.scalar_tensor_tensor(gcols[:, 0:8], lg[:, :], c[:, 3:4],
                                        lg[:, :], ALU.is_equal, ALU.bypass)
            ck.dve.scalar_tensor_tensor(lg[:, :], gcols[:, 0:8], c[:, 4:5],
                                        lg[:, :], ALU.mult, ALU.add)
            ck.vreduce(c[:, 5:6], lg[:, :], AX.X, ALU.max)
            ck.dve.tensor_add(c[:, 6:7], c[:, 3:4], c[:, 5:6])
            ck.dve.tensor_scalar(c[:, 7:8], c[:, 2:3], 1e-10, None, ALU.mult)
            ck.dve.tensor_add(c[:, 6:7], c[:, 6:7], c[:, 7:8])
            ck.recip(c[:, 8:9], c[:, 6:7])
            ck.dve.tensor_mul(c[:, 9:10], c[:, 3:4], c[:, 8:9])
            ck.dve.tensor_mul(c[:, 10:11], c[:, 5:6], c[:, 8:9])
            ck.dve.tensor_add(c[:, 11:12], c[:, 9:10], c[:, 10:11])
            ck.dve.tensor_scalar_max(c[:, 11:12], c[:, 11:12], 1e-10)
            ck.recip(c[:, 12:13], c[:, 11:12])
            ck.dve.tensor_scalar_add(c[:, 12:13], c[:, 12:13], 1e-10)
            ck.dve.tensor_mul(c[:, 13:14], c[:, 9:10], c[:, 12:13])
            ck.dve.tensor_mul(c[:, 14:15], c[:, 10:11], c[:, 12:13])
            ck.dve.scalar_tensor_tensor(lg[:, :], lg[:, :], c[:, 5:6],
                                        lg[:, :], ALU.is_equal, ALU.bypass)
            ck.dve.scalar_tensor_tensor(gcols[:, 0:8], gcols[:, 0:8],
                                        c[:, 13:14], gcols[:, 0:8],
                                        ALU.mult, ALU.bypass)
            ck.dve.scalar_tensor_tensor(gcols[:, 0:8], lg[:, :], c[:, 14:15],
                                        gcols[:, 0:8], ALU.mult, ALU.add)
            tg = ck.tok("dve")
            ck.waits("pe", [tg, bufs.id_tok])
            ck.pe.transpose(bufs.ps_mid[0:8, 256:384], gcols[:, 0:8],
                            bufs.id_f32[:, :])
            tp = ck.tok("pe")
            ck.wait("dve", tp)
            ck.dve.tensor_copy(grows[0:8, ttsl], bufs.ps_mid[0:8, 256:384])
            bufs.mid_free = ck.tok("dve")
        tgr = ck.tok("dve")
        if _os.environ.get("KDBG_L2R"):
            d_gr = nc.dram_tensor("d_gr", [8, T], F32,
                                  kind="ExternalOutput").ap()
            d_c = nc.dram_tensor("d_c", [128, 16], F32,
                                 kind="ExternalOutput").ap()
            ck.dma("qs", d_gr, grows[:, :], after=[tgr])
            ck.dma("qs", d_c, c[:, :], after=[tgr])
            ck.join()
        ck.wait("sp", tgr)
        t_up = ck.dma("qs", ck.scr_grows[:, :], grows[:, :])

    avg = nc.sbuf_tensor("l2_av", [128, KC, T], BF16)
    av_all = avg.__enter__()
    if "A" not in phases:
        ck.join()
        ck.pool.memset(av_all[:, :, :], 0.0)
        ck.join()
    if "A" in phases:
     with (
         nc.sbuf_tensor("l2_wh0", [128, KC, 3 * HD], BF16) as wqkv0,
         nc.sbuf_tensor("l2_wh1", [128, KC, 3 * HD], BF16) as wqkv1,
         nc.sbuf_tensor("l2_qh", [64, T], BF16) as qh,
         nc.sbuf_tensor("l2_kh", [64, T], BF16) as kh,
         nc.sbuf_tensor("l2_vh", [128, 16, HD], BF16) as vh,
         nc.sbuf_tensor("l2_P", [128, T], BF16) as Pb,
         nc.sbuf_tensor("l2_PT", [128, 16, 128], BF16) as PT,
         nc.sbuf_tensor("l2_sq", [64, T], BF16) as sqb,
         nc.sbuf_tensor("l2_nb", [128, 16], F32) as nbc,
         nc.sbuf_tensor("l2_rs", [128, 16], F32) as rsc,
         nc.sbuf_tensor("l2_r16", [16, 128], F32) as r16,
     ):
         ck.join()
         whb = [wqkv0, wqkv1]
         wh_load = {}
         wh_free = [None, None]

         def load_head(h):
             if h >= NH or h in wh_load:
                 return
             b = whb[h % 2]
             after = [wh_free[h % 2]] if wh_free[h % 2] else []
             toks = [ck.dma("qp", b[:, :, 0:HD],
                            W["wq2"][h].rearrange("(kc kp) n -> kp kc n",
                                                  kp=128), after=after)]
             toks.append(ck.dma("qp", b[:, :, HD:2 * HD],
                                W["wk2"][h].rearrange("(kc kp) n -> kp kc n",
                                                      kp=128), after=after))
             toks.append(ck.dma("qp", b[:, :, 2 * HD:3 * HD],
                                W["wv2"][h].rearrange("(kc kp) n -> kp kc n",
                                                      kp=128), after=after))
             wh_load[h] = toks

         load_head(0)
         load_head(1)
         for h in range(NH):
             wh = whb[h % 2]
             ck.waits("pe", wh_load[h])
             for tc in range(NT):
                 tsl = slice(tc * 512, (tc + 1) * 512)
                 bq = bufs.ps_mid[0:64, 0:512]
                 for kc in range(KC):
                     ck.pe.matmul(bq, wh[:, kc, 0:HD], bufs.xn[:, kc, tsl],
                                  start=(kc == 0), stop=(kc == KC - 1))
                 tp = ck.tok("pe")
                 ck.wait("dve", tp)
                 ck.dve.tensor_copy(qh[:, tsl], bq)
                 tq = ck.tok("dve")
                 ck.wait("pe", tq)
                 for kc in range(KC):
                     ck.pe.matmul(bq, wh[:, kc, HD:2 * HD],
                                  bufs.xn[:, kc, tsl],
                                  start=(kc == 0), stop=(kc == KC - 1))
                 tp = ck.tok("pe")
                 ck.wait("dve", tp)
                 ck.dve.tensor_copy(kh[:, tsl], bq)
                 ck.wait("pe", ck.tok("dve"))
             for tt in range(16):
                 ttsl = slice(tt * 128, (tt + 1) * 128)
                 bv = bufs.ps_mid[:, 0:64]
                 for kc in range(KC):
                     ck.pe.matmul(bv, bufs.xn[:, kc, ttsl], wh[:, kc,
                                  2 * HD:3 * HD],
                                  start=(kc == 0), stop=(kc == KC - 1))
                 tp = ck.tok("pe")
                 ck.wait("dve", tp)
                 ck.dve.tensor_copy(vh[:, tt, :], bv)
                 ck.wait("pe", ck.tok("dve"))
             wh_free[h % 2] = ck.tok("pe")
             load_head(h + 1)
             tnb = None
             for qt in range(16):
                 qsl = slice(qt * 128, (qt + 1) * 128)
                 ck.wait("pe", bufs.big_free)
                 for kb in range(4):
                     ck.pe.matmul(bufs.ps_big[:, kb * 512:(kb + 1) * 512],
                                  qh[:, qsl], kh[:, kb * 512:(kb + 1) * 512],
                                  start=True, stop=True)
                 tp = ck.tok("pe")
                 ck.wait("dve", tp)
                 ck.vreduce(nbc[:, qt:qt + 1], bufs.ps_big[:, :], AX.X,
                            ALU.max)
                 ck.dve.tensor_scalar_mul(nbc[:, qt:qt + 1],
                                          nbc[:, qt:qt + 1], -0.125)
                 tnb = ck.tok("dve")
                 ck.waits("act", [tp, tnb])
                 ck.act.activation(Pb[:, :], bufs.ps_big[:, :], AF.Exp,
                                   bias=nbc[:, qt:qt + 1], scale=0.125,
                                   accum_out=rsc[:, qt:qt + 1])
                 ta = ck.tok("act")
                 bufs.big_free = ta
                 for half in range(2):
                     ck.wait("pe", ta)
                     ck.wait("pe", bufs.trb_free)
                     for b8 in range(8):
                         kt = half * 8 + b8
                         ck.pe.transpose(
                             bufs.ps_trb[:, b8 * 128:(b8 + 1) * 128],
                             Pb[:, kt * 128:(kt + 1) * 128], bufs.id_bf[:, :])
                     tp = ck.tok("pe")
                     ck.wait("dve", tp)
                     ck.dve.tensor_copy(
                         PT[:, half * 8:(half + 1) * 8, :],
                         bufs.ps_trb[:, :].rearrange("p (b n) -> p b n", n=128))
                     bufs.trb_free = ck.tok("dve")
                 tpt = ck.tok("dve")
                 ck.wait("pe", tpt)
                 for kt in range(16):
                     ck.pe.matmul(bufs.ps_mid[0:64, 0:128], vh[:, kt, :],
                                  PT[:, kt, :], start=(kt == 0),
                                  stop=(kt == 15))
                 tp = ck.tok("pe")
                 ck.wait("dve", ta)
                 ck.recip(rsc[:, qt:qt + 1], rsc[:, qt:qt + 1])
                 trr = ck.tok("dve")
                 ck.waits("pe", [trr, bufs.id_tok])
                 ck.pe.transpose(bufs.ps_mid[0:1, 384:512],
                                 rsc[:, qt:qt + 1], bufs.id_f32[:, :])
                 tp2 = ck.tok("pe")
                 ck.wait("dve", tp2)
                 ck.dve.tensor_copy(bufs.rowB[0:1, qsl],
                                    bufs.ps_mid[0:1, 384:512])
                 trow = ck.tok("dve")
                 ck.wait("sp", trow)
                 tu3 = ck.dma("qs", ck.scr_rows[18:19, qsl],
                              bufs.rowB[0:1, qsl])
                 tb3 = ck.dma("qs",
                              bufs.sc3[:, qsl].unsqueeze(1),
                              ck.scr_rows[18:19, qsl]
                              .partition_broadcast(128), after=[tu3])
                 ck.waits("dve", [tb3, ck.tok("pe")])
                 hc, hp = h // 2, (h % 2) * 64
                 ck.dve.tensor_mul(av_all[hp:hp + 64, hc, qsl],
                                   bufs.ps_mid[0:64, 0:128],
                                   bufs.sc3[hp:hp + 64, qsl])
                 ck.wait("pe", ck.tok("dve"))

    TCM = 256
    if "M" in phases:
     with (
         nc.sbuf_tensor("l2_xe", [128, 8, KC, TCM], BF16) as xe,
         nc.sbuf_tensor("l2_wo", [128, KC, 128], BF16) as wob,
         nc.sbuf_tensor("l2_we0", [128, KC, 128], BF16) as we0,
         nc.sbuf_tensor("l2_we1", [128, KC, 128], BF16) as we1,
         nc.sbuf_tensor("l2_gb", [128, TCM], F32) as gb,
     ):
         ck.join()
         for tc in range(T // TCM):
             tsl = slice(tc * TCM, (tc + 1) * TCM)
             tprev = ck.tok("dve")
             for e in range(8):
                 tg = ck.dma("qs", gb[:, :].unsqueeze(1),
                             ck.scr_grows[e:e + 1, tsl]
                             .partition_broadcast(128), after=[tprev])
                 ck.wait("dve", tg)
                 for kc in range(KC):
                     ck.dve.tensor_mul(xe[:, e, kc, :], bufs.xn[:, kc, tsl],
                                       gb[:, :])
                 tprev = ck.tok("dve")
             txe = ck.tok("dve")
             web = [we0, we1]
             for oc in range(KC):
                 two = ck.dma("qp", wob[:, :, :],
                              W["wo2"][:, oc * 128:(oc + 1) * 128]
                              .rearrange("(kc kp) n -> kp kc n", kp=128),
                              after=[bufs.wo_free] if bufs.wo_free else [])
                 bk = bufs.ps_big[:, (oc % 4) * 512:(oc % 4) * 512 + TCM]
                 ck.wait("pe", bufs.big_free)
                 ck.waits("pe", [two, txe])
                 for kc in range(KC):
                     ck.pe.matmul(bk, wob[:, kc, :], av_all[:, kc, tsl],
                                  start=(kc == 0), stop=False)
                 bufs.wo_free = ck.tok("pe")
                 for e in range(8):
                     tw = ck.dma("qp", web[e % 2][:, :, :],
                                 W["we2"][e][:, oc * 128:(oc + 1) * 128]
                                 .rearrange("(kc kp) n -> kp kc n", kp=128),
                                 after=[bufs.we_free[e % 2]]
                                 if bufs.we_free[e % 2] else [])
                     ck.wait("pe", tw)
                     for kc in range(KC):
                         ck.pe.matmul(bk, web[e % 2][:, kc, :],
                                      xe[:, e, kc, :], start=False,
                                      stop=(e == 7 and kc == KC - 1))
                     bufs.we_free[e % 2] = ck.tok("pe")
                 tp = ck.tok("pe")
                 ck.wait("dve", tp)
                 ck.dve.tensor_add(bufs.X[:, oc, tsl], bufs.X[:, oc, tsl], bk)
                 bufs.big_free = ck.tok("dve")
    avg.__exit__(None, None, None)




def quantize_bfp_weight(ck, bufs, w_ap, N, Kd, scr_ap, name):
    nc = ck.nc
    kc = Kd // 128
    nchunk = N // 128
    nblk = Kd // 32
    with (
        nc.sbuf_tensor(f"{name}_f", [128, Kd], F32) as fb,
        nc.sbuf_tensor(f"{name}_q", [128, Kd], BF16) as qb,
        nc.sbuf_tensor(f"{name}_am", [128, nblk], F32) as am,
        nc.sbuf_tensor(f"{name}_rc", [128, nblk], F32) as rc,
        nc.sbuf_tensor(f"{name}_st", [128, 1024], BF16) as stg,
    ):
        ck.join()
        prev = None
        prev_store = [None]
        for r in range(nchunk):
            tl = ck.dma("qs", fb[:, :], w_ap[r * 128:(r + 1) * 128, :],
                        after=[prev] if prev else [])
            ck.wait("dve", tl)
            v3 = fb[:, :].rearrange("p (b j) -> p b j", j=32)
            ck.vreduce(am[:, :], v3, AX.X, ALU.max,
                       apply_absolute_value=True)
            ck.dve.tensor_scalar_max(am[:, :], am[:, :], 1.17549435e-38)
            ck.dve.tensor_scalar(am[:, :].bitcast(mybir.dt.int32),
                                 am[:, :].bitcast(mybir.dt.int32),
                                 0x7F800000, None, ALU.bitwise_and)
            ck.dve.drain()
            ck.recip(rc[:, :], am[:, :])
            ck.dve.tensor_scalar_mul(rc[:, :], rc[:, :], 8.0)
            ck.dve.tensor_scalar_mul(am[:, :], am[:, :], 0.125)
            ck.dve.drain()
            rcb = rc[:, :].unsqueeze(2).broadcast_to([128, nblk, 32])
            amb = am[:, :].unsqueeze(2).broadcast_to([128, nblk, 32])
            ck.dve.tensor_tensor(v3, v3, rcb, ALU.mult)
            ck.dve.tensor_scalar(fb[:, :], fb[:, :], RMAGIC, RMAGIC,
                                 ALU.add, ALU.subtract)
            ck.dve.tensor_tensor(qb[:, :].rearrange("p (b j) -> p b j", j=32),
                                 fb[:, :].rearrange("p (b j) -> p b j", j=32),
                                 amb, ALU.mult)
            tq = ck.tok("dve")
            prev = tq
            for g in range(kc // 8):
                ck.waits("pe", [tq, bufs.id_tok, bufs.trb_free])
                for b8 in range(8):
                    kb = g * 8 + b8
                    ck.pe.transpose(bufs.ps_trb[:, b8 * 128:(b8 + 1) * 128],
                                    qb[:, kb * 128:(kb + 1) * 128],
                                    bufs.id_bf[:, :])
                tp = ck.tok("pe")
                ck.wait("dve", tp)
                ck.wait("dve", prev_store[0] if prev_store else None)
                ck.dve.tensor_copy(stg[:, :], bufs.ps_trb[:, :])
                bufs.trb_free = ck.tok("dve")
                ts_ = ck.dma("qs",
                             scr_ap[:, g * 8:(g + 1) * 8,
                                    r * 128:(r + 1) * 128],
                             stg[:, :].rearrange("p (b n) -> p b n", n=128),
                             after=[bufs.trb_free])
                prev_store[0] = ts_
            ck.wait("dve", ck.tok("pe"))
        ck.waits("sp", [prev_store[0]])


def emit_l3(ck, bufs, ln, W, cols, phases="AF"):
    import os as _os
    nc = ck.nc
    NH, HD = 16, 64

    avg = nc.sbuf_tensor("l3_av", [128, KC, T], BF16)
    av_all = avg.__enter__()
    if "A" not in phases:
        ck.join()
        ck.pool.memset(av_all[:, :, :], 0.0)
        ck.join()
    else:
     with (
        nc.sbuf_tensor("l3_em", [128, 2176], BF16) as em,
        nc.sbuf_tensor("l3_wq0", [128, KC, HD], BF16) as wq0,
        nc.sbuf_tensor("l3_wk0", [128, KC, HD], BF16) as wk0,
        nc.sbuf_tensor("l3_wv0", [128, KC, HD], BF16) as wv0,
        nc.sbuf_tensor("l3_wq1", [128, KC, HD], BF16) as wq1,
        nc.sbuf_tensor("l3_wk1", [128, KC, HD], BF16) as wk1,
        nc.sbuf_tensor("l3_wv1", [128, KC, HD], BF16) as wv1,
        nc.sbuf_tensor("l3_qh", [64, T], BF16) as qh,
        nc.sbuf_tensor("l3_kh", [64, T], BF16) as kh,
        nc.sbuf_tensor("l3_vh", [128, 16, HD], BF16) as vh,
        nc.sbuf_tensor("l3_P", [128, T], BF16) as Pb,
        nc.sbuf_tensor("l3_PT", [128, 16, 128], BF16) as PT,
        nc.sbuf_tensor("l3_sq", [64, T], BF16) as sqb,
        nc.sbuf_tensor("l3_nb", [128, 16], F32) as nbc,
        nc.sbuf_tensor("l3_rs", [128, 16], F32) as rsc,
     ):
        ck.join()
        t_em = ck.dma("qs", em[:, :], W["em"])
        bufs.vbrow_tok = ck.dma("qs", bufs.sc1[:, 0:1024].unsqueeze(1),
                                W["bv"].partition_broadcast(128))
        whb = [[wq0, wk0, wv0], [wq1, wk1, wv1]]
        wh_load = {}
        wh_free = [None, None]

        def load_head(h):
            if h >= NH or h in wh_load:
                return
            b = whb[h % 2]
            after = [wh_free[h % 2]] if wh_free[h % 2] else []
            toks = []
            for i in range(3):
                colsl = slice(i * D + h * HD, i * D + (h + 1) * HD)
                toks.append(ck.dma(
                    "qp", b[i][:, :, :],
                    W["wqkv"][:, colsl].rearrange("(kc kp) n -> kp kc n",
                                                  kp=128), after=after))
            wh_load[h] = toks

        load_head(0)
        load_head(1)
        for h in range(NH):
            wh = whb[h % 2]
            ck.waits("pe", wh_load[h])
            for tc in range(NT):
                tsl = slice(tc * 512, (tc + 1) * 512)
                bq = bufs.ps_mid[0:64, 0:512]
                for kc in range(KC):
                    ck.pe.matmul(bq, wh[0][:, kc, :], bufs.xn[:, kc, tsl],
                                 start=(kc == 0), stop=(kc == KC - 1))
                tp = ck.tok("pe")
                ck.wait("dve", tp)
                ck.dve.tensor_scalar(qh[:, tsl], bq,
                                     cols["l3_bq"]
                                     [(h % 2) * 64:(h % 2) * 64 + 64,
                                      h // 2:h // 2 + 1], None, ALU.add)
                ta = ck.tok("dve")
                ck.wait("pe", ta)
                for kc in range(KC):
                    ck.pe.matmul(bq, wh[1][:, kc, :], bufs.xn[:, kc, tsl],
                                 start=(kc == 0), stop=(kc == KC - 1))
                tp = ck.tok("pe")
                ck.wait("dve", tp)
                ck.dve.tensor_scalar(kh[:, tsl], bq,
                                     cols["l3_bk"]
                                     [(h % 2) * 64:(h % 2) * 64 + 64,
                                      h // 2:h // 2 + 1], None, ALU.add)
                ta = ck.tok("dve")
                ck.wait("pe", ta)
            for tt in range(16):
                ttsl = slice(tt * 128, (tt + 1) * 128)
                bv = bufs.ps_mid[:, 0:64]
                for kc in range(KC):
                    ck.pe.matmul(bv, bufs.xn[:, kc, ttsl], wh[2][:, kc, :],
                                 start=(kc == 0), stop=(kc == KC - 1))
                tp = ck.tok("pe")
                ck.waits("dve", [tp, bufs.vbrow_tok])
                ck.dve.tensor_add(vh[:, tt, :], bv,
                                  bufs.sc1[:, h * 64:(h + 1) * 64])
                ck.wait("pe", ck.tok("dve"))
            wh_free[h % 2] = ck.tok("pe")
            load_head(h + 1)
            tnb = None
            for qt in range(16):
                qsl = slice(qt * 128, (qt + 1) * 128)
                Wk = (qt + 1) * 128
                nkb = (Wk + 511) // 512
                ck.wait("pe", bufs.big_free)
                for kb in range(nkb):
                    wkb = min(512, Wk - kb * 512)
                    ck.pe.matmul(bufs.ps_big[:, kb * 512:kb * 512 + wkb],
                                 qh[:, qsl], kh[:, kb * 512:kb * 512 + wkb],
                                 start=True, stop=True)
                tp = ck.tok("pe")
                ck.wait("dve", tp)
                ck.vreduce(nbc[:, qt:qt + 1], bufs.ps_big[:, 0:Wk], AX.X,
                           ALU.max)
                ck.dve.tensor_scalar_mul(nbc[:, qt:qt + 1],
                                         nbc[:, qt:qt + 1], -0.125)
                tnb = ck.tok("dve")
                ck.waits("act", [tp, t_em, tnb])
                ck.act.activation(Pb[:, 0:Wk], bufs.ps_big[:, 0:Wk], AF.Exp,
                                  bias=nbc[:, qt:qt + 1], scale=0.125)
                ta = ck.tok("act")
                ck.wait("dve", ta)
                ck.dve.scalar_tensor_tensor(
                    Pb[:, 0:Wk], Pb[:, 0:Wk], 0.0,
                    em[:, 2048 - qt * 128:2048 - qt * 128 + Wk],
                    ALU.add, ALU.mult, accum_out=rsc[:, qt:qt + 1])
                tmsk = ck.tok("dve")
                bufs.big_free = tmsk
                nkt = qt + 1
                for half in range((nkt + 7) // 8):
                    n8 = min(8, nkt - half * 8)
                    ck.wait("pe", tmsk)
                    ck.wait("pe", bufs.trb_free)
                    for b8 in range(n8):
                        kt = half * 8 + b8
                        ck.pe.transpose(
                            bufs.ps_trb[:, b8 * 128:(b8 + 1) * 128],
                            Pb[:, kt * 128:(kt + 1) * 128], bufs.id_bf[:, :])
                    tp = ck.tok("pe")
                    ck.wait("dve", tp)
                    ck.dve.tensor_copy(
                        PT[:, half * 8:half * 8 + n8, :],
                        bufs.ps_trb[:, 0:n8 * 128]
                        .rearrange("p (b n) -> p b n", n=128))
                    bufs.trb_free = ck.tok("dve")
                tpt = ck.tok("dve")
                ck.wait("pe", tpt)
                for kt in range(nkt):
                    ck.pe.matmul(bufs.ps_mid[0:64, 0:128], vh[:, kt, :],
                                 PT[:, kt, :], start=(kt == 0),
                                 stop=(kt == nkt - 1))
                tp = ck.tok("pe")
                ck.wait("dve", ta)
                ck.recip(rsc[:, qt:qt + 1], rsc[:, qt:qt + 1])
                trr = ck.tok("dve")
                ck.waits("pe", [trr, bufs.id_tok])
                ck.pe.transpose(bufs.ps_mid[0:1, 384:512],
                                rsc[:, qt:qt + 1], bufs.id_f32[:, :])
                tp2 = ck.tok("pe")
                ck.wait("dve", tp2)
                ck.dve.tensor_copy(bufs.rowB[0:1, qsl],
                                   bufs.ps_mid[0:1, 384:512])
                trow = ck.tok("dve")
                ck.wait("sp", trow)
                tu3 = ck.dma("qs", ck.scr_rows[18:19, qsl],
                             bufs.rowB[0:1, qsl])
                tb3 = ck.dma("qs", bufs.sc3[:, qsl].unsqueeze(1),
                             ck.scr_rows[18:19, qsl]
                             .partition_broadcast(128), after=[tu3])
                ck.waits("dve", [tb3, ck.tok("pe")])
                hc, hp = h // 2, (h % 2) * 64
                ck.dve.tensor_mul(av_all[hp:hp + 64, hc, qsl],
                                  bufs.ps_mid[0:64, 0:128],
                                  bufs.sc3[hp:hp + 64, qsl])
                ck.wait("pe", ck.tok("dve"))
            if _os.environ.get("KDBG_L3A") and h == 0:
                d_rs = nc.dram_tensor("d_rs", [128, 16], F32,
                                      kind="ExternalOutput").ap()
                d_nb = nc.dram_tensor("d_nb", [128, 16], F32,
                                      kind="ExternalOutput").ap()
                d_q3 = nc.dram_tensor("d_q3", [64, T], BF16,
                                      kind="ExternalOutput").ap()
                d_k3 = nc.dram_tensor("d_k3", [64, T], BF16,
                                      kind="ExternalOutput").ap()
                d_P3 = nc.dram_tensor("d_P3", [128, T], BF16,
                                      kind="ExternalOutput").ap()
                ck.join()
                ck.dma("qs", d_rs, rsc[:, :])
                ck.dma("qs", d_nb, nbc[:, :])
                ck.dma("qs", d_q3, qh[:, :])
                ck.dma("qs", d_k3, kh[:, :])
                ck.dma("qs", d_P3, Pb[:, :])
                ck.join()

    quantize_bfp_weight(ck, bufs, W["wfc"], 4096, D, W["scr_fc"], "l3fc")
    quantize_bfp_weight(ck, bufs, W["wproj"], D, 4096, W["scr_pj"], "l3pj")
    TC3 = 256
    with (
        nc.sbuf_tensor("l3_tm", [128, 2048], BF16) as tmb,
        nc.sbuf_tensor("l3_am", [128, 64], F32) as am3,
        nc.sbuf_tensor("l3_rc", [128, 64], F32) as rc3,
        nc.sbuf_tensor("l3_h", [128, 32, TC3], BF16) as hbuf,
        nc.sbuf_tensor("l3_wo", [128, KC, 128], BF16) as wo3,
    ):
        ck.join()
        xq = bufs.xn

        def quant_tm(width):
            nb = width // 32
            v3 = tmb[:, 0:width].rearrange("p (b j) -> p b j", j=32)
            ck.vreduce(am3[:, 0:nb], v3, AX.X, ALU.max,
                       apply_absolute_value=True)
            ck.dve.tensor_scalar_max(am3[:, 0:nb], am3[:, 0:nb],
                                     1.17549435e-38)
            ck.dve.tensor_scalar(am3[:, 0:nb].bitcast(mybir.dt.int32),
                                 am3[:, 0:nb].bitcast(mybir.dt.int32),
                                 0x7F800000, None, ALU.bitwise_and)
            ck.dve.drain()
            ck.recip(rc3[:, 0:nb], am3[:, 0:nb])
            ck.dve.tensor_scalar_mul(rc3[:, 0:nb], rc3[:, 0:nb], 8.0)
            ck.dve.tensor_scalar_mul(am3[:, 0:nb], am3[:, 0:nb], 0.125)
            ck.dve.drain()
            rcb = rc3[:, 0:nb].unsqueeze(2).broadcast_to([128, nb, 32])
            amb = am3[:, 0:nb].unsqueeze(2).broadcast_to([128, nb, 32])
            ck.dve.tensor_tensor(bufs.sc2[:, 0:width]
                                 .rearrange("p (b j) -> p b j", j=32),
                                 v3, rcb, ALU.mult)
            ck.dve.tensor_scalar(bufs.sc2[:, 0:width], bufs.sc2[:, 0:width],
                                 RMAGIC, RMAGIC, ALU.add, ALU.subtract)
            ck.dve.tensor_tensor(v3, bufs.sc2[:, 0:width]
                                 .rearrange("p (b j) -> p b j", j=32),
                                 amb, ALU.mult)
            return ck.tok("dve")

        for tt in range(16):
            ttsl = slice(tt * 128, (tt + 1) * 128)
            ck.wait("pe", bufs.trb_free)
            ck.wait("pe", bufs.id_tok)
            for c in range(KC):
                ck.pe.transpose(bufs.ps_trb[:, c * 128:(c + 1) * 128],
                                bufs.xn[:, c, ttsl], bufs.id_bf[:, :])
            tp = ck.tok("pe")
            ck.wait("dve", tp)
            ck.dve.tensor_copy(tmb[:, 0:1024], bufs.ps_trb[:, :])
            bufs.trb_free = ck.tok("dve")
            tq = quant_tm(1024)
            ck.waits("pe", [tq, bufs.trb_free])
            for c in range(KC):
                ck.pe.transpose(bufs.ps_trb[:, c * 128:(c + 1) * 128],
                                tmb[:, c * 128:(c + 1) * 128],
                                bufs.id_bf[:, :])
            tp = ck.tok("pe")
            ck.wait("dve", tp)
            for c in range(KC):
                ck.dve.tensor_copy(xq[:, c, ttsl],
                                   bufs.ps_trb[:, c * 128:(c + 1) * 128])
            bufs.trb_free = ck.tok("dve")

        sfc = ScrStream(ck, "l3sfc", W["scr_fc"], KC, 4096, nb=128)
        spj = ScrStream(ck, "l3spj", W["scr_pj"], 32, D, nb=128)
        for tc in range(T // TC3):
            tsl = slice(tc * TC3, (tc + 1) * TC3)
            sfc.reset(); spj.reset()
            sfc.start()
            htoks = {}
            for n in range(32):
                wt = sfc.get(n)
                bk = bufs.ps_big[:, (n % 2) * 512:(n % 2) * 512 + TC3]
                if n >= 2:
                    ck.wait("pe", htoks[n - 2])
                for kc in range(KC):
                    ck.pe.matmul(bk, wt[:, kc, :], xq[:, kc, tsl],
                                 start=(kc == 0), stop=(kc == KC - 1))
                sfc.done(n)
                tp = ck.tok("pe")
                ck.wait("act", tp)
                ck.act.activation(hbuf[:, n, :], bk, AF.Gelu,
                                  bias=cols["l3_bfc"][:, n:n + 1])
                htoks[n] = ck.tok("act")
            for tt in range(TC3 // 128):
                t0 = tt * 128
                for half in range(2):
                    c0 = half * 16
                    for g in range(2):
                        ck.waits("pe", [htoks[31], bufs.trb_free,
                                        bufs.id_tok])
                        for b8 in range(8):
                            c = c0 + g * 8 + b8
                            ck.pe.transpose(
                                bufs.ps_trb[:, b8 * 128:(b8 + 1) * 128],
                                hbuf[:, c, t0:t0 + 128], bufs.id_bf[:, :])
                        tp = ck.tok("pe")
                        ck.wait("dve", tp)
                        ck.dve.tensor_copy(tmb[:, g * 1024:(g + 1) * 1024],
                                           bufs.ps_trb[:, :])
                        bufs.trb_free = ck.tok("dve")
                    tq = quant_tm(2048)
                    ck.waits("pe", [tq, bufs.trb_free])
                    for g in range(2):
                        ck.wait("pe", bufs.trb_free)
                        for b8 in range(8):
                            cc = g * 8 + b8
                            ck.pe.transpose(
                                bufs.ps_trb[:, b8 * 128:(b8 + 1) * 128],
                                tmb[:, cc * 128:(cc + 1) * 128],
                                bufs.id_bf[:, :])
                        tp = ck.tok("pe")
                        ck.wait("dve", tp)
                        for b8 in range(8):
                            cc = g * 8 + b8
                            ck.dve.tensor_copy(
                                hbuf[:, c0 + cc, t0:t0 + 128],
                                bufs.ps_trb[:, b8 * 128:(b8 + 1) * 128])
                        bufs.trb_free = ck.tok("dve")
            thq = ck.tok("dve")
            xtk = {}
            for oc in range(KC):
                two = ck.dma(
                    "qp", wo3[:, :, :],
                    W["wout"][:, oc * 128:(oc + 1) * 128]
                    .rearrange("(kc kp) n -> kp kc n", kp=128),
                    after=[bufs.wo_free] if bufs.wo_free else [])
                wt = spj.get(oc)
                bk = bufs.ps_big[:, 1024 + (oc % 2) * 512:
                                 1024 + (oc % 2) * 512 + TC3]
                if oc >= 2:
                    ck.wait("pe", xtk[oc - 2])
                ck.waits("pe", [two, thq])
                for kc in range(KC):
                    ck.pe.matmul(bk, wo3[:, kc, :], av_all[:, kc, tsl],
                                 start=(kc == 0), stop=False)
                bufs.wo_free = ck.tok("pe")
                for kc in range(32):
                    ck.pe.matmul(bk, wt[:, kc, :], hbuf[:, kc, :],
                                 start=False, stop=(kc == 31))
                spj.done(oc)
                tp = ck.tok("pe")
                ck.wait("dve", tp)
                ck.dve.scalar_tensor_tensor(
                    bufs.sc3[:, 0:TC3], bk, cols["l3_bo"][:, oc:oc + 1],
                    bufs.X[:, oc, tsl], ALU.add, ALU.add)
                ck.dve.tensor_scalar(bufs.X[:, oc, tsl], bufs.sc3[:, 0:TC3],
                                     cols["l3_bpj"][:, oc:oc + 1], None,
                                     ALU.add)
                xtk[oc] = ck.tok("dve")
            ck.wait("pe", xtk[KC - 1])
        spj.close()
        sfc.close()
    avg.__exit__(None, None, None)




def emit_l0(ck, bufs, ln, W, cols, phases="FM"):
    import os as _os
    nc = ck.nc
    DI = 2048
    CI = DI // 128

    if "F" in phases:
     with (
        nc.sbuf_tensor("l0_wd1", [128, KC, 256], BF16) as wd1a,
        nc.sbuf_tensor("l0_wd1b", [128, KC, 256], BF16) as wd1b,
        nc.sbuf_tensor("l0_wd2", [128, 2, 1024], BF16) as wd2a,
        nc.sbuf_tensor("l0_wd2b", [128, 2, 1024], BF16) as wd2b,
        nc.sbuf_tensor("l0_he", [128, 2, T], BF16) as he,
        nc.sbuf_tensor("l0_lacc", [128, 12], F32) as lacc,
        nc.sbuf_tensor("l0_a", [128, 4, T], BF16) as aab,
        nc.sbuf_tensor("l0_wad", [128, KC, 128], BF16) as wad,
        nc.sbuf_tensor("l0_wau", [128, 4, 128], BF16) as wau,
        nc.sbuf_tensor("l0_wbb", [128, KC, 128], BF16) as wbb,
        nc.sbuf_tensor("l0_ohc", [128, 4], F32) as ohc,
        nc.sbuf_tensor("l0_oh8", [8, 1], BF16) as oh8,
        nc.sbuf_tensor("l0_bup", [128, 8], F32) as bup,
        nc.sbuf_tensor("l0_baupw", [8, 1024], BF16) as baupw,
     ):
        ck.join()
        wd1s = [wd1a, wd1b]
        wd2s = [wd2a, wd2b]
        ld1 = {}
        ld2 = {}
        fr1 = [None, None]
        fr2 = [None, None]

        def load_e(e):
            if e >= 8 or e in ld1:
                return
            b = e % 2
            ld1[e] = ck.dma("qp", wd1s[b][:, :, :],
                            W["wd1"][e].rearrange("(kc kp) n -> kp kc n",
                                                  kp=128),
                            after=[fr1[b]] if fr1[b] else [])
            ld2[e] = ck.dma("qp", wd2s[b][:, :, :],
                            W["wd2"][e].rearrange("(kc kp) n -> kp kc n",
                                                  kp=128),
                            after=[fr2[b]] if fr2[b] else [])

        load_e(0)
        load_e(1)
        t_bw = ck.dma("qp", baupw[:, :], W["baup"])
        ck.wait("pool", bufs.ones_tok)
        ck.pool.memset(lacc[:, 0:12], 0.0)
        tz = ck.tok("pool")
        for e in range(8):
            b = e % 2
            ck.waits("pe", [ld1[e]])
            for n2 in range(2):
                for tc in range(NT):
                    tsl = slice(tc * 512, (tc + 1) * 512)
                    bk = bufs.ps_big[:, ((n2 * NT + tc) % 2) * 512:
                                     ((n2 * NT + tc) % 2) * 512 + 512]
                    ck.wait("pe", bufs.big_free)
                    for kc in range(KC):
                        ck.pe.matmul(bk, wd1s[b][:, kc,
                                     n2 * 128:(n2 + 1) * 128],
                                     bufs.xn[:, kc, tsl],
                                     start=(kc == 0), stop=(kc == KC - 1))
                    tp = ck.tok("pe")
                    ck.wait("act", tp)
                    ck.act.activation(he[:, n2, tsl], bk, AF.Relu,
                                      bias=cols["l0_bd1"][:, e * 2 + n2:
                                                          e * 2 + n2 + 1])
                    bufs.big_free = ck.tok("act")
            fr1[b] = ck.tok("pe")
            the = ck.tok("act")
            ck.waits("pe", [ld2[e], the])
            for oc in range(KC):
                for tc in range(NT):
                    tsl = slice(tc * 512, (tc + 1) * 512)
                    bk = bufs.ps_big[:, ((oc * NT + tc) % 2) * 512:
                                     ((oc * NT + tc) % 2) * 512 + 512]
                    ck.wait("pe", bufs.big_free)
                    for kc in range(2):
                        ck.pe.matmul(bk, wd2s[b][:, kc,
                                     oc * 128:(oc + 1) * 128],
                                     he[:, kc, tsl],
                                     start=(kc == 0), stop=(kc == 1))
                    tp = ck.tok("pe")
                    ck.waits("dve", [tp, tz])
                    ck.dve.scalar_tensor_tensor(
                        bufs.scb1[:, 0:512], bk,
                        cols["l0_bd2"][:, e * 8 + oc:e * 8 + oc + 1],
                        bufs.xn[:, oc, tsl], ALU.add, ALU.subtract)
                    td = ck.tok("dve")
                    bufs.big_free = td
                    ck.wait("act", td)
                    ck.act.activation(bufs.scb2[:, 0:512],
                                      bufs.scb1[:, 0:512], AF.Square,
                                      accum_out=lacc[:, 9:10])
                    ta = ck.tok("act")
                    ck.wait("dve", ta)
                    ck.dve.tensor_add(lacc[:, e:e + 1], lacc[:, e:e + 1],
                                      lacc[:, 9:10])
            fr2[b] = ck.tok("pe")
            load_e(e + 1)
        td = ck.tok("dve")
        ck.waits("pe", [td, bufs.ones_tok])
        ck.pe.matmul(bufs.ps_row[0:1, 0:8], bufs.ones_f32[:, :],
                     lacc[:, 0:8], start=True, stop=True)
        tp = ck.tok("pe")
        ck.wait("dve", tp)
        ck.dve.tensor_copy(bufs.rowA[0:1, 0:8], bufs.ps_row[0:1, 0:8])
        ck.vreduce(bufs.rowA[0:1, 8:9], bufs.rowA[0:1, 0:8], AX.X, ALU.min)
        ck.dve.tensor_scalar(bufs.rowA[0:1, 0:8], bufs.rowA[0:1, 0:8],
                             bufs.rowA[0:1, 8:9], None, ALU.is_le)
        toh = ck.tok("dve")
        ck.wait("sp", toh)
        tup = ck.dma("qs", ck.scr_rows[20:21, 0:8], bufs.rowA[0:1, 0:8])
        ohds = []
        ohv = ck.scr_rows[20:21, 0:8].rearrange("one e -> (one e)") \
            .unsqueeze(1).broadcast_to([8, 64])
        for r in range(4):
            ohds.append(ck.dma("qs", ohc[:, r:r + 1],
                               ohv[2 * r:2 * r + 2, :], after=[tup]))
        t8 = ck.dma("qp", oh8[:, :],
                    ck.scr_rows[20:21, 0:8]
                    .rearrange("one e -> (one e)").unsqueeze(1),
                    after=[tup])
        ck.waits("pe", [t8, t_bw])
        for oc in range(KC):
            ck.pe.matmul(bufs.ps_mid[0:128, 256 + 2 * oc:257 + 2 * oc],
                         baupw[:, oc * 128:(oc + 1) * 128], oh8[:, :],
                         start=True, stop=True)
        tp = ck.tok("pe")
        ck.wait("dve", tp)
        ck.dve.tensor_copy(bup[:, 0:8],
                           bufs.ps_mid[:, 256:272]
                           .rearrange("p (n two) -> p n two", two=2)
                           [:, :, 0])
        ck.dve.tensor_scalar(bup[:, 0:8], bup[:, 0:8], None, None,
                             ALU.bypass) if False else None
        ck.dve.tensor_tensor(bup[:, 0:8], bup[:, 0:8],
                             cols["l0_bb"][:, 0:8], ALU.add)
        tbup = ck.tok("dve")
        tld = ck.dma("qp", wad[:, :, :], W["wadown"][:, 0:128]
                     .rearrange("(kc kp) n -> kp kc n", kp=128))
        for r in range(4):
            tnext = None
            if r < 3:
                pass
            ck.waits("pe", [tld])
            for tc in range(NT):
                tsl = slice(tc * 512, (tc + 1) * 512)
                bk = bufs.ps_big[:, (tc % 2) * 512:(tc % 2) * 512 + 512]
                ck.wait("pe", bufs.big_free)
                for kc in range(KC):
                    ck.pe.matmul(bk, wad[:, kc, :], bufs.xn[:, kc, tsl],
                                 start=(kc == 0), stop=(kc == KC - 1))
                tp = ck.tok("pe")
                ck.wait("act", tp)
                ck.act.activation(bufs.scb1[:, 0:512], bk, AF.Relu,
                                  bias=cols["l0_bad"][:, r:r + 1])
                ta = ck.tok("act")
                bufs.big_free = ta
                ck.waits("dve", [ta] + ohds)
                ck.dve.tensor_scalar(aab[:, r, tsl], bufs.scb1[:, 0:512],
                                     ohc[:, r:r + 1], None, ALU.mult)
            tprev = ck.tok("pe")
            if r < 3:
                tld = ck.dma("qp", wad[:, :, :],
                             W["wadown"][:, (r + 1) * 128:(r + 2) * 128]
                             .rearrange("(kc kp) n -> kp kc n", kp=128),
                             after=[tprev])
        ta_done = ck.tok("dve")
        tlu = ck.dma("qp", wau[:, :, :], W["waup"][:, 0:128]
                     .rearrange("(kc kp) n -> kp kc n", kp=128))
        tlb = ck.dma("qp", wbb[:, :, :], W["wbase"][:, 0:128]
                     .rearrange("(kc kp) n -> kp kc n", kp=128))
        for oc in range(KC):
            ck.waits("pe", [tlu, tlb, ta_done])
            for tc in range(NT):
                tsl = slice(tc * 512, (tc + 1) * 512)
                bk = bufs.ps_big[:, (tc % 2) * 512:(tc % 2) * 512 + 512]
                ck.wait("pe", bufs.big_free)
                for kc in range(KC):
                    ck.pe.matmul(bk, wbb[:, kc, :], bufs.xn[:, kc, tsl],
                                 start=(kc == 0), stop=False)
                for kc in range(4):
                    ck.pe.matmul(bk, wau[:, kc, :], aab[:, kc, tsl],
                                 start=False, stop=(kc == 3))
                tp = ck.tok("pe")
                ck.waits("dve", [tp, tbup])
                ck.dve.scalar_tensor_tensor(
                    bufs.X[:, oc, tsl], bk, bup[:, oc:oc + 1],
                    bufs.X[:, oc, tsl], ALU.add, ALU.add)
                bufs.big_free = ck.tok("dve")
            tprev = ck.tok("pe")
            if oc < KC - 1:
                tlu = ck.dma("qp", wau[:, :, :],
                             W["waup"][:, (oc + 1) * 128:(oc + 2) * 128]
                             .rearrange("(kc kp) n -> kp kc n", kp=128),
                             after=[tprev])
                tlb = ck.dma("qp", wbb[:, :, :],
                             W["wbase"][:, (oc + 1) * 128:(oc + 2) * 128]
                             .rearrange("(kc kp) n -> kp kc n", kp=128),
                             after=[tprev])

    if "M" not in phases:
        return
    with (
        nc.sbuf_tensor("l0_win", [128, CI, 128], BF16) as winb,
        nc.sbuf_tensor("l0_win2", [128, CI, 128], BF16) as winb2,
        nc.sbuf_tensor("l0_xp", [128, CI, 2052], BF16) as xp,
    ):
        ck.join()
        inv_n = 1.0 / D
        for tc in range(NT):
            sl = slice(tc * 512, (tc + 1) * 512)
            for c in range(KC):
                tprev = ck.tok("pe") if c > 0 else None
                if tprev:
                    ck.wait("act", tprev)
                ck.act.activation(ln.sq[:, 0:512], bufs.xn[:, c, sl],
                                  AF.Square)
                tq = ck.tok("act")
                ck.waits("pe", [tq, bufs.ones_tok])
                ck.pe.matmul(ln.ps_q, bufs.ones_bf[:, :], ln.sq[:, 0:512],
                             start=(c == 0), stop=(c == KC - 1))
            tp = ck.tok("pe")
            ck.wait("dve", tp)
            ck.dve.tensor_scalar_mul(bufs.rowB[0:1, sl], ln.ps_q, inv_n)
            trow = ck.tok("dve")
            ck.wait("act", trow)
            ck.act.activation(bufs.rowB[0:1, sl], bufs.rowB[0:1, sl], AF.Ln,
                              bias=1e-6)
            ck.act.activation(bufs.rowB[0:1, sl], bufs.rowB[0:1, sl], AF.Exp,
                              scale=-0.5)
            ck.wait("pe", ck.tok("act"))
        ta = ck.tok("act")
        for c in range(KC):
            ck.dve.tensor_add(bufs.X[:, c, :], bufs.X[:, c, :],
                              bufs.xn[:, c, :])
        t_xadd = ck.tok("dve")
        t_xpage = ck.dma("qs", ck.scr_x, bufs.X[:, :, :], after=[t_xadd])
        ck.wait("sp", ta)
        rb = bcast_rows(ck, ck.scr_rows, [bufs.rowB[0:1, :]],
                        [bufs.sc1[:, :]])
        for c in range(KC):
            ck.waits("dve", rb)
            ck.dve.scalar_tensor_tensor(bufs.xn[:, c, :], bufs.xn[:, c, :],
                                        cols["l0_nw"][:, c:c + 1],
                                        bufs.sc1[:, :], ALU.mult, ALU.mult)
        txr = ck.tok("dve")
        xr = bufs.xn

        ck.wait("pool", txr)
        for c in range(CI):
            ck.pool.memset(xp[:, c, 0:3], 0.0)
        tpad = ck.tok("pool")
        wins = [winb, winb2]
        tw = {}
        cons = {}

        def load_blk(n):
            if n >= 32 or n in tw:
                return
            after = [cons[n - 2]] if (n - 2) in cons else []
            tw[n] = ck.dma("qp", wins[n % 2][:, 0:KC, :],
                           W["win"][:, n * 128:(n + 1) * 128]
                           .rearrange("(kc kp) n -> kp kc n", kp=128),
                           after=after)

        load_blk(0)
        load_blk(1)
        for n in range(32):
            b = n % 2
            ck.waits("pe", [tw[n], txr])
            for tc in range(NT):
                tsl = slice(tc * 512, (tc + 1) * 512)
                bk = bufs.ps_big[:, (tc % 2) * 512:(tc % 2) * 512 + 512]
                ck.wait("pe", bufs.big_free)
                for kc in range(KC):
                    ck.pe.matmul(bk, wins[b][:, kc, :], xr[:, kc, tsl],
                                 start=(kc == 0), stop=(kc == KC - 1))
                tp = ck.tok("pe")
                ck.wait("dve", tp)
                if n < CI:
                    ck.wait("dve", tpad)
                    ck.dve.tensor_copy(
                        xp[:, n, 3 + tc * 512:3 + (tc + 1) * 512], bk)
                    bufs.big_free = ck.tok("dve")
                else:
                    ck.dve.tensor_copy(bufs.scb1[:, 0:512], bk)
                    td = ck.tok("dve")
                    bufs.big_free = td
                    ck.dma("qs", ck.scr_z[(n - CI) * 128:(n - CI + 1) * 128,
                                          tsl],
                           bufs.scb1[:, 0:512], after=[td])
            cons[n] = ck.tok("pe")
            load_blk(n + 2)
        tz_done = ck.qtok("qs")

        for c in range(CI):
            for tc in range(NT - 1, -1, -1):
                t0 = tc * 512
                ck.dve.tensor_scalar(
                    bufs.sc2[:, 0:512], xp[:, c, t0:t0 + 512],
                    cols["l0_cw"][:, c * 4:c * 4 + 1], None, ALU.mult)
                for j in (1, 2, 3):
                    ck.dve.scalar_tensor_tensor(
                        bufs.sc2[:, 0:512], xp[:, c, t0 + j:t0 + j + 512],
                        cols["l0_cw"][:, c * 4 + j:c * 4 + j + 1],
                        bufs.sc2[:, 0:512], ALU.mult, ALU.add)
                td = ck.tok("dve")
                ck.wait("act", td)
                ck.act.activation(xp[:, c, 3 + t0:3 + t0 + 512],
                                  bufs.sc2[:, 0:512], AF.Silu,
                                  bias=cols["l0_cb"][:, c:c + 1])
                ck.wait("dve", ck.tok("act"))
        txc = ck.tok("act")
        xc = xp[:, :, 3:2051]

        if True:
            ck.wait("sp", t_xpage)
            ck.waits("pool", [t_xpage])
            wx = bufs.X[:, 6, 0:528].bitcast(BF16).rearrange(
                "p (a b) -> p a b", b=33)[:, 0:CI, :]
            ssm = bufs.X[0:33, 0, :]
            dtc = bufs.X[:, 1, :].bitcast(BF16)[:, 0:T]
            uc = bufs.X[:, 5, :].bitcast(BF16)[:, 0:T]
            negA = bufs.X[:, 2, 0:256]
            sredt = bufs.X[:, 3, :].bitcast(BF16)[:, 0:2048].rearrange(
                "p (a b) -> p a b", b=128)
            zt = bufs.X[:, 4, :].bitcast(BF16)[:, 0:T]
            twx = ck.dma("qp", wx[:, :, :],
                         W["wx"].rearrange("(kc kp) n -> kp kc n", kp=128))
            ck.waits("pe", [twx, txc])
            for tc in range(NT):
                tsl = slice(tc * 512, (tc + 1) * 512)
                ck.wait("pe", bufs.mid_free)
                for kc in range(CI):
                    ck.pe.matmul(bufs.ps_mid[0:33, 0:512], wx[:, kc, :],
                                 xc[:, kc, tsl],
                                 start=(kc == 0), stop=(kc == CI - 1))
                tp = ck.tok("pe")
                ck.wait("dve", tp)
                ck.dve.tensor_copy(ssm[:, tsl], bufs.ps_mid[0:33, 0:512])
                bufs.mid_free = ck.tok("dve")
            tssm = ck.tok("dve")
            ck.wait("sp", tssm)
            bdr = bcast_rows(ck, ck.scr_rows, [ssm[0:1, :]],
                             [bufs.sc1[:, :]])
            tupB = ck.dma("qs", ck.scr_bc[0:16, :], ssm[1:17, :],
                          after=[tssm])
            tupC = ck.dma("qs", ck.scr_bc[16:32, :], ssm[17:33, :],
                          after=[tssm])
            tB = ck.dma("qp", bufs.xn[:, 6, :],
                        ck.scr_bc[0:16, :].unsqueeze(1)
                        .broadcast_to([16, 8, T]), after=[tupB])
            tC = ck.dma("qp", bufs.xn[:, 7, :],
                        ck.scr_bc[16:32, :].unsqueeze(1)
                        .broadcast_to([16, 8, T]), after=[tupC])
            tA = ck.dma("qs", negA[:, :], W["ascan"])
            ck.wait("act", tA)
            ck.act.activation(negA[:, :], negA[:, :], AF.Exp)
            tae = ck.tok("act")
            ck.wait("dve", tae)
            ck.dve.tensor_scalar_mul(negA[:, :], negA[:, :], -1.0)
            tnegA = ck.tok("dve")

            for c in range(CI):
                csl = slice(c * 128, (c + 1) * 128)
                ck.waits("dve", bdr + [tssm])
                ck.dve.tensor_scalar(bufs.sc2[:, :], bufs.sc1[:, :],
                                     cols["l0_wdt"][:, c:c + 1],
                                     cols["l0_bdt"][:, c:c + 1],
                                     ALU.mult, ALU.add)
                td = ck.tok("dve")
                ck.wait("act", td)
                ck.act.activation(bufs.sc3[:, :], bufs.sc2[:, :], AF.Abs)
                ck.act.activation(bufs.sc3[:, :], bufs.sc3[:, :], AF.Exp,
                                  scale=-1.0)
                ck.act.activation(bufs.sc3[:, :], bufs.sc3[:, :], AF.Ln,
                                  bias=1.0)
                ta = ck.tok("act")
                ck.wait("dve", ta)
                ck.wait("dve", getattr(bufs, "l0_du_tok", None))
                ck.dve.scalar_tensor_tensor(dtc[:, :], bufs.sc2[:, :], 0.0,
                                            bufs.sc3[:, :], ALU.max, ALU.add)
                ck.wait("dve", txc)
                ck.dve.tensor_mul(uc[:, :], dtc[:, :], xc[:, c, :])
                tdu = ck.tok("dve")
                ck.dma("qs", ck.scr_dt[csl, :], dtc[:, :], after=[tdu])
                bufs.l0_du_tok = ck.dma("qs", ck.scr_u[csl, :], uc[:, :],
                                        after=[tdu])
            t_sc = ck.qtok("qs")

            ck.join()
            tsred = ck.dma("qs", sredt[:, :, :], W["sred"])
            rep_tok = {}

            def load_tile(kidx):
                if kidx >= 256 or kidx in rep_tok:
                    return
                par = kidx % 2
                ch0 = (kidx // 16) * 128 + (kidx % 16) * 8
                after = [rep_tok[kidx - 2][2]] if (kidx - 2) in rep_tok \
                    else []
                t1 = ck.dma("qs", bufs.xn[:, 0 + par, :],
                            ck.scr_dt[ch0:ch0 + 8, :].unsqueeze(0)
                            .broadcast_to([16, 8, T]), after=after)
                t2 = ck.dma("qs", bufs.xn[:, 2 + par, :],
                            ck.scr_u[ch0:ch0 + 8, :].unsqueeze(0)
                            .broadcast_to([16, 8, T]), after=after)
                rep_tok[kidx] = [t1, t2, None]

            load_tile(0)
            load_tile(1)
            y_evac = None
            for kc16 in range(CI):
                for j in range(16):
                    kidx = kc16 * 16 + j
                    par = kidx % 2
                    t1, t2, _ = rep_tok[kidx]
                    ck.waits("act", [t1, tnegA])
                    ck.act.activation(bufs.xn[:, par, :], bufs.xn[:, par, :],
                                      AF.Exp, scale=negA[:, kidx:kidx + 1])
                    ta = ck.tok("act")
                    ck.waits("dve", [t2, tB])
                    ck.dve.tensor_mul(bufs.xn[:, 2 + par, :],
                                      bufs.xn[:, 2 + par, :],
                                      bufs.xn[:, 6, :])
                    ck.wait("dve", ta)
                    ck.dve.tensor_tensor_scan(
                        bufs.xn[:, 4 + par, :], bufs.xn[:, par, :],
                        bufs.xn[:, 2 + par, :], 0.0, ALU.mult, ALU.add)
                    ck.wait("dve", tC)
                    ck.dve.tensor_mul(bufs.xn[:, 4 + par, :],
                                      bufs.xn[:, 4 + par, :],
                                      bufs.xn[:, 7, :])
                    ttmp = ck.tok("dve")
                    rep_tok[kidx][2] = ttmp
                    load_tile(kidx + 2)
                    ck.waits("pe", [ttmp, tsred])
                    if j == 0:
                        ck.wait("pe", y_evac)
                    for tc in range(NT):
                        ck.pe.matmul(
                            bufs.ps_big[:, tc * 512:(tc + 1) * 512],
                            sredt[:, j, :],
                            bufs.xn[:, 4 + par, tc * 512:(tc + 1) * 512],
                            start=(j == 0), stop=(j == 15))
                tp = ck.tok("pe")
                tzl = ck.dma("qs", zt[:, :],
                             ck.scr_z[kc16 * 128:(kc16 + 1) * 128, :],
                             after=[y_evac] if y_evac else [])
                ck.waits("act", [tzl, tz_done])
                ck.act.activation(bufs.scb1[:, :], zt[:, :], AF.Silu)
                tsz = ck.tok("act")
                ck.waits("dve", [tp, tsz])
                ck.dve.scalar_tensor_tensor(
                    bufs.sc2[:, :], xc[:, kc16, :],
                    cols["l0_D"][:, kc16:kc16 + 1], bufs.ps_big[:, :],
                    ALU.mult, ALU.add)
                ck.dve.tensor_mul(xc[:, kc16, :], bufs.sc2[:, :],
                                  bufs.scb1[:, :])
                y_evac = ck.tok("dve")

            ck.join()
            t_xr = ck.dma("qs", bufs.X[:, :, :], ck.scr_x)
            ck.wait("dve", t_xr)
            ck.wait("pe", t_xr)
            wouts = [winb, winb2]
            tws = {}
            wcons = {}

            def load_wout(ocn):
                if ocn >= KC or ocn in tws:
                    return
                after = [wcons[ocn - 2]] if (ocn - 2) in wcons else \
                    ([cons[31]] if ocn < 2 else [])
                tws[ocn] = ck.dma("qp", wouts[ocn % 2][:, :, :],
                                  W["wout0"][:, ocn * 128:(ocn + 1) * 128]
                                  .rearrange("(kc kp) n -> kp kc n", kp=128),
                                  after=after)

            load_wout(0)
            load_wout(1)
            for oc in range(KC):
                b = oc % 2
                ck.waits("pe", [tws[oc], y_evac])
                for tc in range(NT):
                    tsl = slice(tc * 512, (tc + 1) * 512)
                    bk = bufs.ps_big[:, (tc % 2) * 512:(tc % 2) * 512 + 512]
                    ck.wait("pe", bufs.big_free)
                    for kc in range(CI):
                        ck.pe.matmul(bk, wouts[b][:, kc, :], xc[:, kc, tsl],
                                     start=(kc == 0), stop=(kc == CI - 1))
                    tp = ck.tok("pe")
                    ck.wait("dve", tp)
                    ck.dve.tensor_add(bufs.X[:, oc, tsl],
                                      bufs.X[:, oc, tsl], bk)
                    bufs.big_free = ck.tok("dve")
                wcons[oc] = ck.tok("pe")
                load_wout(oc + 2)


# version 04
# speedup vs baseline: 1.0750x; 1.0686x over previous
import numpy as np
import ml_dtypes
import concourse.bass as bass
import concourse.mybir as mybir
from concourse.bass_utils import run_bass_kernel_spmd

F32 = mybir.dt.float32
BF16 = mybir.dt.bfloat16
AF = mybir.ActivationFunctionType
ALU = mybir.AluOpType
AX = mybir.AxisListType

T = 2048
D = 1024
KC = D // 128
NT = T // 512
RMAGIC = 12582912.0
KONST_VALUES = (1e-5, 1e-6, 1e-10, 1.0, 0.5)



_DVE_TENSOR_OPS = (
    "tensor_", "scalar_tensor_tensor", "reduce_", "reciprocal", "memset",
    "memzero", "select", "iota", "pool", "bn_")
_ACT_TENSOR_OPS = ("activation", "square", "sqrt", "mul", "copy", "add",
                   "sign", "memzero")


class _EngW:

    def __init__(self, ck, name, eng):
        object.__setattr__(self, "_ck", ck)
        object.__setattr__(self, "_name", name)
        object.__setattr__(self, "_eng", eng)

    def __getattr__(self, attr):
        fn = getattr(self._eng, attr)
        guard = ((self._name in ("dve", "pool")
                  and any(attr.startswith(p) for p in _DVE_TENSOR_OPS))
                 or (self._name == "act"
                     and any(attr.startswith(p) for p in _ACT_TENSOR_OPS)))

        def wrap(*a, **k):
            inst = fn(*a, **k)
            if inst is not None and hasattr(inst, "then_inc"):
                if guard and a and hasattr(a[0], "free_size"):
                    try:
                        small = a[0].free_size() <= 16
                    except Exception:
                        small = False
                    if small:
                        inst = self._eng.drain()
                self._ck._last[self._name] = inst
                self._ck._last_tok[self._name] = None
            return inst

        return wrap


class Ck:

    DMAQ = {"qs": "sp", "qa": "act", "qp": "pool"}

    def __init__(self, nc):
        self.nc = nc
        self.raw = dict(pe=nc.tensor, dve=nc.vector, act=nc.scalar,
                        pool=nc.gpsimd, sp=nc.sync)
        self.pe = _EngW(self, "pe", nc.tensor)
        self.dve = _EngW(self, "dve", nc.vector)
        self.act = _EngW(self, "act", nc.scalar)
        self.pool = _EngW(self, "pool", nc.gpsimd)
        self.sp = _EngW(self, "sp", nc.sync)
        self.sem, self.cnt, self._last, self._last_tok = {}, {}, {}, {}
        self.waited = {}
        for name in ("pe", "dve", "act", "pool", "sp", "qs", "qa", "qp"):
            self.sem[name] = nc.alloc_semaphore(name=f"ck_{name}")
            self.cnt[name] = 0
            self._last[name] = None
            self._last_tok[name] = None

    def tok(self, eng):
        if self._last[eng] is None:
            return (eng, self.cnt[eng])
        if self._last_tok[eng] is None:
            self.cnt[eng] += 1
            self._last[eng].then_inc(self.sem[eng], 1)
            self._last_tok[eng] = (eng, self.cnt[eng])
        return self._last_tok[eng]

    def wait(self, eng, tok):
        if tok is None:
            return
        src, val = tok
        if src == eng or val <= 0:
            return
        key = (eng, src)
        if self.waited.get(key, 0) >= val:
            return
        self.raw[eng].wait_ge(self.sem[src], val)
        self.waited[key] = val

    def waits(self, eng, toks):
        for t in toks:
            self.wait(eng, t)

    def dma(self, q, out, in_, after=(), **kw):
        iss = self.DMAQ[q]
        for t in after:
            self.wait(iss, t)
        inst = self.raw[iss].dma_start(out=out, in_=in_, **kw)
        self.cnt[q] += 16
        inst.then_inc(self.sem[q], 16)
        return (q, self.cnt[q])

    def vreduce(self, out, in_, axis, op, **kw):
        self.raw["dve"].tensor_reduce(out, in_, axis, op, **kw)
        inst = self.raw["dve"].drain()
        self._last["dve"] = inst
        self._last_tok["dve"] = None
        return inst

    def recip(self, out, in_):
        self.raw["dve"].drain()
        self.raw["dve"].reciprocal(out, in_)
        inst = self.raw["dve"].drain()
        self._last["dve"] = inst
        self._last_tok["dve"] = None
        return inst

    def qtok(self, q):
        return (q, self.cnt[q])

    def join(self):
        engines = ("pe", "dve", "act", "pool", "sp")
        toks = [self.tok(e) for e in engines]
        toks += [self.qtok("qs"), self.qtok("qa"), self.qtok("qp")]
        for e in engines:
            for t in toks:
                self.wait(e, t)

    def finish(self, out_toks):
        self.join()
        for t in out_toks:
            self.wait("sp", t)
            self.wait("pool", t)
        for name in self.sem:
            self.raw["pool"].sem_clear(self.sem[name])




class WStream:

    def __init__(self, ck, name, ap, K, N, nb=128, depth=2):
        nc = ck.nc
        self.ck, self.ap, self.K, self.N, self.nb = ck, ap, K, N, nb
        assert K % 128 == 0 and N % nb == 0
        self.kc = K // 128
        self.nblk = N // nb
        self.depth = min(depth, self.nblk)
        self._ctx, self._bufs = [], []
        for i in range(self.depth):
            g = nc.sbuf_tensor(f"{name}_b{i}", [128, self.kc, nb], BF16)
            self._bufs.append(g.__enter__())
            self._ctx.append(g)
        self._load_tok = {}
        self._free_tok = [None] * self.depth
        self._epoch = 0

    def close(self):
        for g in reversed(self._ctx):
            g.__exit__(None, None, None)

    def reset(self):
        self._epoch += 1

    def _issue(self, j):
        if j >= self.nblk or (self._epoch, j) in self._load_tok:
            return
        b = j % self.depth
        src = self.ap[:, j * self.nb:(j + 1) * self.nb].rearrange(
            "(kc kp) n -> kp kc n", kp=128)
        after = [self._free_tok[b]] if self._free_tok[b] else []
        self._load_tok[(self._epoch, j)] = self.ck.dma(
            "qp", self._bufs[b][:, :, :], src, after=after)

    def start(self):
        for j in range(self.depth):
            self._issue(j)

    def get(self, j, eng="pe"):
        self._issue(j)
        self.ck.wait(eng, self._load_tok[(self._epoch, j)])
        return self._bufs[j % self.depth]

    def done(self, j, eng="pe"):
        self._free_tok[j % self.depth] = self.ck.tok(eng)
        self._issue(j + self.depth)




def bcast_rows(ck, scr_rows, row_aps, out_tiles, width=T):
    toks = []
    ups = []
    for i, r in enumerate(row_aps):
        ups.append(ck.dma("qs", scr_rows[i:i + 1, 0:width], r))
    for i, o in enumerate(out_tiles):
        src = scr_rows[i:i + 1, 0:width].partition_broadcast(128)
        toks.append(ck.dma("qs", o.unsqueeze(1), src, after=ups))
    return toks


class LNParams:

    def __init__(self, nc, ck, bufs):
        self.ck = ck
        self._ctx = []

        def sb(name, shape, dt=F32):
            g = nc.sbuf_tensor(name, shape, dt)
            self._ctx.append(g)
            return g.__enter__()

        self.bufs = bufs
        self.ones_col = bufs.ones_bf
        self.sq = bufs.scb1
        self.b1 = bufs.sc1
        self.b2 = bufs.sc2
        self.tmp = bufs.sc3
        self.ps_s = bufs.ps_row[0:1, 0:512]
        self.ps_q = bufs.ps_row[0:1, 512:1024]
        self.init_tok = bufs.ones_tok


def emit_ln(ck, ln, src_bf, outs, eps, gamma_col=None, beta_col=None,
            extra_bf=None, n_chunks=KC, src_tok=None):
    inv_n = 1.0 / (128 * n_chunks)
    last_read = None
    for tc in range(NT):
        sl = slice(tc * 512, (tc + 1) * 512)
        for c in range(n_chunks):
            ck.wait("act", ln.init_tok)
            if src_tok is not None:
                ck.wait("act", src_tok)
                ck.wait("pe", src_tok)
            tprev = ck.tok("pe") if c > 0 else last_read
            if tprev is not None:
                ck.wait("act", tprev)
            ck.act.activation(ln.sq[:, 0:512], src_bf[c][:, sl], AF.Square)
            tq = ck.tok("act")
            ck.wait("pe", ln.init_tok)
            if last_read is not None:
                ck.wait("pe", last_read)
            ck.pe.matmul(ln.ps_s, ln.ones_col[:, :], src_bf[c][:, sl],
                         start=(c == 0), stop=(c == n_chunks - 1))
            ck.wait("pe", tq)
            ck.pe.matmul(ln.ps_q, ln.ones_col[:, :], ln.sq[:, 0:512],
                         start=(c == 0), stop=(c == n_chunks - 1))
        tp = ck.tok("pe")
        ck.wait("dve", tp)
        rA, rB = ck.nc, None
        bufs = ln.bufs
        ck.dve.tensor_scalar_mul(bufs.rowA[0:1, sl], ln.ps_s, inv_n)
        ck.dve.tensor_scalar_mul(bufs.rowB[0:1, sl], ln.ps_q, inv_n)
        last_read = ck.tok("dve")
        ck.wait("act", last_read)
        ck.act.activation(bufs.sc3[0:1, sl], bufs.rowA[0:1, sl], AF.Square)
        ta = ck.tok("act")
        ck.wait("dve", ta)
        ck.dve.tensor_sub(bufs.rowB[0:1, sl], bufs.rowB[0:1, sl],
                          bufs.sc3[0:1, sl])
        td = ck.tok("dve")
        ck.wait("act", td)
        ck.act.activation(bufs.rowB[0:1, sl], bufs.rowB[0:1, sl], AF.Ln,
                          bias=float(eps))
        ck.act.activation(bufs.rowB[0:1, sl], bufs.rowB[0:1, sl], AF.Exp,
                          scale=-0.5)
        ta = ck.tok("act")
        ck.wait("dve", ta)
        ck.dve.tensor_mul(bufs.rowA[0:1, sl], bufs.rowA[0:1, sl],
                          bufs.rowB[0:1, sl])
        ck.dve.tensor_scalar_mul(bufs.rowA[0:1, sl], bufs.rowA[0:1, sl],
                                 -1.0)
    trows = ck.tok("dve")
    ck.wait("sp", trows)
    bufs = ln.bufs
    bts = bcast_rows(ck, ck.scr_rows, [bufs.rowB[0:1, :], bufs.rowA[0:1, :]],
                     [ln.b1[:, :], ln.b2[:, :]])
    out_toks = []
    for c in range(n_chunks):
        ck.waits("dve", bts)
        if gamma_col is None:
            ck.dve.tensor_mul(ln.tmp[:, :], src_bf[c][:, :], ln.b1[:, :])
            ck.dve.tensor_add(outs[c], ln.tmp[:, :], ln.b2[:, :])
        else:
            ck.dve.tensor_mul(ln.tmp[:, :], src_bf[c][:, :], ln.b1[:, :])
            ck.dve.tensor_add(ln.tmp[:, :], ln.tmp[:, :], ln.b2[:, :])
            ck.dve.tensor_scalar(outs[c], ln.tmp[:, :], gamma_col[c],
                                 beta_col[c], ALU.mult, ALU.add)
        if extra_bf is not None:
            ck.dve.tensor_copy(extra_bf[c], outs[c])
    return ck.tok("dve")




class Bufs:

    def __init__(self, nc, ck):
        self.nc, self.ck = nc, ck
        self._ctx = []

        def sb(name, shape, dt=F32):
            g = nc.sbuf_tensor(name, shape, dt)
            self._ctx.append(g)
            return g.__enter__()

        def psum(name, shape, dt=F32):
            g = nc.psum_tensor(name, shape, dt)
            self._ctx.append(g)
            return g.__enter__()

        self.X = sb("Xres", [128, KC, T])
        self.xn = sb("xn_bf", [128, KC, T], BF16)
        self.sc1 = sb("sc1", [128, T])
        self.sc2 = sb("sc2", [128, T])
        self.sc3 = sb("sc3", [128, T])
        self.scb1 = sb("scb1", [128, T], BF16)
        self.scb2 = sb("scb2", [128, T], BF16)
        self.colA = sb("colA", [128, 64])
        self.colB = sb("colB", [128, 64])
        self.rowA = self.sc1
        self.rowB = self.sc2
        self.id_bf = sb("id_bf", [128, 128], BF16)
        self.id_f32 = sb("id_f32", [128, 128])
        self.ps_big = psum("ps_big", [128, 2048])
        self.ps_mid = psum("ps_mid", [128, 512])
        self.ps_row = psum("ps_row", [1, 1024])
        self.ps_trb = psum("ps_trb", [128, 1024], BF16)
        self.colA2 = sb("colA2", [16, 128])
        self.ones_bf = sb("ones_bf", [128, 1], BF16)
        self.ones_f32 = sb("ones_f32", [128, 1])
        ck.pool.memset(self.ones_bf[:, :], 1.0)
        ck.pool.memset(self.ones_f32[:, :], 1.0)
        self.konst = sb("konst", [128, 8])
        for i, v in enumerate(KONST_VALUES):
            ck.pool.memset(self.konst[:, i:i + 1], float(v))
            nc.const_aps.register(F32, float(v), self.konst[:, i:i + 1])
        self.ones_tok = ck.tok("pool")
        self.id_tok = None
        self.trb_free = None
        self.mid_free = None
        self.big_free = None
        self.wo_free = None
        self.we_free = [None, None]



def col_loader(ck, nc, name, dram_ap, ncols):
    g = nc.sbuf_tensor(name, [128, ncols], F32)
    h = g.__enter__()
    tok = ck.dma("qs", h[:, :], dram_ap)
    return h, tok, g


class ScrStream:

    def __init__(self, ck, name, scr_ap, kc, N, nb=128, depth=2):
        nc = ck.nc
        self.ck, self.scr, self.kc, self.N, self.nb = ck, scr_ap, kc, N, nb
        self.nblk = N // nb
        self.depth = min(depth, self.nblk)
        self._ctx, self._bufs = [], []
        for i in range(self.depth):
            g = nc.sbuf_tensor(f"{name}_b{i}", [128, kc, nb], BF16)
            self._bufs.append(g.__enter__())
            self._ctx.append(g)
        self._load_tok = {}
        self._free_tok = [None] * self.depth
        self._epoch = 0

    def close(self):
        for g in reversed(self._ctx):
            g.__exit__(None, None, None)

    def reset(self):
        self._epoch += 1

    def _key(self, j):
        return (self._epoch, j)

    def _issue(self, j, ready_tok=None):
        if j >= self.nblk or self._key(j) in self._load_tok:
            return
        b = j % self.depth
        src = self.scr[:, :, j * self.nb:(j + 1) * self.nb]
        after = [t for t in (self._free_tok[b], ready_tok) if t]
        self._load_tok[self._key(j)] = self.ck.dma(
            "qs", self._bufs[b][:, :, :], src, after=after)

    def start(self, ready_tok=None):
        for j in range(self.depth):
            self._issue(j, ready_tok)

    def get(self, j, eng="pe"):
        self._issue(j)
        self.ck.wait(eng, self._load_tok[self._key(j)])
        return self._bufs[j % self.depth]

    def done(self, j, eng="pe"):
        self._free_tok[j % self.depth] = self.ck.tok(eng)
        self._issue(j + self.depth)


def quantize_bitlinear_weight(ck, bufs, w_ap, K, N, scr_ap, name):
    nc = ck.nc
    kc = K // 128
    nb = 2048 // kc
    nblk = N // nb
    ctx, fbufs, qbufs = [], [], []
    for i in range(2):
        g = nc.sbuf_tensor(f"{name}_f{i}", [128, kc, nb], F32)
        fbufs.append(g.__enter__())
        ctx.append(g)
        g = nc.sbuf_tensor(f"{name}_q{i}", [128, kc, nb], BF16)
        qbufs.append(g.__enter__())
        ctx.append(g)
    load_tok = [None] * nblk
    free_tok = [None, None]

    def issue(j, extra=()):
        if j >= nblk or load_tok[j] is not None:
            return
        b = j % 2
        src = w_ap[:, j * nb:(j + 1) * nb].rearrange(
            "(kc kp) n -> kp kc n", kp=128)
        aft = [t for t in (free_tok[b], *extra) if t]
        load_tok[j] = ck.dma("qs", fbufs[b][:, :, :], src, after=aft)

    tpe0 = ck.tok("pe")
    td0 = ck.tok("dve")
    ck.wait("pool", bufs.ones_tok)
    ck.wait("pool", tpe0)
    ck.wait("pool", td0)
    ck.wait("act", td0)
    ck.pool.memset(bufs.colA[:, 0:1], 0.0)
    tz = ck.tok("pool")
    issue(0)
    issue(1)
    for j in range(nblk):
        b = j % 2
        ck.wait("act", load_tok[j])
        ck.act.activation(fbufs[b][:, :, :], fbufs[b][:, :, :], AF.Abs,
                          accum_out=bufs.colA[:, 1:2])
        ta = ck.tok("act")
        ck.waits("dve", [ta, tz])
        ck.dve.tensor_add(bufs.colA[:, 0:1], bufs.colA[:, 0:1],
                          bufs.colA[:, 1:2])
        free_tok[b] = ck.tok("dve")
        load_tok[j] = None
        issue(j + 2)
    td = ck.tok("dve")
    ck.wait("pe", td)
    ck.pe.matmul(bufs.ps_row[0:1, 0:1], bufs.ones_f32[:, :],
                 bufs.colA[:, 0:1], start=True, stop=True)
    tp = ck.tok("pe")
    ck.wait("dve", tp)
    ck.wait("dve", ck.qtok("qs"))
    ck.dve.tensor_scalar(bufs.rowA[0:1, 0:1], bufs.ps_row[0:1, 0:1],
                         1.0 / (K * N), 1e-5, ALU.mult, ALU.max)
    ck.recip(bufs.rowA[0:1, 1:2], bufs.rowA[0:1, 0:1])
    trow = ck.tok("dve")
    t_up = ck.dma("qs", ck.scr_rows[0:1, 0:2], bufs.rowA[0:1, 0:2],
                  after=[trow])
    t_mn = ck.dma("qs", bufs.colA[:, 4:5].unsqueeze(1),
                  ck.scr_rows[0:1, 0:1].partition_broadcast(128),
                  after=[t_up])
    t_s = ck.dma("qs", bufs.colA[:, 5:6].unsqueeze(1),
                 ck.scr_rows[0:1, 1:2].partition_broadcast(128),
                 after=[t_up])
    out_toks = []
    for j in range(nblk):
        b = j % 2
        issue(j)
        ck.wait("dve", load_tok[j])
        ck.waits("dve", [t_mn, t_s])
        f = fbufs[b][:, :, :]
        q = qbufs[b][:, :, :]
        if j >= 2:
            ck.wait("dve", out_toks[j - 2])
        ck.dve.tensor_scalar(f, f, bufs.colA[:, 5:6], RMAGIC,
                             ALU.mult, ALU.add)
        ck.dve.tensor_scalar(f, f, RMAGIC, -1.0, ALU.subtract, ALU.max)
        ck.dve.tensor_scalar(q, f, 1.0, bufs.colA[:, 4:5],
                             ALU.min, ALU.mult)
        tq = ck.tok("dve")
        out_toks.append(ck.dma("qs", scr_ap[:, :, j * nb:(j + 1) * nb], q,
                               after=[tq]))
        free_tok[b] = out_toks[-1]
        load_tok[j] = None
        issue(j + 1, extra=(tq,))
    ck.waits("sp", out_toks)
    for g in reversed(ctx):
        g.__exit__(None, None, None)
    return out_toks





def act_quant(ck, bufs, src, out_q, s_b, sinv_b, src_tok=None):
    nch = len(src)
    w = src[0].shape[1]
    ntt = w // 128
    mx = bufs.colB
    for tt in range(ntt):
        tsl = slice(tt * 128, (tt + 1) * 128)
        for c in range(nch):
            ck.wait("pe", bufs.id_tok)
            if src_tok is not None:
                ck.wait("pe", src_tok)
            if bufs.trb_free is not None:
                ck.wait("pe", bufs.trb_free)
            pst = bufs.ps_trb[:, (c % 4) * 128:(c % 4) * 128 + 128]
            ck.pe.transpose(pst, src[c][:, tsl], bufs.id_bf[:, :])
            tp = ck.tok("pe")
            ck.wait("dve", tp)
            sc = 24 + (c % 2)
            ck.vreduce(mx[:, sc:sc + 1], pst, AX.X, ALU.max,
                       apply_absolute_value=True)
            if c == 0:
                ck.dve.tensor_copy(mx[:, tt:tt + 1], mx[:, sc:sc + 1])
            else:
                ck.dve.tensor_max(mx[:, tt:tt + 1], mx[:, tt:tt + 1],
                                  mx[:, sc:sc + 1])
            bufs.trb_free = ck.tok("dve")
    ck.dve.tensor_scalar_max(mx[:, 0:ntt], mx[:, 0:ntt], 1e-5)
    ck.dve.tensor_scalar_mul(mx[:, 8 + ntt:8 + 2 * ntt], mx[:, 0:ntt],
                             1.0 / 127.0)
    ck.recip(mx[:, 8:8 + ntt], mx[:, 0:ntt])
    ck.dve.tensor_scalar_mul(mx[:, 8:8 + ntt], mx[:, 8:8 + ntt], 127.0)
    tcols = ck.tok("dve")
    ck.waits("pe", [tcols, bufs.id_tok, bufs.mid_free])
    ck.pe.transpose(bufs.ps_mid[0:2 * ntt, 0:128], mx[:, 8:8 + 2 * ntt],
                    bufs.id_f32[:, :])
    tp = ck.tok("pe")
    ck.wait("dve", tp)
    ck.dve.tensor_copy(bufs.colA2[0:2 * ntt, :],
                       bufs.ps_mid[0:2 * ntt, 0:128])
    trows = ck.tok("dve")
    bufs.mid_free = trows
    if getattr(bufs, "dbg_aq", None) is not None:
        d1, d2 = bufs.dbg_aq
        ck.dma("qs", d1, bufs.colB[:, 0:32], after=[trows])
        ck.dma("qs", d2, bufs.colA2[:, :], after=[trows])
        bufs.dbg_aq = None
    dns = []
    for r in range(ntt):
        u1 = ck.dma("qs", ck.scr_rows[4 + r:5 + r, 0:128],
                    bufs.colA2[r:r + 1, :], after=[trows])
        u2 = ck.dma("qs", ck.scr_rows[12 + r:13 + r, 0:128],
                    bufs.colA2[ntt + r:ntt + r + 1, :], after=[trows])
        dns.append(ck.dma("qs", s_b[:, r * 128:(r + 1) * 128].unsqueeze(1),
                          ck.scr_rows[4 + r:5 + r, 0:128]
                          .partition_broadcast(128), after=[u1]))
        dns.append(ck.dma("qs", sinv_b[:, r * 128:(r + 1) * 128].unsqueeze(1),
                          ck.scr_rows[12 + r:13 + r, 0:128]
                          .partition_broadcast(128), after=[u2]))
    for c in range(nch):
        ck.waits("dve", dns)
        ck.dve.tensor_mul(bufs.sc3[:, 0:w], src[c], s_b[:, 0:w])
        ck.dve.tensor_scalar(bufs.sc3[:, 0:w], bufs.sc3[:, 0:w], RMAGIC,
                             RMAGIC, ALU.add, ALU.subtract)
        ck.dve.tensor_mul(out_q[c], bufs.sc3[:, 0:w], sinv_b[:, 0:w])
    return ck.tok("dve")




def emit_l1(ck, bufs, ln, W, cols, phases="ABC"):
    nc = ck.nc
    scr_g, scr_u, scr_d = W["scr_g"], W["scr_u"], W["scr_d"]
    if "A" in phases:
        quantize_bitlinear_weight(ck, bufs, W["wg"], D, 4096, scr_g, "l1wg")
        quantize_bitlinear_weight(ck, bufs, W["wu"], D, 4096, scr_u, "l1wu")
        quantize_bitlinear_weight(ck, bufs, W["wd"], 4096, D, scr_d, "l1wd")

    import os as _os
    _dbgA = _os.environ.get("KDBG_L1A")
    if _dbgA:
        bufs.dbg_aq = (
            nc.dram_tensor("d_colB", [128, 32], F32,
                           kind="ExternalOutput").ap(),
            nc.dram_tensor("d_colA2", [16, 128], F32,
                           kind="ExternalOutput").ap())
        d_xq = nc.dram_tensor("d_xq", [128, KC, 512], BF16,
                              kind="ExternalOutput").ap()
        d_gu = nc.dram_tensor("d_gu", [128, 32, 512], BF16,
                              kind="ExternalOutput").ap()
        d_sb = nc.dram_tensor("d_sb", [128, 512], F32,
                              kind="ExternalOutput").ap()
        d_sib = nc.dram_tensor("d_sib", [128, 512], F32,
                               kind="ExternalOutput").ap()
    if "A" in phases:
        with (
          nc.sbuf_tensor("l1_xq", [128, KC, 512], BF16) as xq,
          nc.sbuf_tensor("l1_gu", [128, 32, 512], BF16) as gu,
          nc.sbuf_tensor("l1_sb", [128, 512], F32) as s_b,
          nc.sbuf_tensor("l1_sib", [128, 512], F32) as sinv_b,
      ):
          sg = ScrStream(ck, "l1sg", scr_g, KC, 4096, nb=128)
          su = ScrStream(ck, "l1su", scr_u, KC, 4096, nb=128)
          sd = ScrStream(ck, "l1sd", scr_d, 32, D, nb=128)
          for tc in range(NT):
              tsl = slice(tc * 512, (tc + 1) * 512)
              tq = act_quant(ck, bufs, [bufs.xn[:, c, tsl] for c in range(KC)],
                             [xq[:, c, :] for c in range(KC)], s_b[:, :],
                             sinv_b[:, :])
              sg.reset(); su.reset(); sd.reset()
              sg.start(); su.start()
              gu_toks = {}
              for n in range(32):
                  gt = sg.get(n)
                  ut = su.get(n)
                  bk_g = bufs.ps_big[:, (n % 2) * 512:(n % 2) * 512 + 512]
                  bk_u = bufs.ps_big[:, 1024 + (n % 2) * 512:1536 + (n % 2) * 512]
                  if n >= 2:
                      ck.wait("pe", gu_toks[n - 2])
                  ck.wait("pe", tq)
                  for kc in range(KC):
                      ck.pe.matmul(bk_g, gt[:, kc, :], xq[:, kc, :],
                                   start=(kc == 0), stop=(kc == KC - 1))
                  sg.done(n)
                  for kc in range(KC):
                      ck.pe.matmul(bk_u, ut[:, kc, :], xq[:, kc, :],
                                   start=(kc == 0), stop=(kc == KC - 1))
                  su.done(n)
                  tp = ck.tok("pe")
                  ck.wait("act", tp)
                  ck.act.activation(bufs.scb2[:, 0:512], bk_g, AF.Silu)
                  ta = ck.tok("act")
                  ck.waits("dve", [tp, ta])
                  ck.dve.tensor_mul(gu[:, n, :], bufs.scb2[:, 0:512], bk_u)
                  gu_toks[n] = ck.tok("dve")
              if _dbgA and tc == 0:
                  ck.waits("sp", [tq, gu_toks[31]])
                  ck.dma("qs", d_xq, xq[:, :, :])
                  ck.dma("qs", d_gu, gu[:, :, :])
                  ck.dma("qs", d_sb, s_b[:, :])
                  ck.dma("qs", d_sib, sinv_b[:, :])
                  ck.join()
              tq2 = act_quant(ck, bufs, [gu[:, n, :] for n in range(32)],
                              [gu[:, n, :] for n in range(32)], s_b[:, :],
                              sinv_b[:, :], src_tok=gu_toks[31])
              xtoks = {}
              for oc in range(KC):
                  dt_ = sd.get(oc)
                  bk = bufs.ps_big[:, (oc % 4) * 512:(oc % 4) * 512 + 512]
                  if oc >= 4:
                      ck.wait("pe", xtoks[oc - 4])
                  ck.wait("pe", tq2)
                  for kc in range(32):
                      ck.pe.matmul(bk, dt_[:, kc, :], gu[:, kc, :],
                                   start=(kc == 0), stop=(kc == 31))
                  sd.done(oc)
                  tp = ck.tok("pe")
                  ck.wait("dve", tp)
                  ck.dve.tensor_add(bufs.X[:, oc, tsl], bufs.X[:, oc, tsl], bk)
                  xtoks[oc] = ck.tok("dve")
              ck.wait("pe", xtoks[KC - 1])
          sd.close(); su.close(); sg.close()

    if "B" in phases:
      with (
          nc.sbuf_tensor("l1_x2", [128, KC, T], BF16) as x2,
          nc.sbuf_tensor("l1_wh00", [128, KC, 128], BF16) as wh00,
          nc.sbuf_tensor("l1_wh01", [128, KC, 128], BF16) as wh01,
          nc.sbuf_tensor("l1_wh02", [128, KC, 128], BF16) as wh02,
          nc.sbuf_tensor("l1_wh10", [128, KC, 128], BF16) as wh10,
          nc.sbuf_tensor("l1_wh11", [128, KC, 128], BF16) as wh11,
          nc.sbuf_tensor("l1_wh12", [128, KC, 128], BF16) as wh12,
          nc.sbuf_tensor("l1_kh", [128, 16, 128], BF16) as kh,
          nc.sbuf_tensor("l1_vh", [128, 16, 128], BF16) as vh,
          nc.sbuf_tensor("l1_Qh", [128, T], BF16) as Qh,
          nc.sbuf_tensor("l1_kvh", [128, 128], BF16) as kvh,
          nc.sbuf_tensor("l1_ksc", [128, 1], BF16) as ksc,
      ):
          whb = [[wh00, wh01, wh02], [wh10, wh11, wh12]]
          wh_load = {}
          wh_free = [None, None]

          def load_head(h):
              if h >= KC or h in wh_load:
                  return
              b = whb[h % 2]
              after = [wh_free[h % 2]] if wh_free[h % 2] else []
              toks = []
              for i, wn in enumerate(("wq", "wk", "wv")):
                  hs = slice(h * 128, (h + 1) * 128)
                  srcap = W[wn][:, hs].rearrange("(kc kp) n -> kp kc n", kp=128)
                  toks.append(ck.dma("qp", b[i][:, :, :], srcap, after=after))
              wh_load[h] = toks

          ck.join()
          load_head(0)
          load_head(1)
          tln = emit_ln(ck, ln, [bufs.xn[:, c, :] for c in range(KC)],
                        [x2[:, c, :] for c in range(KC)], 1e-5,
                        gamma_col=[cols["l1_g1"][:, c:c + 1] for c in range(KC)],
                        beta_col=[cols["l1_be1"][:, c:c + 1] for c in range(KC)])

          def elu1_evac(ps, out_bf, w):
              ck.dve.tensor_scalar_min(bufs.scb2[:, 0:w], ps, 0.0)
              td = ck.tok("dve")
              ck.wait("act", td)
              ck.act.activation(bufs.scb1[:, 0:w], bufs.scb2[:, 0:w], AF.Exp)
              ta = ck.tok("act")
              ck.wait("dve", ta)
              ck.dve.scalar_tensor_tensor(out_bf, ps, 0.0, bufs.scb1[:, 0:w],
                                          ALU.max, ALU.add)
              return ck.tok("dve")

          _dbgH = _os.environ.get("KDBG_L1H")
          if _dbgH:
              d_kv = nc.dram_tensor("d_kv", [2, 128, 128], BF16,
                                    kind="ExternalOutput").ap()
              d_ks = nc.dram_tensor("d_ks", [2, 128, 1], BF16,
                                    kind="ExternalOutput").ap()
              d_q = nc.dram_tensor("d_q", [2, 128, T], BF16,
                                   kind="ExternalOutput").ap()
              d_z = nc.dram_tensor("d_z", [2, 1, T], F32,
                                   kind="ExternalOutput").ap()
              d_kh = nc.dram_tensor("d_kh", [2, 128, 16, 128], BF16,
                                    kind="ExternalOutput").ap()
              d_kps = nc.dram_tensor("d_kps", [128, 128], F32,
                                     kind="ExternalOutput").ap()
              d_whk = nc.dram_tensor("d_whk", [128, KC, 128], BF16,
                                     kind="ExternalOutput").ap()
              d_x2 = nc.dram_tensor("d_x2", [128, KC, T], BF16,
                                    kind="ExternalOutput").ap()
          prev_zb = []
          for h in range(KC):
              wh = whb[h % 2]
              ck.waits("pe", wh_load[h])
              ck.wait("pe", tln)
              ck.wait("pe", bufs.mid_free)
              for tt in range(16):
                  ttsl = slice(tt * 128, (tt + 1) * 128)
                  bk = bufs.ps_mid
                  for kc in range(KC):
                      ck.pe.matmul(bk[:, 0:128], x2[:, kc, ttsl],
                                   wh[1][:, kc, :],
                                   start=(kc == 0), stop=(kc == KC - 1))
                  for kc in range(KC):
                      ck.pe.matmul(bk[:, 128:256], x2[:, kc, ttsl],
                                   wh[2][:, kc, :],
                                   start=(kc == 0), stop=(kc == KC - 1))
                  tp = ck.tok("pe")
                  ck.wait("dve", tp)
                  if _dbgH and h == 0 and tt == 0:
                      ck.dve.tensor_copy(bufs.sc2[:, 0:128], bk[:, 0:128])
                      _td = ck.tok("dve")
                      _t6 = ck.dma("qs", d_kps, bufs.sc2[:, 0:128], after=[_td])
                      _t7 = ck.dma("qs", d_x2, x2[:, :, :], after=[_td])
                      _t8 = ck.dma("qs", d_whk, wh[1][:, :, :], after=[_td])
                      ck.waits("dve", [_t6, _t7, _t8])
                  t1 = elu1_evac(bk[:, 0:128], kh[:, tt, :], 128)
                  ck.wait("act", tp)
                  ck.act.activation(vh[:, tt, :], bk[:, 128:256], AF.Copy,
                                    scale=1.0 / T)
                  tv = ck.tok("act")
                  ck.waits("pe", [t1, tv])
              ck.wait("pe", bufs.mid_free)
              for tt in range(16):
                  ck.pe.matmul(bufs.ps_mid[:, 0:128], kh[:, tt, :],
                               vh[:, tt, :], start=(tt == 0), stop=(tt == 15))
              ck.wait("pe", bufs.ones_tok)
              for tt in range(16):
                  ck.pe.matmul(bufs.ps_mid[:, 256:257], kh[:, tt, :],
                               bufs.ones_bf[:, :], start=(tt == 0),
                               stop=(tt == 15))
              tp = ck.tok("pe")
              ck.wait("dve", tp)
              ck.dve.tensor_copy(kvh[:, :], bufs.ps_mid[:, 0:128])
              ck.dve.tensor_copy(ksc[:, :], bufs.ps_mid[:, 256:257])
              tks = ck.tok("dve")
              bufs.mid_free = tks
              if _dbgH and h < 2:
                  _t1 = ck.dma("qs", d_kv[h], kvh[:, :], after=[tks])
                  _t2 = ck.dma("qs", d_ks[h], ksc[:, :], after=[tks])
                  _t3 = ck.dma("qs", d_kh[h], kh[:, :, :], after=[tks])
                  ck.waits("pe", [_t1, _t2, _t3])
                  ck.waits("dve", [_t1, _t2, _t3])
              for tc2 in range(NT):
                  tsl = slice(tc2 * 512, (tc2 + 1) * 512)
                  bk = bufs.ps_big[:, (tc2 % 2) * 512:(tc2 % 2) * 512 + 512]
                  for kc in range(KC):
                      ck.pe.matmul(bk, wh[0][:, kc, :], x2[:, kc, tsl],
                                   start=(kc == 0), stop=(kc == KC - 1))
                  tp = ck.tok("pe")
                  ck.wait("dve", tp)
                  tqe = elu1_evac(bk, Qh[:, tsl], 512)
                  ck.wait("pe", tqe)
              wh_free[h % 2] = ck.tok("pe")
              load_head(h + 1)
              ck.waits("dve", prev_zb)
              for tc2 in range(NT):
                  tsl = slice(tc2 * 512, (tc2 + 1) * 512)
                  ck.wait("pe", tks)
                  ck.pe.matmul(bufs.ps_row[0:1, 512:1024], ksc[:, :],
                               Qh[:, tsl], start=True, stop=True)
                  tp = ck.tok("pe")
                  ck.wait("dve", tp)
                  ck.dve.tensor_scalar_add(bufs.rowB[0:1, tsl],
                                           bufs.ps_row[0:1, 512:1024], 1e-6)
                  ck.wait("pe", ck.tok("dve"))
              ck.recip(bufs.rowB[0:1, :], bufs.rowB[0:1, :])
              ck.dve.tensor_scalar_mul(bufs.rowB[0:1, :], bufs.rowB[0:1, :],
                                       float(T))
              tz = ck.tok("dve")
              ck.wait("sp", tz)
              if _dbgH and h < 2:
                  _t4 = ck.dma("qs", d_q[h], Qh[:, :], after=[tz])
                  _t5 = ck.dma("qs", d_z[h], bufs.rowB[0:1, :], after=[tz])
                  ck.waits("pe", [_t4, _t5])
                  ck.waits("dve", [_t4, _t5])
              zb = bcast_rows(ck, ck.scr_rows, [bufs.rowB[0:1, :]],
                              [bufs.sc3[:, :]])
              prev_zb = zb
              for tc2 in range(NT):
                  tsl = slice(tc2 * 512, (tc2 + 1) * 512)
                  bk = bufs.ps_big[:, 1024 + (tc2 % 2) * 512:1536 + (tc2 % 2) * 512]
                  ck.pe.matmul(bk, kvh[:, :], Qh[:, tsl], start=True, stop=True)
                  tp = ck.tok("pe")
                  ck.waits("dve", zb + [tp])
                  ck.dve.tensor_mul(bufs.sc2[:, 0:512], bk, bufs.sc3[:, tsl])
                  ck.dve.tensor_add(bufs.xn[:, h, tsl], bufs.sc2[:, 0:512],
                                    bufs.xn[:, h, tsl])
                  ck.wait("pe", ck.tok("dve"))


    if _os.environ.get("KDBG_L1B"):
        d_xl1 = nc.dram_tensor("d_xl1", [128, KC, T], BF16,
                               kind="ExternalOutput").ap()
        ck.join()
        ck.dma("qs", d_xl1, bufs.xn[:, :, :])
        ck.join()

    if "C" in phases:
      TC3 = 256
      with (
          nc.sbuf_tensor("l1_x3", [128, KC, T], BF16) as x3,
          nc.sbuf_tensor("l1_h", [128, 32, TC3], BF16) as hbuf,
      ):
          ck.join()
          tln3 = emit_ln(ck, ln, [bufs.xn[:, c, :] for c in range(KC)],
                         [x3[:, c, :] for c in range(KC)], 1e-5,
                         gamma_col=[cols["l1_g2"][:, c:c + 1] for c in range(KC)],
                         beta_col=[cols["l1_be2"][:, c:c + 1] for c in range(KC)])
          ck.wait("pe", tln3)
          w1s = WStream(ck, "l1w1", W["w1"], D, 4096, nb=128)
          w2s = WStream(ck, "l1w2", W["w2"], 4096, D, nb=128)
          for tc in range(T // TC3):
              tsl = slice(tc * TC3, (tc + 1) * TC3)
              w1s.reset(); w2s.reset()
              w1s.start()
              htoks = {}
              for n in range(32):
                  wt = w1s.get(n)
                  bk = bufs.ps_big[:, (n % 2) * 512:(n % 2) * 512 + TC3]
                  if n >= 2:
                      ck.wait("pe", htoks[n - 2])
                  for kc in range(KC):
                      ck.pe.matmul(bk, wt[:, kc, :], x3[:, kc, tsl],
                                   start=(kc == 0), stop=(kc == KC - 1))
                  w1s.done(n)
                  tp = ck.tok("pe")
                  ck.wait("act", tp)
                  ck.act.activation(hbuf[:, n, :], bk, AF.Gelu,
                                    bias=cols["l1_b1"][:, n:n + 1])
                  htoks[n] = ck.tok("act")
              xtk = {}
              for oc in range(KC):
                  wt = w2s.get(oc)
                  bk = bufs.ps_big[:, 1024 + (oc % 2) * 512:1024 + (oc % 2) * 512 + TC3]
                  ck.wait("pe", htoks[31])
                  if oc >= 2:
                      ck.wait("pe", xtk[oc - 2])
                  for kc in range(32):
                      ck.pe.matmul(bk, wt[:, kc, :], hbuf[:, kc, :],
                                   start=(kc == 0), stop=(kc == 31))
                  w2s.done(oc)
                  tp = ck.tok("pe")
                  ck.wait("dve", tp)
                  ck.dve.scalar_tensor_tensor(
                      bufs.sc3[:, 0:TC3], bk, cols["l1_b2"][:, oc:oc + 1],
                      bufs.xn[:, oc, tsl], ALU.add, ALU.add)
                  ck.dve.tensor_add(bufs.X[:, oc, tsl], bufs.X[:, oc, tsl],
                                    bufs.sc3[:, 0:TC3])
                  xtk[oc] = ck.tok("dve")
              ck.wait("pe", xtk[KC - 1])
          w2s.close()
          w1s.close()




def emit_outer_ln(ck, bufs, ln, x_tok=None):
    for c in range(KC):
        if x_tok is not None:
            ck.wait("dve", x_tok)
        ck.dve.tensor_copy(bufs.xn[:, c, :], bufs.X[:, c, :])
    tcast = ck.tok("dve")
    chunks = [bufs.xn[:, c, :] for c in range(KC)]
    return emit_ln(ck, ln, chunks, chunks, 1e-5, src_tok=tcast)



LAYER_INPUTS = {
    0: dict(win=[D, 4096], wx=[2048, 33], wout0=[2048, D], wbase=[D, D],
            wd1=[8, D, 256], wd2=[8, 256, D], wadown=[D, 512],
            waup=[512, D], baup=[8, D], ascan=[128, 256]),
    1: dict(wq=[D, D], wk=[D, D], wv=[D, D], w1=[D, 4096], w2=[4096, D],
            wg=[D, 4096], wu=[D, 4096], wd=[4096, D]),
    2: dict(wq2=[16, D, 64], wk2=[16, D, 64], wv2=[16, D, 64],
            wo2=[D, D], rw=[D, 8], rb=[8], we2=[8, D, D]),
    3: dict(wqkv=[D, 3 * D], wout=[D, D], wfc=[4096, D], wproj=[D, 4096],
            bv=[1, D]),
}
LAYER_COLS = {
    0: dict(l0_nw=8, l0_cw=64, l0_cb=16, l0_wdt=16, l0_bdt=16, l0_D=16,
            l0_bb=8, l0_bd1=16, l0_bd2=64, l0_bad=4),
    1: dict(l1_g1=8, l1_be1=8, l1_g2=8, l1_be2=8, l1_b1=32, l1_b2=8),
    2: dict(),
    3: dict(l3_bq=8, l3_bk=8, l3_bo=8, l3_bfc=32, l3_bpj=8),
}
LAYER_SCR = {
    0: dict(),
    1: dict(scr_g=[128, KC, 4096], scr_u=[128, KC, 4096],
            scr_d=[128, 32, D]),
    2: dict(),
    3: dict(scr_fc=[128, KC, 4096], scr_pj=[128, 32, D]),
}


def build_program(layers, l1_phases="ABC"):
    nc = bass.Bass("TRN2", target_bir_lowering=False)
    ck = Ck(nc)
    x_in = nc.dram_tensor("x_in", [D, T], F32, kind="ExternalInput").ap()
    y_out = nc.dram_tensor("y_out", [D, T], F32, kind="ExternalOutput").ap()
    id_bf_d = nc.dram_tensor("c_id_bf", [128, 128], BF16,
                             kind="ExternalInput").ap()
    id_f_d = nc.dram_tensor("c_id_f32", [128, 128], F32,
                            kind="ExternalInput").ap()
    ck.scr_rows = nc.dram_tensor("scr_rows", [32, T], F32,
                                 kind="Internal").ap()
    ck.scr_grows = nc.dram_tensor("scr_grows", [8, T], F32,
                                  kind="Internal").ap()
    Ws, scrs = {}, {}
    for L in layers:
        for name, shape in LAYER_INPUTS.get(L, {}).items():
            Ws[name] = nc.dram_tensor(f"L{L}_{name}", shape, F32,
                                      kind="ExternalInput").ap()
        if L == 3:
            Ws["em"] = nc.dram_tensor("L3_em", [128, 2176], BF16,
                                      kind="ExternalInput").ap()
        if L == 0:
            Ws["sred"] = nc.dram_tensor("L0_sred", [128, 16, 128], BF16,
                                        kind="ExternalInput").ap()
            ck.scr_dt = nc.dram_tensor("scr_dt", [2048, T], BF16,
                                       kind="Internal").ap()
            ck.scr_u = nc.dram_tensor("scr_u", [2048, T], BF16,
                                      kind="Internal").ap()
            ck.scr_z = nc.dram_tensor("scr_z", [2048, T], BF16,
                                      kind="Internal").ap()
            ck.scr_bc = nc.dram_tensor("scr_bc", [32, T], F32,
                                       kind="Internal").ap()
            ck.scr_x = nc.dram_tensor("scr_x", [128, KC, T], F32,
                                      kind="Internal").ap()
        import os as _os
        _k = "ExternalOutput" if _os.environ.get("KDBG_SCR") else "Internal"
        for name, shape in LAYER_SCR.get(L, {}).items():
            Ws[name] = nc.dram_tensor(f"L{L}_{name}", shape, BF16,
                                      kind=_k).ap()
    col_handles = {}
    col_ctx = []
    bufs_ctx = None
    with_stack = []

    bufs = Bufs(nc, ck)
    ln = LNParams(nc, ck, bufs)
    t1 = ck.dma("qs", bufs.id_bf[:, :], id_bf_d)
    t2 = ck.dma("qs", bufs.id_f32[:, :], id_f_d)
    bufs.id_tok = t2
    ck.wait("pe", t1)
    for L in layers:
        for name, ncols in LAYER_COLS.get(L, {}).items():
            d = nc.dram_tensor(f"c_{name}", [128, ncols], F32,
                               kind="ExternalInput").ap()
            g = nc.sbuf_tensor(f"s_{name}", [128, ncols], F32)
            h = g.__enter__()
            col_ctx.append(g)
            tk = ck.dma("qs", h[:, :], d)
            ck.wait("dve", tk)
            ck.wait("act", tk)
            col_handles[name] = h
    tx = ck.dma("qs", bufs.X[:, :, :],
                x_in.rearrange("(kc kp) t -> kp kc t", kp=128))
    ck.wait("dve", tx)
    for L in layers:
        emit_outer_ln(ck, bufs, ln)
        ck.join()
        if L == 0:
            emit_l0(ck, bufs, ln, Ws, col_handles,
                    phases=l1_phases if l1_phases != "ABC" else "FM")
        elif L == 1:
            emit_l1(ck, bufs, ln, Ws, col_handles, phases=l1_phases)
        elif L == 2:
            emit_l2(ck, bufs, ln, Ws, col_handles, phases=l1_phases
                    if l1_phases != "ABC" else "RAM")
        elif L == 3:
            emit_l3(ck, bufs, ln, Ws, col_handles,
                    phases=l1_phases if l1_phases != "ABC" else "AF")
        else:
            raise NotImplementedError(L)
        ck.join()
    tX = ck.tok("dve")
    tout = ck.dma("qs", y_out.rearrange("(kc kp) t -> kp kc t", kp=128),
                  bufs.X[:, :, :], after=[tX])
    ck.finish([tout])
    for g in reversed(col_ctx):
        g.__exit__(None, None, None)
    return nc




def _pt(a):
    return np.ascontiguousarray(np.asarray(a).T)


def _cols(a, n):
    return np.ascontiguousarray(np.asarray(a).reshape(n, 128).T)


def prep_shared(inputs, layers):
    sh = {}
    sh["c_id_bf"] = np.eye(128, dtype=ml_dtypes.bfloat16)
    sh["c_id_f32"] = np.eye(128, dtype=np.float32)
    if 0 in layers:
        sh["L0_win"] = np.ascontiguousarray(inputs["L0_attn_W_in"])
        sh["L0_wx"] = np.ascontiguousarray(inputs["L0_attn_W_x"])
        sh["L0_wout0"] = np.ascontiguousarray(inputs["L0_attn_W_out"])
        sh["L0_wbase"] = np.ascontiguousarray(inputs["L0_ffn_W_base"])
        sh["L0_wd1"] = np.ascontiguousarray(inputs["L0_ffn_Wd1"])
        sh["L0_wd2"] = np.ascontiguousarray(inputs["L0_ffn_Wd2"])
        sh["L0_wadown"] = np.concatenate(
            [inputs["L0_ffn_Wa_down"][e] for e in range(8)], axis=1)
        sh["L0_waup"] = np.concatenate(
            [inputs["L0_ffn_Wa_up"][e] for e in range(8)], axis=0)
        sh["L0_baup"] = np.ascontiguousarray(inputs["L0_ffn_ba_up"])
        al = np.asarray(inputs["L0_attn_A_log"])
        asc = np.zeros((256, 128), np.float32)
        for kc in range(16):
            for j in range(16):
                for s in range(16):
                    for cl in range(8):
                        asc[kc * 16 + j, s * 8 + cl] =                             al[kc * 128 + 8 * j + cl, s]
        sh["L0_ascan"] = np.ascontiguousarray(asc.T)
        sred = np.zeros((16, 128, 128), np.float32)
        for j in range(16):
            for p in range(128):
                sred[j, p, 8 * j + (p % 8)] = 1.0
        sh["L0_sred"] = sred.transpose(1, 0, 2).astype(
            ml_dtypes.bfloat16).copy()
        sh["c_l0_nw"] = _cols(inputs["L0_attn_norm_w"], 8)
        cw = np.asarray(inputs["L0_attn_conv_w"]).reshape(2048, 4)
        sh["c_l0_cw"] = np.ascontiguousarray(
            cw.reshape(16, 128, 4).transpose(1, 0, 2).reshape(128, 64))
        sh["c_l0_cb"] = _cols(inputs["L0_attn_conv_b"], 16)
        sh["c_l0_wdt"] = _cols(inputs["L0_attn_w_dt"], 16)
        sh["c_l0_bdt"] = _cols(inputs["L0_attn_b_dt"], 16)
        sh["c_l0_D"] = _cols(inputs["L0_attn_D_param"], 16)
        sh["c_l0_bb"] = _cols(inputs["L0_ffn_b_base"], 8)
        bd1 = np.asarray(inputs["L0_ffn_bd1"])
        sh["c_l0_bd1"] = np.ascontiguousarray(
            bd1.reshape(8, 2, 128).transpose(2, 0, 1).reshape(128, 16))
        bd2 = np.asarray(inputs["L0_ffn_bd2"])
        sh["c_l0_bd2"] = np.ascontiguousarray(
            bd2.reshape(8, 8, 128).transpose(2, 0, 1).reshape(128, 64))
        sh["c_l0_bad"] = _cols(np.asarray(inputs["L0_ffn_ba_down"])
                               .reshape(512), 4)
    if 1 in layers:
        sh["L1_wq"] = _pt(inputs["L1_attn_wq"])
        sh["L1_wk"] = _pt(inputs["L1_attn_wk"])
        sh["L1_wv"] = _pt(inputs["L1_attn_wv"])
        sh["L1_w1"] = _pt(inputs["L1_attn_w1"])
        sh["L1_w2"] = _pt(inputs["L1_attn_w2"])
        sh["L1_wg"] = _pt(inputs["L1_ffn_w_gate"])
        sh["L1_wu"] = _pt(inputs["L1_ffn_w_up"])
        sh["L1_wd"] = _pt(inputs["L1_ffn_w_down"])
        sh["c_l1_g1"] = _cols(inputs["L1_attn_g1"], 8)
        sh["c_l1_be1"] = _cols(inputs["L1_attn_be1"], 8)
        sh["c_l1_g2"] = _cols(inputs["L1_attn_g2"], 8)
        sh["c_l1_be2"] = _cols(inputs["L1_attn_be2"], 8)
        sh["c_l1_b1"] = _cols(inputs["L1_attn_b1"], 32)
        sh["c_l1_b2"] = _cols(inputs["L1_attn_b2"], 8)
    if 2 in layers:
        sh["L2_wq2"] = np.ascontiguousarray(inputs["L2_attn_W_q"])
        sh["L2_wk2"] = np.ascontiguousarray(inputs["L2_attn_W_k"])
        sh["L2_wv2"] = np.ascontiguousarray(inputs["L2_attn_W_v"])
        sh["L2_wo2"] = np.ascontiguousarray(inputs["L2_attn_W_o"])
        sh["L2_rw"] = _pt(inputs["L2_ffn_router_w"])
        sh["L2_rb"] = np.ascontiguousarray(inputs["L2_ffn_router_b"])
        sh["L2_we2"] = np.ascontiguousarray(inputs["L2_ffn_expert_weights"])
    if 3 in layers:
        sh["L3_wqkv"] = _pt(inputs["L3_attn_w_qkv"])
        sh["L3_wout"] = _pt(inputs["L3_attn_w_out"])
        sh["L3_wfc"] = np.ascontiguousarray(inputs["L3_ffn_w_fc"])
        sh["L3_wproj"] = np.ascontiguousarray(inputs["L3_ffn_w_proj"])
        b = np.asarray(inputs["L3_attn_b_qkv"])
        sh["c_l3_bq"] = _cols(b[0:D], 8)
        sh["c_l3_bk"] = _cols(b[D:2 * D], 8)
        sh["L3_bv"] = np.ascontiguousarray(b[2 * D:3 * D]).reshape(1, D)
        sh["c_l3_bo"] = _cols(inputs["L3_attn_b_out"], 8)
        sh["c_l3_bfc"] = _cols(inputs["L3_ffn_b_fc"], 32)
        sh["c_l3_bpj"] = _cols(inputs["L3_ffn_b_proj"], 8)
        p = np.arange(128)[:, None]
        f = np.arange(2176)[None, :]
        ramp = (f - 2048) - p
        em = np.where(ramp <= 0, np.exp(np.minimum(ramp, 0)), 0.0)
        sh["L3_em"] = em.astype(ml_dtypes.bfloat16)
    return sh


def run_layers(inputs, x_stack, layers):
    sh = prep_shared(inputs, layers)
    nc = build_program(layers)
    in_maps = []
    for c in range(8):
        m = dict(sh)
        m["x_in"] = x_stack[c]
        in_maps.append(m)
    res = run_bass_kernel_spmd(nc, in_maps, core_ids=list(range(8)))
    return [np.asarray(res.results[c]["y_out"]) for c in range(8)]


def kernel(**inputs):
    x = np.asarray(inputs["L0_attn_x"])
    xs = [np.ascontiguousarray(x[c].T) for c in range(8)]
    xs = run_layers(inputs, xs, [0, 1, 2, 3])
    return np.stack([o.T for o in xs]).astype(np.float32)




def emit_l2(ck, bufs, ln, W, cols, phases="RAM"):
    import os as _os
    nc = ck.nc
    NH, HD = 16, 64

    with (
        nc.sbuf_tensor("l2_rw", [128, KC, 8], BF16) as rw,
        nc.sbuf_tensor("l2_rbt", [128, 8], F32) as rbt,
        nc.sbuf_tensor("l2_lg", [128, 8], F32) as lg,
        nc.sbuf_tensor("l2_gcols", [128, 8], F32) as gcols,
        nc.sbuf_tensor("l2_grows", [8, T], F32) as grows,
        nc.sbuf_tensor("l2_c", [128, 16], F32) as c,
    ):
        ck.join()
        t_rw = ck.dma("qp", rw[:, :, :],
                      W["rw"].rearrange("(kc kp) n -> kp kc n", kp=128))
        t_rb = ck.dma("qs", rbt[:, :].unsqueeze(1),
                      W["rb"].unsqueeze(0).partition_broadcast(128))
        for tt in range(16):
            ttsl = slice(tt * 128, (tt + 1) * 128)
            ck.waits("pe", [t_rw])
            ck.wait("pe", bufs.mid_free)
            for kc in range(KC):
                ck.pe.matmul(bufs.ps_mid[:, 0:8], bufs.xn[:, kc, ttsl],
                             rw[:, kc, :], start=(kc == 0),
                             stop=(kc == KC - 1))
            tp = ck.tok("pe")
            ck.waits("dve", [tp, t_rb])
            ck.dve.tensor_add(lg[:, :], bufs.ps_mid[:, 0:8], rbt[:, :])
            bufs.mid_free = ck.tok("dve")
            ck.vreduce(c[:, 0:1], lg[:, :], AX.X, ALU.max)
            ck.dve.tensor_scalar_mul(c[:, 1:2], c[:, 0:1], -1.0)
            td = ck.tok("dve")
            ck.wait("act", td)
            ck.act.activation(lg[:, :], lg[:, :], AF.Exp, bias=c[:, 1:2],
                              accum_out=c[:, 2:3])
            ta = ck.tok("act")
            ck.wait("dve", ta)
            ck.vreduce(c[:, 3:4], lg[:, :], AX.X, ALU.max)
            ck.dve.tensor_scalar(c[:, 4:5], c[:, 3:4], -1.0, None, ALU.mult)
            ck.dve.scalar_tensor_tensor(gcols[:, 0:8], lg[:, :], c[:, 3:4],
                                        lg[:, :], ALU.is_equal, ALU.bypass)
            ck.dve.scalar_tensor_tensor(lg[:, :], gcols[:, 0:8], c[:, 4:5],
                                        lg[:, :], ALU.mult, ALU.add)
            ck.vreduce(c[:, 5:6], lg[:, :], AX.X, ALU.max)
            ck.dve.tensor_add(c[:, 6:7], c[:, 3:4], c[:, 5:6])
            ck.dve.tensor_scalar(c[:, 7:8], c[:, 2:3], 1e-10, None, ALU.mult)
            ck.dve.tensor_add(c[:, 6:7], c[:, 6:7], c[:, 7:8])
            ck.recip(c[:, 8:9], c[:, 6:7])
            ck.dve.tensor_mul(c[:, 9:10], c[:, 3:4], c[:, 8:9])
            ck.dve.tensor_mul(c[:, 10:11], c[:, 5:6], c[:, 8:9])
            ck.dve.tensor_add(c[:, 11:12], c[:, 9:10], c[:, 10:11])
            ck.dve.tensor_scalar_max(c[:, 11:12], c[:, 11:12], 1e-10)
            ck.recip(c[:, 12:13], c[:, 11:12])
            ck.dve.tensor_scalar_add(c[:, 12:13], c[:, 12:13], 1e-10)
            ck.dve.tensor_mul(c[:, 13:14], c[:, 9:10], c[:, 12:13])
            ck.dve.tensor_mul(c[:, 14:15], c[:, 10:11], c[:, 12:13])
            ck.dve.scalar_tensor_tensor(lg[:, :], lg[:, :], c[:, 5:6],
                                        lg[:, :], ALU.is_equal, ALU.bypass)
            ck.dve.scalar_tensor_tensor(gcols[:, 0:8], gcols[:, 0:8],
                                        c[:, 13:14], gcols[:, 0:8],
                                        ALU.mult, ALU.bypass)
            ck.dve.scalar_tensor_tensor(gcols[:, 0:8], lg[:, :], c[:, 14:15],
                                        gcols[:, 0:8], ALU.mult, ALU.add)
            tg = ck.tok("dve")
            ck.waits("pe", [tg, bufs.id_tok])
            ck.pe.transpose(bufs.ps_mid[0:8, 256:384], gcols[:, 0:8],
                            bufs.id_f32[:, :])
            tp = ck.tok("pe")
            ck.wait("dve", tp)
            ck.dve.tensor_copy(grows[0:8, ttsl], bufs.ps_mid[0:8, 256:384])
            bufs.mid_free = ck.tok("dve")
        tgr = ck.tok("dve")
        if _os.environ.get("KDBG_L2R"):
            d_gr = nc.dram_tensor("d_gr", [8, T], F32,
                                  kind="ExternalOutput").ap()
            d_c = nc.dram_tensor("d_c", [128, 16], F32,
                                 kind="ExternalOutput").ap()
            ck.dma("qs", d_gr, grows[:, :], after=[tgr])
            ck.dma("qs", d_c, c[:, :], after=[tgr])
            ck.join()
        ck.wait("sp", tgr)
        t_up = ck.dma("qs", ck.scr_grows[:, :], grows[:, :])

    avg = nc.sbuf_tensor("l2_av", [128, KC, T], BF16)
    av_all = avg.__enter__()
    if "A" not in phases:
        ck.join()
        ck.pool.memset(av_all[:, :, :], 0.0)
        ck.join()
    if "A" in phases:
     with (
         nc.sbuf_tensor("l2_wh0", [128, KC, 3 * HD], BF16) as wqkv0,
         nc.sbuf_tensor("l2_wh1", [128, KC, 3 * HD], BF16) as wqkv1,
         nc.sbuf_tensor("l2_qh", [64, T], BF16) as qh,
         nc.sbuf_tensor("l2_kh", [64, T], BF16) as kh,
         nc.sbuf_tensor("l2_vh", [128, 16, HD], BF16) as vh,
         nc.sbuf_tensor("l2_P", [128, T], BF16) as Pb,
         nc.sbuf_tensor("l2_PT", [128, 16, 128], BF16) as PT,
         nc.sbuf_tensor("l2_sq", [64, T], BF16) as sqb,
         nc.sbuf_tensor("l2_nb", [128, 16], F32) as nbc,
         nc.sbuf_tensor("l2_rs", [128, 16], F32) as rsc,
         nc.sbuf_tensor("l2_r16", [16, 128], F32) as r16,
     ):
         ck.join()
         whb = [wqkv0, wqkv1]
         wh_load = {}
         wh_free = [None, None]

         def load_head(h):
             if h >= NH or h in wh_load:
                 return
             b = whb[h % 2]
             after = [wh_free[h % 2]] if wh_free[h % 2] else []
             toks = [ck.dma("qp", b[:, :, 0:HD],
                            W["wq2"][h].rearrange("(kc kp) n -> kp kc n",
                                                  kp=128), after=after)]
             toks.append(ck.dma("qp", b[:, :, HD:2 * HD],
                                W["wk2"][h].rearrange("(kc kp) n -> kp kc n",
                                                      kp=128), after=after))
             toks.append(ck.dma("qp", b[:, :, 2 * HD:3 * HD],
                                W["wv2"][h].rearrange("(kc kp) n -> kp kc n",
                                                      kp=128), after=after))
             wh_load[h] = toks

         load_head(0)
         load_head(1)
         for h in range(NH):
             wh = whb[h % 2]
             ck.waits("pe", wh_load[h])
             for tc in range(NT):
                 tsl = slice(tc * 512, (tc + 1) * 512)
                 bq = bufs.ps_mid[0:64, 0:512]
                 for kc in range(KC):
                     ck.pe.matmul(bq, wh[:, kc, 0:HD], bufs.xn[:, kc, tsl],
                                  start=(kc == 0), stop=(kc == KC - 1))
                 tp = ck.tok("pe")
                 ck.wait("dve", tp)
                 ck.dve.tensor_copy(qh[:, tsl], bq)
                 tq = ck.tok("dve")
                 ck.wait("pe", tq)
                 for kc in range(KC):
                     ck.pe.matmul(bq, wh[:, kc, HD:2 * HD],
                                  bufs.xn[:, kc, tsl],
                                  start=(kc == 0), stop=(kc == KC - 1))
                 tp = ck.tok("pe")
                 ck.wait("dve", tp)
                 ck.dve.tensor_copy(kh[:, tsl], bq)
                 ck.wait("pe", ck.tok("dve"))
             for tt in range(16):
                 ttsl = slice(tt * 128, (tt + 1) * 128)
                 bv = bufs.ps_mid[:, 0:64]
                 for kc in range(KC):
                     ck.pe.matmul(bv, bufs.xn[:, kc, ttsl], wh[:, kc,
                                  2 * HD:3 * HD],
                                  start=(kc == 0), stop=(kc == KC - 1))
                 tp = ck.tok("pe")
                 ck.wait("dve", tp)
                 ck.dve.tensor_copy(vh[:, tt, :], bv)
                 ck.wait("pe", ck.tok("dve"))
             wh_free[h % 2] = ck.tok("pe")
             load_head(h + 1)
             tnb = None
             for qt in range(16):
                 qsl = slice(qt * 128, (qt + 1) * 128)
                 ck.wait("pe", bufs.big_free)
                 for kb in range(4):
                     ck.pe.matmul(bufs.ps_big[:, kb * 512:(kb + 1) * 512],
                                  qh[:, qsl], kh[:, kb * 512:(kb + 1) * 512],
                                  start=True, stop=True)
                 tp = ck.tok("pe")
                 ck.wait("dve", tp)
                 ck.vreduce(nbc[:, qt:qt + 1], bufs.ps_big[:, :], AX.X,
                            ALU.max)
                 ck.dve.tensor_scalar_mul(nbc[:, qt:qt + 1],
                                          nbc[:, qt:qt + 1], -0.125)
                 tnb = ck.tok("dve")
                 ck.waits("act", [tp, tnb])
                 ck.act.activation(Pb[:, :], bufs.ps_big[:, :], AF.Exp,
                                   bias=nbc[:, qt:qt + 1], scale=0.125,
                                   accum_out=rsc[:, qt:qt + 1])
                 ta = ck.tok("act")
                 bufs.big_free = ta
                 for half in range(2):
                     ck.wait("pe", ta)
                     ck.wait("pe", bufs.trb_free)
                     for b8 in range(8):
                         kt = half * 8 + b8
                         ck.pe.transpose(
                             bufs.ps_trb[:, b8 * 128:(b8 + 1) * 128],
                             Pb[:, kt * 128:(kt + 1) * 128], bufs.id_bf[:, :])
                     tp = ck.tok("pe")
                     ck.wait("dve", tp)
                     ck.dve.tensor_copy(
                         PT[:, half * 8:(half + 1) * 8, :],
                         bufs.ps_trb[:, :].rearrange("p (b n) -> p b n", n=128))
                     bufs.trb_free = ck.tok("dve")
                 tpt = ck.tok("dve")
                 ck.wait("pe", tpt)
                 for kt in range(16):
                     ck.pe.matmul(bufs.ps_mid[0:64, 0:128], vh[:, kt, :],
                                  PT[:, kt, :], start=(kt == 0),
                                  stop=(kt == 15))
                 tp = ck.tok("pe")
                 ck.wait("dve", ta)
                 ck.recip(rsc[:, qt:qt + 1], rsc[:, qt:qt + 1])
                 trr = ck.tok("dve")
                 ck.waits("pe", [trr, bufs.id_tok])
                 ck.pe.transpose(bufs.ps_mid[0:1, 384:512],
                                 rsc[:, qt:qt + 1], bufs.id_f32[:, :])
                 tp2 = ck.tok("pe")
                 ck.wait("dve", tp2)
                 ck.dve.tensor_copy(bufs.rowB[0:1, qsl],
                                    bufs.ps_mid[0:1, 384:512])
                 trow = ck.tok("dve")
                 ck.wait("sp", trow)
                 tu3 = ck.dma("qs", ck.scr_rows[18:19, qsl],
                              bufs.rowB[0:1, qsl])
                 tb3 = ck.dma("qs",
                              bufs.sc3[:, qsl].unsqueeze(1),
                              ck.scr_rows[18:19, qsl]
                              .partition_broadcast(128), after=[tu3])
                 ck.waits("dve", [tb3, ck.tok("pe")])
                 hc, hp = h // 2, (h % 2) * 64
                 ck.dve.tensor_mul(av_all[hp:hp + 64, hc, qsl],
                                   bufs.ps_mid[0:64, 0:128],
                                   bufs.sc3[hp:hp + 64, qsl])
                 ck.wait("pe", ck.tok("dve"))

    with (
        nc.sbuf_tensor("l2_xe", [128, KC, T], BF16) as xe,
        nc.sbuf_tensor("l2_we0", [128, KC, 128], BF16) as we0,
        nc.sbuf_tensor("l2_we1", [128, KC, 128], BF16) as we1,
        nc.sbuf_tensor("l2_gb", [128, T], F32) as gb,
    ):
        ck.join()
        web = [we0, we1]
        wfree = [None, None]
        srcs = [("wo2", None, av_all)] + [("we2", e, xe) for e in range(8)]
        for si, (wname, e, rhs) in enumerate(srcs):
            if e is not None:
                tprev = ck.tok("dve")
                tg = ck.dma("qs", gb[:, :].unsqueeze(1),
                            ck.scr_grows[e:e + 1, :].partition_broadcast(128),
                            after=[tprev])
                ck.wait("dve", tg)
                for kc in range(KC):
                    ck.dve.tensor_mul(xe[:, kc, :], bufs.xn[:, kc, :],
                                      gb[:, :])
                txe = ck.tok("dve")
            else:
                txe = None
            for oc in range(KC):
                b = oc % 2
                wap = W["wo2"] if e is None else W["we2"][e]
                tw = ck.dma("qp", web[b][:, :, :],
                            wap[:, oc * 128:(oc + 1) * 128]
                            .rearrange("(kc kp) n -> kp kc n", kp=128),
                            after=[wfree[b]] if wfree[b] else [])
                ck.waits("pe", [tw, txe, bufs.big_free])
                for kc in range(KC):
                    for tc in range(NT):
                        ck.pe.matmul(
                            bufs.ps_big[:, tc * 512:(tc + 1) * 512],
                            web[b][:, kc, :],
                            rhs[:, kc, tc * 512:(tc + 1) * 512],
                            start=(kc == 0), stop=(kc == KC - 1))
                wfree[b] = ck.tok("pe")
                tp = wfree[b]
                ck.wait("dve", tp)
                ck.dve.tensor_add(bufs.X[:, oc, :], bufs.X[:, oc, :],
                                  bufs.ps_big[:, :])
                bufs.big_free = ck.tok("dve")
    avg.__exit__(None, None, None)




def quantize_bfp_weight(ck, bufs, w_ap, N, Kd, scr_ap, name):
    nc = ck.nc
    kc = Kd // 128
    nchunk = N // 128
    nblk = Kd // 32
    with (
        nc.sbuf_tensor(f"{name}_f", [128, Kd], F32) as fb,
        nc.sbuf_tensor(f"{name}_q", [128, Kd], BF16) as qb,
        nc.sbuf_tensor(f"{name}_am", [128, nblk], F32) as am,
        nc.sbuf_tensor(f"{name}_rc", [128, nblk], F32) as rc,
        nc.sbuf_tensor(f"{name}_st", [128, 1024], BF16) as stg,
    ):
        ck.join()
        prev = None
        prev_store = [None]
        for r in range(nchunk):
            tl = ck.dma("qs", fb[:, :], w_ap[r * 128:(r + 1) * 128, :],
                        after=[prev] if prev else [])
            ck.wait("dve", tl)
            v3 = fb[:, :].rearrange("p (b j) -> p b j", j=32)
            ck.vreduce(am[:, :], v3, AX.X, ALU.max,
                       apply_absolute_value=True)
            ck.dve.tensor_scalar_max(am[:, :], am[:, :], 1.17549435e-38)
            ck.dve.tensor_scalar(am[:, :].bitcast(mybir.dt.int32),
                                 am[:, :].bitcast(mybir.dt.int32),
                                 0x7F800000, None, ALU.bitwise_and)
            ck.dve.drain()
            ck.recip(rc[:, :], am[:, :])
            ck.dve.tensor_scalar_mul(rc[:, :], rc[:, :], 8.0)
            ck.dve.tensor_scalar_mul(am[:, :], am[:, :], 0.125)
            ck.dve.drain()
            rcb = rc[:, :].unsqueeze(2).broadcast_to([128, nblk, 32])
            amb = am[:, :].unsqueeze(2).broadcast_to([128, nblk, 32])
            ck.dve.tensor_tensor(v3, v3, rcb, ALU.mult)
            ck.dve.tensor_scalar(fb[:, :], fb[:, :], RMAGIC, RMAGIC,
                                 ALU.add, ALU.subtract)
            ck.dve.tensor_tensor(qb[:, :].rearrange("p (b j) -> p b j", j=32),
                                 fb[:, :].rearrange("p (b j) -> p b j", j=32),
                                 amb, ALU.mult)
            tq = ck.tok("dve")
            prev = tq
            for g in range(kc // 8):
                ck.waits("pe", [tq, bufs.id_tok, bufs.trb_free])
                for b8 in range(8):
                    kb = g * 8 + b8
                    ck.pe.transpose(bufs.ps_trb[:, b8 * 128:(b8 + 1) * 128],
                                    qb[:, kb * 128:(kb + 1) * 128],
                                    bufs.id_bf[:, :])
                tp = ck.tok("pe")
                ck.wait("dve", tp)
                ck.wait("dve", prev_store[0] if prev_store else None)
                ck.dve.tensor_copy(stg[:, :], bufs.ps_trb[:, :])
                bufs.trb_free = ck.tok("dve")
                ts_ = ck.dma("qs",
                             scr_ap[:, g * 8:(g + 1) * 8,
                                    r * 128:(r + 1) * 128],
                             stg[:, :].rearrange("p (b n) -> p b n", n=128),
                             after=[bufs.trb_free])
                prev_store[0] = ts_
            ck.wait("dve", ck.tok("pe"))
        ck.waits("sp", [prev_store[0]])


def emit_l3(ck, bufs, ln, W, cols, phases="AF"):
    import os as _os
    nc = ck.nc
    NH, HD = 16, 64

    avg = nc.sbuf_tensor("l3_av", [128, KC, T], BF16)
    av_all = avg.__enter__()
    if "A" not in phases:
        ck.join()
        ck.pool.memset(av_all[:, :, :], 0.0)
        ck.join()
    else:
     with (
        nc.sbuf_tensor("l3_em", [128, 2176], BF16) as em,
        nc.sbuf_tensor("l3_wq0", [128, KC, HD], BF16) as wq0,
        nc.sbuf_tensor("l3_wk0", [128, KC, HD], BF16) as wk0,
        nc.sbuf_tensor("l3_wv0", [128, KC, HD], BF16) as wv0,
        nc.sbuf_tensor("l3_wq1", [128, KC, HD], BF16) as wq1,
        nc.sbuf_tensor("l3_wk1", [128, KC, HD], BF16) as wk1,
        nc.sbuf_tensor("l3_wv1", [128, KC, HD], BF16) as wv1,
        nc.sbuf_tensor("l3_qh", [64, T], BF16) as qh,
        nc.sbuf_tensor("l3_kh", [64, T], BF16) as kh,
        nc.sbuf_tensor("l3_vh", [128, 16, HD], BF16) as vh,
        nc.sbuf_tensor("l3_P", [128, T], BF16) as Pb,
        nc.sbuf_tensor("l3_PT", [128, 16, 128], BF16) as PT,
        nc.sbuf_tensor("l3_sq", [64, T], BF16) as sqb,
        nc.sbuf_tensor("l3_nb", [128, 16], F32) as nbc,
        nc.sbuf_tensor("l3_rs", [128, 16], F32) as rsc,
     ):
        ck.join()
        t_em = ck.dma("qs", em[:, :], W["em"])
        bufs.vbrow_tok = ck.dma("qs", bufs.sc1[:, 0:1024].unsqueeze(1),
                                W["bv"].partition_broadcast(128))
        whb = [[wq0, wk0, wv0], [wq1, wk1, wv1]]
        wh_load = {}
        wh_free = [None, None]

        def load_head(h):
            if h >= NH or h in wh_load:
                return
            b = whb[h % 2]
            after = [wh_free[h % 2]] if wh_free[h % 2] else []
            toks = []
            for i in range(3):
                colsl = slice(i * D + h * HD, i * D + (h + 1) * HD)
                toks.append(ck.dma(
                    "qp", b[i][:, :, :],
                    W["wqkv"][:, colsl].rearrange("(kc kp) n -> kp kc n",
                                                  kp=128), after=after))
            wh_load[h] = toks

        load_head(0)
        load_head(1)
        for h in range(NH):
            wh = whb[h % 2]
            ck.waits("pe", wh_load[h])
            for tc in range(NT):
                tsl = slice(tc * 512, (tc + 1) * 512)
                bq = bufs.ps_mid[0:64, 0:512]
                for kc in range(KC):
                    ck.pe.matmul(bq, wh[0][:, kc, :], bufs.xn[:, kc, tsl],
                                 start=(kc == 0), stop=(kc == KC - 1))
                tp = ck.tok("pe")
                ck.wait("dve", tp)
                ck.dve.tensor_scalar(qh[:, tsl], bq,
                                     cols["l3_bq"]
                                     [(h % 2) * 64:(h % 2) * 64 + 64,
                                      h // 2:h // 2 + 1], None, ALU.add)
                ta = ck.tok("dve")
                ck.wait("pe", ta)
                for kc in range(KC):
                    ck.pe.matmul(bq, wh[1][:, kc, :], bufs.xn[:, kc, tsl],
                                 start=(kc == 0), stop=(kc == KC - 1))
                tp = ck.tok("pe")
                ck.wait("dve", tp)
                ck.dve.tensor_scalar(kh[:, tsl], bq,
                                     cols["l3_bk"]
                                     [(h % 2) * 64:(h % 2) * 64 + 64,
                                      h // 2:h // 2 + 1], None, ALU.add)
                ta = ck.tok("dve")
                ck.wait("pe", ta)
            for tt in range(16):
                ttsl = slice(tt * 128, (tt + 1) * 128)
                bv = bufs.ps_mid[:, 0:64]
                for kc in range(KC):
                    ck.pe.matmul(bv, bufs.xn[:, kc, ttsl], wh[2][:, kc, :],
                                 start=(kc == 0), stop=(kc == KC - 1))
                tp = ck.tok("pe")
                ck.waits("dve", [tp, bufs.vbrow_tok])
                ck.dve.tensor_add(vh[:, tt, :], bv,
                                  bufs.sc1[:, h * 64:(h + 1) * 64])
                ck.wait("pe", ck.tok("dve"))
            wh_free[h % 2] = ck.tok("pe")
            load_head(h + 1)
            tnb = None
            for qt in range(16):
                qsl = slice(qt * 128, (qt + 1) * 128)
                Wk = (qt + 1) * 128
                nkb = (Wk + 511) // 512
                ck.wait("pe", bufs.big_free)
                for kb in range(nkb):
                    wkb = min(512, Wk - kb * 512)
                    ck.pe.matmul(bufs.ps_big[:, kb * 512:kb * 512 + wkb],
                                 qh[:, qsl], kh[:, kb * 512:kb * 512 + wkb],
                                 start=True, stop=True)
                tp = ck.tok("pe")
                ck.wait("dve", tp)
                ck.vreduce(nbc[:, qt:qt + 1], bufs.ps_big[:, 0:Wk], AX.X,
                           ALU.max)
                ck.dve.tensor_scalar_mul(nbc[:, qt:qt + 1],
                                         nbc[:, qt:qt + 1], -0.125)
                tnb = ck.tok("dve")
                ck.waits("act", [tp, t_em, tnb])
                ck.act.activation(Pb[:, 0:Wk], bufs.ps_big[:, 0:Wk], AF.Exp,
                                  bias=nbc[:, qt:qt + 1], scale=0.125)
                ta = ck.tok("act")
                ck.wait("dve", ta)
                ck.dve.scalar_tensor_tensor(
                    Pb[:, 0:Wk], Pb[:, 0:Wk], 0.0,
                    em[:, 2048 - qt * 128:2048 - qt * 128 + Wk],
                    ALU.add, ALU.mult, accum_out=rsc[:, qt:qt + 1])
                tmsk = ck.tok("dve")
                bufs.big_free = tmsk
                nkt = qt + 1
                for half in range((nkt + 7) // 8):
                    n8 = min(8, nkt - half * 8)
                    ck.wait("pe", tmsk)
                    ck.wait("pe", bufs.trb_free)
                    for b8 in range(n8):
                        kt = half * 8 + b8
                        ck.pe.transpose(
                            bufs.ps_trb[:, b8 * 128:(b8 + 1) * 128],
                            Pb[:, kt * 128:(kt + 1) * 128], bufs.id_bf[:, :])
                    tp = ck.tok("pe")
                    ck.wait("dve", tp)
                    ck.dve.tensor_copy(
                        PT[:, half * 8:half * 8 + n8, :],
                        bufs.ps_trb[:, 0:n8 * 128]
                        .rearrange("p (b n) -> p b n", n=128))
                    bufs.trb_free = ck.tok("dve")
                tpt = ck.tok("dve")
                ck.wait("pe", tpt)
                for kt in range(nkt):
                    ck.pe.matmul(bufs.ps_mid[0:64, 0:128], vh[:, kt, :],
                                 PT[:, kt, :], start=(kt == 0),
                                 stop=(kt == nkt - 1))
                tp = ck.tok("pe")
                ck.wait("dve", ta)
                ck.recip(rsc[:, qt:qt + 1], rsc[:, qt:qt + 1])
                trr = ck.tok("dve")
                ck.waits("pe", [trr, bufs.id_tok])
                ck.pe.transpose(bufs.ps_mid[0:1, 384:512],
                                rsc[:, qt:qt + 1], bufs.id_f32[:, :])
                tp2 = ck.tok("pe")
                ck.wait("dve", tp2)
                ck.dve.tensor_copy(bufs.rowB[0:1, qsl],
                                   bufs.ps_mid[0:1, 384:512])
                trow = ck.tok("dve")
                ck.wait("sp", trow)
                tu3 = ck.dma("qs", ck.scr_rows[18:19, qsl],
                             bufs.rowB[0:1, qsl])
                tb3 = ck.dma("qs", bufs.sc3[:, qsl].unsqueeze(1),
                             ck.scr_rows[18:19, qsl]
                             .partition_broadcast(128), after=[tu3])
                ck.waits("dve", [tb3, ck.tok("pe")])
                hc, hp = h // 2, (h % 2) * 64
                ck.dve.tensor_mul(av_all[hp:hp + 64, hc, qsl],
                                  bufs.ps_mid[0:64, 0:128],
                                  bufs.sc3[hp:hp + 64, qsl])
                ck.wait("pe", ck.tok("dve"))
            if _os.environ.get("KDBG_L3A") and h == 0:
                d_rs = nc.dram_tensor("d_rs", [128, 16], F32,
                                      kind="ExternalOutput").ap()
                d_nb = nc.dram_tensor("d_nb", [128, 16], F32,
                                      kind="ExternalOutput").ap()
                d_q3 = nc.dram_tensor("d_q3", [64, T], BF16,
                                      kind="ExternalOutput").ap()
                d_k3 = nc.dram_tensor("d_k3", [64, T], BF16,
                                      kind="ExternalOutput").ap()
                d_P3 = nc.dram_tensor("d_P3", [128, T], BF16,
                                      kind="ExternalOutput").ap()
                ck.join()
                ck.dma("qs", d_rs, rsc[:, :])
                ck.dma("qs", d_nb, nbc[:, :])
                ck.dma("qs", d_q3, qh[:, :])
                ck.dma("qs", d_k3, kh[:, :])
                ck.dma("qs", d_P3, Pb[:, :])
                ck.join()

    quantize_bfp_weight(ck, bufs, W["wfc"], 4096, D, W["scr_fc"], "l3fc")
    quantize_bfp_weight(ck, bufs, W["wproj"], D, 4096, W["scr_pj"], "l3pj")
    TC3 = 256
    with (
        nc.sbuf_tensor("l3_tm", [128, 2048], BF16) as tmb,
        nc.sbuf_tensor("l3_am", [128, 64], F32) as am3,
        nc.sbuf_tensor("l3_rc", [128, 64], F32) as rc3,
        nc.sbuf_tensor("l3_h", [128, 32, TC3], BF16) as hbuf,
        nc.sbuf_tensor("l3_wo", [128, KC, 128], BF16) as wo3,
    ):
        ck.join()
        xq = bufs.xn

        def quant_tm(width):
            nb = width // 32
            v3 = tmb[:, 0:width].rearrange("p (b j) -> p b j", j=32)
            ck.vreduce(am3[:, 0:nb], v3, AX.X, ALU.max,
                       apply_absolute_value=True)
            ck.dve.tensor_scalar_max(am3[:, 0:nb], am3[:, 0:nb],
                                     1.17549435e-38)
            ck.dve.tensor_scalar(am3[:, 0:nb].bitcast(mybir.dt.int32),
                                 am3[:, 0:nb].bitcast(mybir.dt.int32),
                                 0x7F800000, None, ALU.bitwise_and)
            ck.dve.drain()
            ck.recip(rc3[:, 0:nb], am3[:, 0:nb])
            ck.dve.tensor_scalar_mul(rc3[:, 0:nb], rc3[:, 0:nb], 8.0)
            ck.dve.tensor_scalar_mul(am3[:, 0:nb], am3[:, 0:nb], 0.125)
            ck.dve.drain()
            rcb = rc3[:, 0:nb].unsqueeze(2).broadcast_to([128, nb, 32])
            amb = am3[:, 0:nb].unsqueeze(2).broadcast_to([128, nb, 32])
            ck.dve.tensor_tensor(bufs.sc2[:, 0:width]
                                 .rearrange("p (b j) -> p b j", j=32),
                                 v3, rcb, ALU.mult)
            ck.dve.tensor_scalar(bufs.sc2[:, 0:width], bufs.sc2[:, 0:width],
                                 RMAGIC, RMAGIC, ALU.add, ALU.subtract)
            ck.dve.tensor_tensor(v3, bufs.sc2[:, 0:width]
                                 .rearrange("p (b j) -> p b j", j=32),
                                 amb, ALU.mult)
            return ck.tok("dve")

        for tt in range(16):
            ttsl = slice(tt * 128, (tt + 1) * 128)
            ck.wait("pe", bufs.trb_free)
            ck.wait("pe", bufs.id_tok)
            for c in range(KC):
                ck.pe.transpose(bufs.ps_trb[:, c * 128:(c + 1) * 128],
                                bufs.xn[:, c, ttsl], bufs.id_bf[:, :])
            tp = ck.tok("pe")
            ck.wait("dve", tp)
            ck.dve.tensor_copy(tmb[:, 0:1024], bufs.ps_trb[:, :])
            bufs.trb_free = ck.tok("dve")
            tq = quant_tm(1024)
            ck.waits("pe", [tq, bufs.trb_free])
            for c in range(KC):
                ck.pe.transpose(bufs.ps_trb[:, c * 128:(c + 1) * 128],
                                tmb[:, c * 128:(c + 1) * 128],
                                bufs.id_bf[:, :])
            tp = ck.tok("pe")
            ck.wait("dve", tp)
            for c in range(KC):
                ck.dve.tensor_copy(xq[:, c, ttsl],
                                   bufs.ps_trb[:, c * 128:(c + 1) * 128])
            bufs.trb_free = ck.tok("dve")

        sfc = ScrStream(ck, "l3sfc", W["scr_fc"], KC, 4096, nb=128)
        spj = ScrStream(ck, "l3spj", W["scr_pj"], 32, D, nb=128)
        for tc in range(T // TC3):
            tsl = slice(tc * TC3, (tc + 1) * TC3)
            sfc.reset(); spj.reset()
            sfc.start()
            htoks = {}
            for n in range(32):
                wt = sfc.get(n)
                bk = bufs.ps_big[:, (n % 2) * 512:(n % 2) * 512 + TC3]
                if n >= 2:
                    ck.wait("pe", htoks[n - 2])
                for kc in range(KC):
                    ck.pe.matmul(bk, wt[:, kc, :], xq[:, kc, tsl],
                                 start=(kc == 0), stop=(kc == KC - 1))
                sfc.done(n)
                tp = ck.tok("pe")
                ck.wait("act", tp)
                ck.act.activation(hbuf[:, n, :], bk, AF.Gelu,
                                  bias=cols["l3_bfc"][:, n:n + 1])
                htoks[n] = ck.tok("act")
            for tt in range(TC3 // 128):
                t0 = tt * 128
                for half in range(2):
                    c0 = half * 16
                    for g in range(2):
                        ck.waits("pe", [htoks[31], bufs.trb_free,
                                        bufs.id_tok])
                        for b8 in range(8):
                            c = c0 + g * 8 + b8
                            ck.pe.transpose(
                                bufs.ps_trb[:, b8 * 128:(b8 + 1) * 128],
                                hbuf[:, c, t0:t0 + 128], bufs.id_bf[:, :])
                        tp = ck.tok("pe")
                        ck.wait("dve", tp)
                        ck.dve.tensor_copy(tmb[:, g * 1024:(g + 1) * 1024],
                                           bufs.ps_trb[:, :])
                        bufs.trb_free = ck.tok("dve")
                    tq = quant_tm(2048)
                    ck.waits("pe", [tq, bufs.trb_free])
                    for g in range(2):
                        ck.wait("pe", bufs.trb_free)
                        for b8 in range(8):
                            cc = g * 8 + b8
                            ck.pe.transpose(
                                bufs.ps_trb[:, b8 * 128:(b8 + 1) * 128],
                                tmb[:, cc * 128:(cc + 1) * 128],
                                bufs.id_bf[:, :])
                        tp = ck.tok("pe")
                        ck.wait("dve", tp)
                        for b8 in range(8):
                            cc = g * 8 + b8
                            ck.dve.tensor_copy(
                                hbuf[:, c0 + cc, t0:t0 + 128],
                                bufs.ps_trb[:, b8 * 128:(b8 + 1) * 128])
                        bufs.trb_free = ck.tok("dve")
            thq = ck.tok("dve")
            xtk = {}
            for oc in range(KC):
                two = ck.dma(
                    "qp", wo3[:, :, :],
                    W["wout"][:, oc * 128:(oc + 1) * 128]
                    .rearrange("(kc kp) n -> kp kc n", kp=128),
                    after=[bufs.wo_free] if bufs.wo_free else [])
                wt = spj.get(oc)
                bk = bufs.ps_big[:, 1024 + (oc % 2) * 512:
                                 1024 + (oc % 2) * 512 + TC3]
                if oc >= 2:
                    ck.wait("pe", xtk[oc - 2])
                ck.waits("pe", [two, thq])
                for kc in range(KC):
                    ck.pe.matmul(bk, wo3[:, kc, :], av_all[:, kc, tsl],
                                 start=(kc == 0), stop=False)
                bufs.wo_free = ck.tok("pe")
                for kc in range(32):
                    ck.pe.matmul(bk, wt[:, kc, :], hbuf[:, kc, :],
                                 start=False, stop=(kc == 31))
                spj.done(oc)
                tp = ck.tok("pe")
                ck.wait("dve", tp)
                ck.dve.scalar_tensor_tensor(
                    bufs.sc3[:, 0:TC3], bk, cols["l3_bo"][:, oc:oc + 1],
                    bufs.X[:, oc, tsl], ALU.add, ALU.add)
                ck.dve.tensor_scalar(bufs.X[:, oc, tsl], bufs.sc3[:, 0:TC3],
                                     cols["l3_bpj"][:, oc:oc + 1], None,
                                     ALU.add)
                xtk[oc] = ck.tok("dve")
            ck.wait("pe", xtk[KC - 1])
        spj.close()
        sfc.close()
    avg.__exit__(None, None, None)




def emit_l0(ck, bufs, ln, W, cols, phases="FM"):
    import os as _os
    nc = ck.nc
    DI = 2048
    CI = DI // 128

    if "F" in phases:
     with (
        nc.sbuf_tensor("l0_wd1", [128, KC, 256], BF16) as wd1a,
        nc.sbuf_tensor("l0_wd1b", [128, KC, 256], BF16) as wd1b,
        nc.sbuf_tensor("l0_wd2", [128, 2, 1024], BF16) as wd2a,
        nc.sbuf_tensor("l0_wd2b", [128, 2, 1024], BF16) as wd2b,
        nc.sbuf_tensor("l0_he", [128, 2, T], BF16) as he,
        nc.sbuf_tensor("l0_lacc", [128, 12], F32) as lacc,
        nc.sbuf_tensor("l0_a", [128, 4, T], BF16) as aab,
        nc.sbuf_tensor("l0_wad", [128, KC, 128], BF16) as wad,
        nc.sbuf_tensor("l0_wau", [128, 4, 128], BF16) as wau,
        nc.sbuf_tensor("l0_wbb", [128, KC, 128], BF16) as wbb,
        nc.sbuf_tensor("l0_ohc", [128, 4], F32) as ohc,
        nc.sbuf_tensor("l0_oh8", [8, 1], BF16) as oh8,
        nc.sbuf_tensor("l0_bup", [128, 8], F32) as bup,
        nc.sbuf_tensor("l0_baupw", [8, 1024], BF16) as baupw,
     ):
        ck.join()
        wd1s = [wd1a, wd1b]
        wd2s = [wd2a, wd2b]
        ld1 = {}
        ld2 = {}
        fr1 = [None, None]
        fr2 = [None, None]

        def load_e(e):
            if e >= 8 or e in ld1:
                return
            b = e % 2
            ld1[e] = ck.dma("qp", wd1s[b][:, :, :],
                            W["wd1"][e].rearrange("(kc kp) n -> kp kc n",
                                                  kp=128),
                            after=[fr1[b]] if fr1[b] else [])
            ld2[e] = ck.dma("qp", wd2s[b][:, :, :],
                            W["wd2"][e].rearrange("(kc kp) n -> kp kc n",
                                                  kp=128),
                            after=[fr2[b]] if fr2[b] else [])

        load_e(0)
        load_e(1)
        t_bw = ck.dma("qp", baupw[:, :], W["baup"])
        ck.wait("pool", bufs.ones_tok)
        ck.pool.memset(lacc[:, 0:12], 0.0)
        tz = ck.tok("pool")
        for e in range(8):
            b = e % 2
            ck.waits("pe", [ld1[e]])
            for n2 in range(2):
                ck.wait("pe", bufs.big_free)
                for kc in range(KC):
                    for tc in range(NT):
                        ck.pe.matmul(bufs.ps_big[:, tc * 512:(tc + 1) * 512],
                                     wd1s[b][:, kc, n2 * 128:(n2 + 1) * 128],
                                     bufs.xn[:, kc, tc * 512:(tc + 1) * 512],
                                     start=(kc == 0), stop=(kc == KC - 1))
                tp = ck.tok("pe")
                ck.wait("act", tp)
                ck.act.activation(he[:, n2, :], bufs.ps_big[:, :], AF.Relu,
                                  bias=cols["l0_bd1"][:, e * 2 + n2:
                                                      e * 2 + n2 + 1])
                bufs.big_free = ck.tok("act")
            fr1[b] = ck.tok("pe")
            the = ck.tok("act")
            ck.waits("pe", [ld2[e], the])
            for oc in range(KC):
                ck.wait("pe", bufs.big_free)
                for kc in range(2):
                    for tc in range(NT):
                        ck.pe.matmul(bufs.ps_big[:, tc * 512:(tc + 1) * 512],
                                     wd2s[b][:, kc, oc * 128:(oc + 1) * 128],
                                     he[:, kc, tc * 512:(tc + 1) * 512],
                                     start=(kc == 0), stop=(kc == 1))
                tp = ck.tok("pe")
                ck.waits("dve", [tp, tz])
                ck.dve.scalar_tensor_tensor(
                    bufs.scb1[:, :], bufs.ps_big[:, :],
                    cols["l0_bd2"][:, e * 8 + oc:e * 8 + oc + 1],
                    bufs.xn[:, oc, :], ALU.add, ALU.subtract)
                td = ck.tok("dve")
                bufs.big_free = td
                ck.wait("act", td)
                ck.act.activation(bufs.scb2[:, :], bufs.scb1[:, :],
                                  AF.Square, accum_out=lacc[:, 9:10])
                ta = ck.tok("act")
                ck.wait("dve", ta)
                ck.dve.tensor_add(lacc[:, e:e + 1], lacc[:, e:e + 1],
                                  lacc[:, 9:10])
            fr2[b] = ck.tok("pe")
            load_e(e + 1)
        td = ck.tok("dve")
        ck.waits("pe", [td, bufs.ones_tok])
        ck.pe.matmul(bufs.ps_row[0:1, 0:8], bufs.ones_f32[:, :],
                     lacc[:, 0:8], start=True, stop=True)
        tp = ck.tok("pe")
        ck.wait("dve", tp)
        ck.dve.tensor_copy(bufs.rowA[0:1, 0:8], bufs.ps_row[0:1, 0:8])
        ck.vreduce(bufs.rowA[0:1, 8:9], bufs.rowA[0:1, 0:8], AX.X, ALU.min)
        ck.dve.tensor_scalar(bufs.rowA[0:1, 0:8], bufs.rowA[0:1, 0:8],
                             bufs.rowA[0:1, 8:9], None, ALU.is_le)
        toh = ck.tok("dve")
        ck.wait("sp", toh)
        tup = ck.dma("qs", ck.scr_rows[20:21, 0:8], bufs.rowA[0:1, 0:8])
        ohds = []
        ohv = ck.scr_rows[20:21, 0:8].rearrange("one e -> (one e)") \
            .unsqueeze(1).broadcast_to([8, 64])
        for r in range(4):
            ohds.append(ck.dma("qs", ohc[:, r:r + 1],
                               ohv[2 * r:2 * r + 2, :], after=[tup]))
        t8 = ck.dma("qp", oh8[:, :],
                    ck.scr_rows[20:21, 0:8]
                    .rearrange("one e -> (one e)").unsqueeze(1),
                    after=[tup])
        ck.waits("pe", [t8, t_bw])
        for oc in range(KC):
            ck.pe.matmul(bufs.ps_mid[0:128, 256 + 2 * oc:257 + 2 * oc],
                         baupw[:, oc * 128:(oc + 1) * 128], oh8[:, :],
                         start=True, stop=True)
        tp = ck.tok("pe")
        ck.wait("dve", tp)
        ck.dve.tensor_copy(bup[:, 0:8],
                           bufs.ps_mid[:, 256:272]
                           .rearrange("p (n two) -> p n two", two=2)
                           [:, :, 0])
        ck.dve.tensor_scalar(bup[:, 0:8], bup[:, 0:8], None, None,
                             ALU.bypass) if False else None
        ck.dve.tensor_tensor(bup[:, 0:8], bup[:, 0:8],
                             cols["l0_bb"][:, 0:8], ALU.add)
        tbup = ck.tok("dve")
        tld = ck.dma("qp", wad[:, :, :], W["wadown"][:, 0:128]
                     .rearrange("(kc kp) n -> kp kc n", kp=128))
        for r in range(4):
            ck.waits("pe", [tld, bufs.big_free])
            for kc in range(KC):
                for tc in range(NT):
                    ck.pe.matmul(bufs.ps_big[:, tc * 512:(tc + 1) * 512],
                                 wad[:, kc, :],
                                 bufs.xn[:, kc, tc * 512:(tc + 1) * 512],
                                 start=(kc == 0), stop=(kc == KC - 1))
            tp = ck.tok("pe")
            ck.wait("act", tp)
            ck.act.activation(bufs.scb1[:, :], bufs.ps_big[:, :], AF.Relu,
                              bias=cols["l0_bad"][:, r:r + 1])
            ta = ck.tok("act")
            bufs.big_free = ta
            ck.waits("dve", [ta] + ohds)
            ck.dve.tensor_scalar(aab[:, r, :], bufs.scb1[:, :],
                                 ohc[:, r:r + 1], None, ALU.mult)
            tprev = ck.tok("pe")
            if r < 3:
                tld = ck.dma("qp", wad[:, :, :],
                             W["wadown"][:, (r + 1) * 128:(r + 2) * 128]
                             .rearrange("(kc kp) n -> kp kc n", kp=128),
                             after=[tprev])
        ta_done = ck.tok("dve")
        tlu = ck.dma("qp", wau[:, :, :], W["waup"][:, 0:128]
                     .rearrange("(kc kp) n -> kp kc n", kp=128))
        tlb = ck.dma("qp", wbb[:, :, :], W["wbase"][:, 0:128]
                     .rearrange("(kc kp) n -> kp kc n", kp=128))
        for oc in range(KC):
            ck.waits("pe", [tlu, tlb, ta_done, bufs.big_free])
            for kc in range(KC):
                for tc in range(NT):
                    ck.pe.matmul(bufs.ps_big[:, tc * 512:(tc + 1) * 512],
                                 wbb[:, kc, :],
                                 bufs.xn[:, kc, tc * 512:(tc + 1) * 512],
                                 start=(kc == 0), stop=False)
            for kc in range(4):
                for tc in range(NT):
                    ck.pe.matmul(bufs.ps_big[:, tc * 512:(tc + 1) * 512],
                                 wau[:, kc, :],
                                 aab[:, kc, tc * 512:(tc + 1) * 512],
                                 start=False, stop=(kc == 3))
            tp = ck.tok("pe")
            ck.waits("dve", [tp, tbup])
            ck.dve.scalar_tensor_tensor(
                bufs.X[:, oc, :], bufs.ps_big[:, :], bup[:, oc:oc + 1],
                bufs.X[:, oc, :], ALU.add, ALU.add)
            bufs.big_free = ck.tok("dve")
            tprev = ck.tok("pe")
            if oc < KC - 1:
                tlu = ck.dma("qp", wau[:, :, :],
                             W["waup"][:, (oc + 1) * 128:(oc + 2) * 128]
                             .rearrange("(kc kp) n -> kp kc n", kp=128),
                             after=[tprev])
                tlb = ck.dma("qp", wbb[:, :, :],
                             W["wbase"][:, (oc + 1) * 128:(oc + 2) * 128]
                             .rearrange("(kc kp) n -> kp kc n", kp=128),
                             after=[tprev])

    if "M" not in phases:
        return
    with (
        nc.sbuf_tensor("l0_win", [128, CI, 128], BF16) as winb,
        nc.sbuf_tensor("l0_win2", [128, CI, 128], BF16) as winb2,
        nc.sbuf_tensor("l0_xp", [128, CI, 2052], BF16) as xp,
    ):
        ck.join()
        inv_n = 1.0 / D
        for tc in range(NT):
            sl = slice(tc * 512, (tc + 1) * 512)
            for c in range(KC):
                tprev = ck.tok("pe") if c > 0 else None
                if tprev:
                    ck.wait("act", tprev)
                ck.act.activation(ln.sq[:, 0:512], bufs.xn[:, c, sl],
                                  AF.Square)
                tq = ck.tok("act")
                ck.waits("pe", [tq, bufs.ones_tok])
                ck.pe.matmul(ln.ps_q, bufs.ones_bf[:, :], ln.sq[:, 0:512],
                             start=(c == 0), stop=(c == KC - 1))
            tp = ck.tok("pe")
            ck.wait("dve", tp)
            ck.dve.tensor_scalar_mul(bufs.rowB[0:1, sl], ln.ps_q, inv_n)
            trow = ck.tok("dve")
            ck.wait("act", trow)
            ck.act.activation(bufs.rowB[0:1, sl], bufs.rowB[0:1, sl], AF.Ln,
                              bias=1e-6)
            ck.act.activation(bufs.rowB[0:1, sl], bufs.rowB[0:1, sl], AF.Exp,
                              scale=-0.5)
            ck.wait("pe", ck.tok("act"))
        ta = ck.tok("act")
        for c in range(KC):
            ck.dve.tensor_add(bufs.X[:, c, :], bufs.X[:, c, :],
                              bufs.xn[:, c, :])
        t_xadd = ck.tok("dve")
        t_xpage = ck.dma("qs", ck.scr_x, bufs.X[:, :, :], after=[t_xadd])
        ck.wait("sp", ta)
        rb = bcast_rows(ck, ck.scr_rows, [bufs.rowB[0:1, :]],
                        [bufs.sc1[:, :]])
        for c in range(KC):
            ck.waits("dve", rb)
            ck.dve.scalar_tensor_tensor(bufs.xn[:, c, :], bufs.xn[:, c, :],
                                        cols["l0_nw"][:, c:c + 1],
                                        bufs.sc1[:, :], ALU.mult, ALU.mult)
        txr = ck.tok("dve")
        xr = bufs.xn

        ck.wait("pool", txr)
        for c in range(CI):
            ck.pool.memset(xp[:, c, 0:3], 0.0)
        tpad = ck.tok("pool")
        wins = [winb, winb2]
        tw = {}
        cons = {}

        def load_blk(n):
            if n >= 32 or n in tw:
                return
            after = [cons[n - 2]] if (n - 2) in cons else []
            tw[n] = ck.dma("qp", wins[n % 2][:, 0:KC, :],
                           W["win"][:, n * 128:(n + 1) * 128]
                           .rearrange("(kc kp) n -> kp kc n", kp=128),
                           after=after)

        load_blk(0)
        load_blk(1)
        tzst = None
        for n in range(32):
            b = n % 2
            ck.waits("pe", [tw[n], txr, bufs.big_free])
            for kc in range(KC):
                for tc in range(NT):
                    ck.pe.matmul(bufs.ps_big[:, tc * 512:(tc + 1) * 512],
                                 wins[b][:, kc, :],
                                 xr[:, kc, tc * 512:(tc + 1) * 512],
                                 start=(kc == 0), stop=(kc == KC - 1))
            cons[n] = ck.tok("pe")
            ck.wait("dve", cons[n])
            if n < CI:
                ck.wait("dve", tpad)
                ck.dve.tensor_copy(xp[:, n, 3:3 + T], bufs.ps_big[:, :])
                bufs.big_free = ck.tok("dve")
            else:
                ck.wait("dve", tzst)
                ck.dve.tensor_copy(bufs.scb1[:, :], bufs.ps_big[:, :])
                td = ck.tok("dve")
                bufs.big_free = td
                tzst = ck.dma("qs",
                              ck.scr_z[(n - CI) * 128:(n - CI + 1) * 128, :],
                              bufs.scb1[:, :], after=[td])
            load_blk(n + 2)
        tz_done = ck.qtok("qs")

        for c in range(CI):
            for tc in range(NT - 1, -1, -1):
                t0 = tc * 512
                ck.dve.tensor_scalar(
                    bufs.sc2[:, 0:512], xp[:, c, t0:t0 + 512],
                    cols["l0_cw"][:, c * 4:c * 4 + 1], None, ALU.mult)
                for j in (1, 2, 3):
                    ck.dve.scalar_tensor_tensor(
                        bufs.sc2[:, 0:512], xp[:, c, t0 + j:t0 + j + 512],
                        cols["l0_cw"][:, c * 4 + j:c * 4 + j + 1],
                        bufs.sc2[:, 0:512], ALU.mult, ALU.add)
                td = ck.tok("dve")
                ck.wait("act", td)
                ck.act.activation(xp[:, c, 3 + t0:3 + t0 + 512],
                                  bufs.sc2[:, 0:512], AF.Silu,
                                  bias=cols["l0_cb"][:, c:c + 1])
                ck.wait("dve", ck.tok("act"))
        txc = ck.tok("act")
        xc = xp[:, :, 3:2051]

        if True:
            ck.wait("sp", t_xpage)
            ck.waits("pool", [t_xpage])
            wx = bufs.X[:, 6, 0:528].bitcast(BF16).rearrange(
                "p (a b) -> p a b", b=33)[:, 0:CI, :]
            ssm = bufs.X[0:33, 0, :]
            dtc = bufs.X[:, 1, :].bitcast(BF16)[:, 0:T]
            uc = bufs.X[:, 5, :].bitcast(BF16)[:, 0:T]
            negA = bufs.X[:, 2, 0:256]
            sredt = bufs.X[:, 3, :].bitcast(BF16)[:, 0:2048].rearrange(
                "p (a b) -> p a b", b=128)
            zt = bufs.X[:, 4, :].bitcast(BF16)[:, 0:T]
            twx = ck.dma("qp", wx[:, :, :],
                         W["wx"].rearrange("(kc kp) n -> kp kc n", kp=128))
            ck.waits("pe", [twx, txc])
            for tc in range(NT):
                tsl = slice(tc * 512, (tc + 1) * 512)
                ck.wait("pe", bufs.mid_free)
                for kc in range(CI):
                    ck.pe.matmul(bufs.ps_mid[0:33, 0:512], wx[:, kc, :],
                                 xc[:, kc, tsl],
                                 start=(kc == 0), stop=(kc == CI - 1))
                tp = ck.tok("pe")
                ck.wait("dve", tp)
                ck.dve.tensor_copy(ssm[:, tsl], bufs.ps_mid[0:33, 0:512])
                bufs.mid_free = ck.tok("dve")
            tssm = ck.tok("dve")
            ck.wait("sp", tssm)
            bdr = bcast_rows(ck, ck.scr_rows, [ssm[0:1, :]],
                             [bufs.sc1[:, :]])
            tupB = ck.dma("qs", ck.scr_bc[0:16, :], ssm[1:17, :],
                          after=[tssm])
            tupC = ck.dma("qs", ck.scr_bc[16:32, :], ssm[17:33, :],
                          after=[tssm])
            tB = ck.dma("qp", bufs.xn[:, 6, :],
                        ck.scr_bc[0:16, :].unsqueeze(1)
                        .broadcast_to([16, 8, T]), after=[tupB])
            tC = ck.dma("qp", bufs.xn[:, 7, :],
                        ck.scr_bc[16:32, :].unsqueeze(1)
                        .broadcast_to([16, 8, T]), after=[tupC])
            tA = ck.dma("qs", negA[:, :], W["ascan"])
            ck.wait("act", tA)
            ck.act.activation(negA[:, :], negA[:, :], AF.Exp)
            tae = ck.tok("act")
            ck.wait("dve", tae)
            ck.dve.tensor_scalar_mul(negA[:, :], negA[:, :], -1.0)
            tnegA = ck.tok("dve")

            for c in range(CI):
                csl = slice(c * 128, (c + 1) * 128)
                ck.waits("dve", bdr + [tssm])
                ck.dve.tensor_scalar(bufs.sc2[:, :], bufs.sc1[:, :],
                                     cols["l0_wdt"][:, c:c + 1],
                                     cols["l0_bdt"][:, c:c + 1],
                                     ALU.mult, ALU.add)
                td = ck.tok("dve")
                ck.wait("act", td)
                ck.act.activation(bufs.sc3[:, :], bufs.sc2[:, :], AF.Abs)
                ck.act.activation(bufs.sc3[:, :], bufs.sc3[:, :], AF.Exp,
                                  scale=-1.0)
                ck.act.activation(bufs.sc3[:, :], bufs.sc3[:, :], AF.Ln,
                                  bias=1.0)
                ta = ck.tok("act")
                ck.wait("dve", ta)
                ck.wait("dve", getattr(bufs, "l0_du_tok", None))
                ck.dve.scalar_tensor_tensor(dtc[:, :], bufs.sc2[:, :], 0.0,
                                            bufs.sc3[:, :], ALU.max, ALU.add)
                ck.wait("dve", txc)
                ck.dve.tensor_mul(uc[:, :], dtc[:, :], xc[:, c, :])
                tdu = ck.tok("dve")
                ck.dma("qs", ck.scr_dt[csl, :], dtc[:, :], after=[tdu])
                bufs.l0_du_tok = ck.dma("qs", ck.scr_u[csl, :], uc[:, :],
                                        after=[tdu])
            t_sc = ck.qtok("qs")

            ck.join()
            tsred = ck.dma("qs", sredt[:, :, :], W["sred"])
            rep_tok = {}

            def load_tile(kidx):
                if kidx >= 256 or kidx in rep_tok:
                    return
                par = kidx % 2
                ch0 = (kidx // 16) * 128 + (kidx % 16) * 8
                after = [rep_tok[kidx - 2][2]] if (kidx - 2) in rep_tok \
                    else []
                t1 = ck.dma("qs", bufs.xn[:, 0 + par, :],
                            ck.scr_dt[ch0:ch0 + 8, :].unsqueeze(0)
                            .broadcast_to([16, 8, T]), after=after)
                t2 = ck.dma("qs", bufs.xn[:, 2 + par, :],
                            ck.scr_u[ch0:ch0 + 8, :].unsqueeze(0)
                            .broadcast_to([16, 8, T]), after=after)
                rep_tok[kidx] = [t1, t2, None]

            load_tile(0)
            load_tile(1)
            y_evac = None
            for kc16 in range(CI):
                for j in range(16):
                    kidx = kc16 * 16 + j
                    par = kidx % 2
                    t1, t2, _ = rep_tok[kidx]
                    ck.waits("act", [t1, tnegA])
                    ck.act.activation(bufs.xn[:, par, :], bufs.xn[:, par, :],
                                      AF.Exp, scale=negA[:, kidx:kidx + 1])
                    ta = ck.tok("act")
                    ck.waits("dve", [t2, tB])
                    ck.dve.tensor_mul(bufs.xn[:, 2 + par, :],
                                      bufs.xn[:, 2 + par, :],
                                      bufs.xn[:, 6, :])
                    ck.wait("dve", ta)
                    ck.dve.tensor_tensor_scan(
                        bufs.xn[:, 4 + par, :], bufs.xn[:, par, :],
                        bufs.xn[:, 2 + par, :], 0.0, ALU.mult, ALU.add)
                    ck.wait("dve", tC)
                    ck.dve.tensor_mul(bufs.xn[:, 4 + par, :],
                                      bufs.xn[:, 4 + par, :],
                                      bufs.xn[:, 7, :])
                    ttmp = ck.tok("dve")
                    rep_tok[kidx][2] = ttmp
                    load_tile(kidx + 2)
                    ck.waits("pe", [ttmp, tsred])
                    if j == 0:
                        ck.wait("pe", y_evac)
                    for tc in range(NT):
                        ck.pe.matmul(
                            bufs.ps_big[:, tc * 512:(tc + 1) * 512],
                            sredt[:, j, :],
                            bufs.xn[:, 4 + par, tc * 512:(tc + 1) * 512],
                            start=(j == 0), stop=(j == 15))
                tp = ck.tok("pe")
                tzl = ck.dma("qs", zt[:, :],
                             ck.scr_z[kc16 * 128:(kc16 + 1) * 128, :],
                             after=[y_evac] if y_evac else [])
                ck.waits("act", [tzl, tz_done])
                ck.act.activation(bufs.scb1[:, :], zt[:, :], AF.Silu)
                tsz = ck.tok("act")
                ck.waits("dve", [tp, tsz])
                ck.dve.scalar_tensor_tensor(
                    bufs.sc2[:, :], xc[:, kc16, :],
                    cols["l0_D"][:, kc16:kc16 + 1], bufs.ps_big[:, :],
                    ALU.mult, ALU.add)
                ck.dve.tensor_mul(xc[:, kc16, :], bufs.sc2[:, :],
                                  bufs.scb1[:, :])
                y_evac = ck.tok("dve")

            ck.join()
            t_xr = ck.dma("qs", bufs.X[:, :, :], ck.scr_x)
            ck.wait("dve", t_xr)
            ck.wait("pe", t_xr)
            wouts = [winb, winb2]
            tws = {}
            wcons = {}

            def load_wout(ocn):
                if ocn >= KC or ocn in tws:
                    return
                after = [wcons[ocn - 2]] if (ocn - 2) in wcons else \
                    ([cons[31]] if ocn < 2 else [])
                tws[ocn] = ck.dma("qp", wouts[ocn % 2][:, :, :],
                                  W["wout0"][:, ocn * 128:(ocn + 1) * 128]
                                  .rearrange("(kc kp) n -> kp kc n", kp=128),
                                  after=after)

            load_wout(0)
            load_wout(1)
            for oc in range(KC):
                b = oc % 2
                ck.waits("pe", [tws[oc], y_evac, bufs.big_free])
                for kc in range(CI):
                    for tc in range(NT):
                        ck.pe.matmul(
                            bufs.ps_big[:, tc * 512:(tc + 1) * 512],
                            wouts[b][:, kc, :],
                            xc[:, kc, tc * 512:(tc + 1) * 512],
                            start=(kc == 0), stop=(kc == CI - 1))
                wcons[oc] = ck.tok("pe")
                ck.wait("dve", wcons[oc])
                ck.dve.tensor_add(bufs.X[:, oc, :], bufs.X[:, oc, :],
                                  bufs.ps_big[:, :])
                bufs.big_free = ck.tok("dve")
                load_wout(oc + 2)
